# version 02
# baseline (speedup 1.0000x reference)
import math
import contextlib
import numpy as np
import ml_dtypes
import concourse.bass as bass
import concourse.mybir as mybir
from concourse.bass_utils import run_bass_kernel_spmd

F32 = mybir.dt.float32
BF16 = mybir.dt.bfloat16
AF = mybir.ActivationFunctionType
ALU = mybir.AluOpType
AX = mybir.AxisListType.X

S = 4096
D = 1024
NT = S // 128
NC = D // 128
NEG = -1e9

ENGINES = ("pe", "dve", "act", "pool", "sp")
_ENG_ATTR = {"pe": "tensor", "dve": "vector", "act": "scalar", "pool": "gpsimd", "sp": "sync"}
_QUEUES = ("sp", "act", "pool")
_NROT = 4


class Op:
    __slots__ = ("build", "waits", "incs")

    def __init__(self, build):
        self.build = build
        self.waits = []
        self.incs = []


def _key(ap):
    t = getattr(ap, "tensor", ap)
    return t.name


class Tr:

    def __init__(self, nc):
        self.nc = nc
        self.ops = {e: [] for e in ENGINES}
        self.counts = {}
        self.mem = {}
        self.marks = {e: {} for e in ENGINES}
        self.sems = set()
        for e in ("pe", "dve", "act", "pool"):
            self._sem("S_" + e)
        self.qrot = {}
        for q in _QUEUES:
            for i in range(_NROT):
                self._sem(f"D_{q}{i}")
            self.qrot[q] = {"i": 0, "cum": [0] * _NROT}
        self.pe_pending_reads = []
        self.pe_open = set()
        self._last_op = {e: None for e in ENGINES}

    def _sem(self, name):
        self.sems.add(name)
        self.counts.setdefault(name, 0)

    def _rec(self, key):
        return self.mem.setdefault(key, {"w": None, "r": []})

    def _wait(self, op, eng, sem, val):
        if val <= 0:
            return
        wm = self.marks[eng]
        if wm.get(sem, 0) >= val:
            return
        op.waits.append((sem, val))
        wm[sem] = val

    def _engine_done_token(self, eng):
        return ("S_" + eng, self.counts["S_" + eng])

    def _flush_pe_reads(self, val):
        for key in self.pe_pending_reads:
            self._rec(key)["r"].append(("S_pe", val))
        self.pe_pending_reads = []

    def _force_pe_inc(self):
        if not self.pe_pending_reads:
            return
        last = self._last_op["pe"]
        assert last is not None
        self.counts["S_pe"] += 1
        v = self.counts["S_pe"]
        last.incs.append(("S_pe", 1))
        self._flush_pe_reads(v)

    def op(self, eng, build, reads=(), writes=(), inc=True,
           pe_start=True, pe_stop=True, is_mm=False):
        o = Op(build)
        for ap in reads:
            assert _key(ap) not in self.pe_open or eng == "pe", _key(ap)
        for ap in reads:
            k = _key(ap)
            r = self._rec(k)
            if r["w"] is not None:
                sem, val = r["w"]
                if not (eng == "pe" and sem == "S_pe"):
                    self._wait(o, eng, sem, val)
        for ap in writes:
            k = _key(ap)
            if k in self.pe_pending_reads and eng != "pe":
                self._force_pe_inc()
            r = self._rec(k)
            if r["w"] is not None:
                sem, val = r["w"]
                if not (eng == "pe" and sem == "S_pe"):
                    self._wait(o, eng, sem, val)
            for sem, val in r["r"]:
                if not (eng == "pe" and sem == "S_pe"):
                    self._wait(o, eng, sem, val)
        if eng == "pe":
            for ap in reads:
                self.pe_pending_reads.append(_key(ap))
            do_inc = inc and pe_stop
            if do_inc:
                self.counts["S_pe"] += 1
                v = self.counts["S_pe"]
                o.incs.append(("S_pe", 1))
                self._flush_pe_reads(v)
                for ap in writes:
                    rr = self._rec(_key(ap))
                    rr["w"] = ("S_pe", v)
                    rr["r"] = []
            else:
                pass
        else:
            sem = "S_" + eng
            self.counts[sem] += 1
            v = self.counts[sem]
            o.incs.append((sem, 1))
            for ap in writes:
                rr = self._rec(_key(ap))
                rr["w"] = (sem, v)
                rr["r"] = []
            for ap in reads:
                self._rec(_key(ap))["r"].append((sem, v))
        self.ops[eng].append(o)
        self._last_op[eng] = o
        return o

    def dma(self, q, dst, src, **kw):
        rot = self.qrot[q]
        slot = rot["i"] % _NROT
        rot["i"] += 1
        sem = f"D_{q}{slot}"
        o = Op(lambda e, dst=dst, src=src, kw=kw: e.dma_start(dst, src, **kw))
        if rot["cum"][slot] > 0:
            self._wait(o, q, sem, rot["cum"][slot])
        for ap in (src,):
            k = _key(ap)
            r = self._rec(k)
            if r["w"] is not None:
                self._wait(o, q, *r["w"])
        kd = _key(dst)
        if kd in self.pe_pending_reads:
            self._force_pe_inc()
        r = self._rec(kd)
        if r["w"] is not None:
            self._wait(o, q, *r["w"])
        for s_, v_ in r["r"]:
            self._wait(o, q, s_, v_)
        rot["cum"][slot] += 16
        self.counts[sem] = rot["cum"][slot]
        v = rot["cum"][slot]
        o.incs.append((sem, 16))
        rr = self._rec(kd)
        rr["w"] = (sem, v)
        rr["r"] = []
        self._rec(_key(src))["r"].append((sem, v))
        self.ops[q].append(o)
        self._last_op[q] = o
        return o

    def mm(self, out, lhsT, rhs, start=True, stop=True, **kw):
        k = _key(out)
        if start:
            assert k not in self.pe_open, k
        else:
            assert k in self.pe_open, k
        if stop:
            self.pe_open.discard(k)
        else:
            self.pe_open.add(k)
        return self.op(
            "pe",
            lambda e, o=out, l=lhsT, r=rhs, s=start, t=stop, kw=kw:
                e.matmul(o, l, r, start=s, stop=t, **kw),
            reads=[lhsT, rhs], writes=[out], pe_stop=stop)

    def trans(self, out, in_, ident, **kw):
        assert _key(out) not in self.pe_open
        return self.op(
            "pe",
            lambda e, o=out, i=in_, d=ident, kw=kw: e.transpose(o, i, d, **kw),
            reads=[in_, ident], writes=[out])

    def act(self, out, in_, func, bias=0.0, scale=1.0, accum_out=None):
        reads = [in_]
        writes = [out]
        if isinstance(bias, bass.AP):
            reads.append(bias)
        if isinstance(scale, bass.AP):
            reads.append(scale)
        if accum_out is not None:
            writes.append(accum_out)
        return self.op(
            "act",
            lambda e, o=out, i=in_, f=func, b=bias, s=scale, a=accum_out:
                e.activation(o, i, f, bias=b, scale=s, accum_out=a),
            reads=reads, writes=writes)

    def ee(self, eng, meth, out, *ins, extra_reads=(), **kw):
        reads = [a for a in ins if isinstance(a, bass.AP)] + list(extra_reads)
        for v in kw.values():
            if isinstance(v, bass.AP):
                reads.append(v)
        return self.op(
            eng,
            lambda e, m=meth, o=out, i=ins, kw=kw: getattr(e, m)(o, *i, **kw),
            reads=reads, writes=[out])

    def barrier(self):
        self._force_pe_inc()
        targets = [(s, v) for s, v in sorted(self.counts.items()) if v > 0]
        for e in ENGINES:
            o = Op(lambda en: en.nop())
            wm = self.marks[e]
            for s_, v_ in targets:
                if wm.get(s_, 0) < v_:
                    o.waits.append((s_, v_))
                    wm[s_] = v_
            self.ops[e].append(o)
            self._last_op[e] = o
        for rec in self.mem.values():
            rec["r"] = []

    def finalize(self):
        o = Op(lambda e: e.nop())
        for q in _QUEUES:
            rot = self.qrot[q]
            for i in range(_NROT):
                sem = f"D_{q}{i}"
                if rot["cum"][i] > 0:
                    o.waits.append((sem, rot["cum"][i]))
        for e in ("pe", "dve", "act", "pool"):
            sem = "S_" + e
            if self.counts[sem] > 0:
                o.waits.append((sem, self.counts[sem]))
        self.ops["sp"].append(o)

    def emit(self):
        nc = self.nc
        with contextlib.ExitStack() as es:
            handles = {}
            for name in sorted(self.sems):
                handles[name] = es.enter_context(nc.semaphore(name))
            blk = es.enter_context(nc.Block())

            def emit_engine(engobj, eng_name):
                for o in self.ops[eng_name]:
                    for sem, val in o.waits:
                        engobj.wait_ge(handles[sem], val)
                    inst = o.build(engobj)
                    for sem, n in o.incs:
                        inst.then_inc(handles[sem], n)

            blk.tensor(lambda e: emit_engine(e, "pe"))
            blk.vector(lambda e: emit_engine(e, "dve"))
            blk.scalar(lambda e: emit_engine(e, "act"))
            blk.gpsimd(lambda e: emit_engine(e, "pool"))
            blk.sync(lambda e: emit_engine(e, "sp"))



class Ctx:

    def __init__(self, tr, es):
        nc = tr.nc
        self.pb = [es.enter_context(nc.psum_tensor(f"pb{i}", [128, 512], F32))
                   for i in range(8)]
        self.pbb = [p.bitcast(BF16) for p in self.pb]
        self.ident_bf = es.enter_context(nc.sbuf_tensor("identbf", [128, 128], BF16))
        self.ident_f = es.enter_context(nc.sbuf_tensor("identf", [128, 128], F32))
        self.mask = es.enter_context(nc.sbuf_tensor("maskdg", [128, 128], F32))
        self.ones = es.enter_context(nc.sbuf_tensor("onesr", [1, 128], F32))
        self.eps = es.enter_context(nc.sbuf_tensor("epsc", [128, 1], F32))
        t = tr
        t.ee("pool", "memset", self.eps[:], 1e-5)
        t.ee("pool", "memset", self.ones[:], 1.0)
        t.ee("pool", "memset", self.ident_bf[:], 0.0)
        t.op("pool", lambda e, a=self.ident_bf: e.affine_select(
            out=a[:], in_=a[:], compare_op=ALU.not_equal, fill=1.0, base=0,
            pattern=[[-1, 128]], channel_multiplier=1),
            reads=[self.ident_bf[:]], writes=[self.ident_bf[:]])
        t.ee("pool", "memset", self.ident_f[:], 0.0)
        t.op("pool", lambda e, a=self.ident_f: e.affine_select(
            out=a[:], in_=a[:], compare_op=ALU.not_equal, fill=1.0, base=0,
            pattern=[[-1, 128]], channel_multiplier=1),
            reads=[self.ident_f[:]], writes=[self.ident_f[:]])
        t.ee("pool", "memset", self.mask[:], 0.0)
        t.op("pool", lambda e, a=self.mask: e.affine_select(
            out=a[:], in_=a[:], compare_op=ALU.is_ge, fill=NEG, base=0,
            pattern=[[-1, 128]], channel_multiplier=1),
            reads=[self.mask[:]], writes=[self.mask[:]])


def _alloc(es, nc, name, shape, dtype):
    return es.enter_context(nc.sbuf_tensor(name, shape, dtype))


def _memset(tr, ap, val, eng="pool"):
    tr.op(eng, lambda e, a=ap, v=val: e.memset(a, v), reads=[], writes=[ap])



def ph_ln(tr, es, C, T, src, dst, pref):
    nc = tr.nc
    nt = S // 128
    xt = [_alloc(es, nc, f"{pref}xt{i}", [128, 1024], F32) for i in range(2)]
    xnb = [_alloc(es, nc, f"{pref}xnb{i}", [128, 1024], BF16) for i in range(2)]
    sq = [_alloc(es, nc, f"{pref}sq{i}", [128, 1024], F32) for i in range(2)]
    sm = [_alloc(es, nc, f"{pref}sm{i}", [128, 8], F32) for i in range(2)]
    for t in range(nt):
        b = t % 2
        negm = sm[b][:, 0:1]
        ssq = sm[b][:, 1:2]
        var = sm[b][:, 2:3]
        std = sm[b][:, 3:4]
        rstd = sm[b][:, 4:5]
        nb = sm[b][:, 5:6]
        tr.dma("sp", xt[b][:], T[src][t * 128:(t + 1) * 128, :])
        tr.ee("dve", "reduce_sum", negm, xt[b][:], axis=AX, negate=True)
        tr.ee("dve", "tensor_scalar", negm, negm, 1.0 / 1024, None, op0=ALU.mult)
        tr.act(sq[b][:], xt[b][:], AF.Square, bias=negm, accum_out=ssq)
        tr.act(std, ssq, AF.Sqrt, bias=C.eps[:, 0:1], scale=1.0 / 1024)
        tr.ee("dve", "reciprocal", rstd, std)
        tr.ee("dve", "tensor_mul", nb, negm, rstd)
        tr.ee("dve", "tensor_scalar", xnb[b][:], xt[b][:], rstd, nb,
              op0=ALU.mult, op1=ALU.add)
        tr.dma("sp", T[dst][t * 128:(t + 1) * 128, :], xnb[b][:])


def ph_quant(tr, es, C, T):
    nc = tr.nc
    wt = [_alloc(es, nc, f"qwt{i}", [128, 4096], F32) for i in range(2)]
    sgn = _alloc(es, nc, "qsgn", [128, 4096], BF16)
    absw = _alloc(es, nc, "qabs", [128, 4096], F32)
    mbit = _alloc(es, nc, "qmb", [128, 4096], BF16)
    wq = _alloc(es, nc, "qwq", [128, 4096], BF16)
    colacc = _alloc(es, nc, "qcol", [128, 1], F32)
    tmp1 = _alloc(es, nc, "qtmp", [128, 1], F32)
    oncol = _alloc(es, nc, "qonc", [128, 1], F32)
    thr = _alloc(es, nc, "qthr", [1, 1], F32)
    cbc = _alloc(es, nc, "qcbc", [128, 1], F32)
    wqT = _alloc(es, nc, "qwqT", [128, 8 * 4096], BF16)

    for (wname, oname, ntile, width) in (("W1", "wq1T", 32, 1024),
                                         ("W2", "wq2T", 8, 4096)):
        numel = ntile * 128 * width
        _memset(tr, colacc[:], 0.0, "dve")
        _memset(tr, oncol[:], 1.0, "dve")
        for t in range(ntile):
            b = t % 2
            tr.dma("sp", wt[b][:, 0:width], T[wname][t * 128:(t + 1) * 128, :])
            tr.ee("dve", "reduce_sum", tmp1[:], wt[b][:, 0:width], axis=AX,
                  apply_absolute_value=True)
            tr.ee("dve", "tensor_add", colacc[:], colacc[:], tmp1[:])
        tr.mm(C.pb[0][0:1, 0:1], colacc[:], oncol[:])
        tr.ee("dve", "tensor_scalar", thr[:], C.pb[0][0:1, 0:1], 0.5 / numel,
              0.5e-5, op0=ALU.mult, op1=ALU.add)
        tr.mm(C.pb[1][:, 0:1], C.ones[0:1, :], thr[:])
        tr.ee("dve", "tensor_copy", cbc[:], C.pb[1][:, 0:1])
        nchunk = width // 128
        for t in range(ntile):
            b = t % 2
            tr.dma("sp", wt[b][:, 0:width], T[wname][t * 128:(t + 1) * 128, :])
            tr.op("act", lambda e, o=sgn[:, 0:width], i=wt[b][:, 0:width]:
                  e.sign(o, i), reads=[wt[b][:, 0:width]], writes=[sgn[:, 0:width]])
            tr.act(absw[:, 0:width], wt[b][:, 0:width], AF.Abs)
            tr.ee("dve", "tensor_scalar", mbit[:, 0:width], absw[:, 0:width],
                  cbc[:, 0:1], None, op0=ALU.is_gt)
            tr.ee("dve", "tensor_mul", wq[:, 0:width], mbit[:, 0:width],
                  sgn[:, 0:width])
            for c0 in range(0, nchunk, 4):
                cg = min(4, nchunk - c0)
                for q in range(cg):
                    tr.trans(C.pbb[2][:, q * 128:(q + 1) * 128],
                             wq[:, (c0 + q) * 128:(c0 + q + 1) * 128],
                             C.ident_bf[:])
                dst = wqT[:].rearrange("p (c n) -> p c n", c=8 if wname == "W1" else 32)[
                    :, c0:c0 + cg, t * 128:t * 128 + 128]
                src = C.pbb[2][:].rearrange("p (q n) -> p q n", q=8)[:, 0:cg, 0:128]
                tr.ee("dve", "tensor_copy", dst, src)
        for c in range(nchunk):
            tr.dma("sp", T[oname][c * 128:(c + 1) * 128, :],
                   wqT[:, c * (32768 // nchunk):(c * (32768 // nchunk)) + (32768 // nchunk)])


def _load_xT(tr, dst_sb, src_dram, scol0, swidth):
    for c in range(8):
        tr.dma("sp", dst_sb[:, c * swidth:(c + 1) * swidth],
               src_dram[scol0:scol0 + swidth, c * 128:(c + 1) * 128],
               transpose=True)


def _load_nat(tr, dst_sb, src_dram, width=1024):
    nt = S // 128
    for g in range(4):
        t0, t1 = g * nt // 4, (g + 1) * nt // 4
        dst = dst_sb[:].rearrange("p (t d) -> p t d", d=width)[:, t0:t1, :]
        src = src_dram.rearrange("(t p) d -> p t d", p=128)[:, t0:t1, :]
        tr.dma("sp", dst, src)


def ph_attn0(tr, es, C, T):
    nc = tr.nc
    nt = S // 128
    xnat = _alloc(es, nc, "a0nat", [128, nt * 1024], BF16)
    xT = _alloc(es, nc, "a0xT", [128, 8 * S], BF16)
    Rsb = _alloc(es, nc, "a0R", [128, 8 * 1024], BF16)
    encR = _alloc(es, nc, "a0enc", [128, 8 * 512], BF16)
    Srow = _alloc(es, nc, "a0S", [128, S], F32)
    Prow = _alloc(es, nc, "a0P", [128, S], BF16)
    PT = _alloc(es, nc, "a0PT", [128, S], BF16)
    Zsb = [_alloc(es, nc, f"a0Z{i}", [128, 1024], F32) for i in range(2)]
    sm = _alloc(es, nc, "a0sm", [128, 8], F32)
    _load_nat(tr, xnat, T["xn0n"])
    _load_xT(tr, xT, T["xn0n"], 0, S)
    tr.dma("pool", Rsb[:].rearrange("p (c e) -> p c e", c=8),
           T["R"].rearrange("(c p) e -> p c e", p=128))
    for qg in range(S // 512):
        for co in range(8):
            for ci in range(8):
                tr.mm(C.pb[6][:, 0:512],
                      Rsb[:, ci * 1024 + co * 128: ci * 1024 + co * 128 + 128],
                      xT[:, ci * S + qg * 512: ci * S + qg * 512 + 512],
                      start=(ci == 0), stop=(ci == 7))
            tr.ee("dve", "tensor_copy", encR[:, co * 512:(co + 1) * 512],
                  C.pb[6][:, 0:512])
        for qt in range(4):
            q = qg * 4 + qt
            klen = 128 * (q + 1)
            ng = (klen + 511) // 512
            for kg in range(ng):
                w = min(512, klen - kg * 512)
                bank = C.pb[kg % 2]
                for ci in range(8):
                    tr.mm(bank[:, 0:w],
                          encR[:, ci * 512 + qt * 128: ci * 512 + qt * 128 + 128],
                          xT[:, ci * S + kg * 512: ci * S + kg * 512 + w],
                          start=(ci == 0), stop=(ci == 7))
                if kg == ng - 1:
                    if w > 128:
                        tr.ee("dve", "tensor_copy",
                              Srow[:, kg * 512: kg * 512 + w - 128],
                              bank[:, 0:w - 128])
                    tr.ee("dve", "tensor_add",
                          Srow[:, kg * 512 + w - 128: kg * 512 + w],
                          bank[:, w - 128:w], C.mask[:])
                else:
                    tr.ee("dve", "tensor_copy", Srow[:, kg * 512: kg * 512 + w],
                          bank[:, 0:w])
            negmax = sm[:, 0:1]
            sumexp = sm[:, 1:2]
            rs = sm[:, 2:3]
            tr.ee("dve", "reduce_max", negmax, Srow[:, 0:klen], axis=AX, negate=True)
            tr.act(Prow[:, 0:klen], Srow[:, 0:klen], AF.Exp, bias=negmax,
                   accum_out=sumexp)
            tr.ee("dve", "reciprocal", rs, sumexp)
            nk = klen // 128
            for j0 in range(0, nk, 4):
                jg = min(4, nk - j0)
                for q_ in range(jg):
                    tr.trans(C.pbb[2][:, q_ * 128:(q_ + 1) * 128],
                             Prow[:, (j0 + q_) * 128:(j0 + q_ + 1) * 128],
                             C.ident_bf[:])
                tr.ee("dve", "tensor_copy", PT[:, j0 * 128: j0 * 128 + jg * 128],
                      C.pbb[2][:, 0:jg * 128])
            for j in range(nk):
                for h in range(2):
                    tr.mm(C.pb[3 + h][:, 0:512],
                          PT[:, j * 128:(j + 1) * 128],
                          xnat[:, j * 1024 + h * 512: j * 1024 + h * 512 + 512],
                          start=(j == 0), stop=(j == nk - 1))
            zb = Zsb[q % 2]
            for h in range(2):
                tr.ee("dve", "tensor_scalar", zb[:, h * 512:(h + 1) * 512],
                      C.pb[3 + h][:, 0:512], rs, None, op0=ALU.mult)
            tr.dma("sp", T["Z0"][q * 128:(q + 1) * 128, :], zb[:])


def ph_ffn0(tr, es, C, T):
    nc = tr.nc
    xT = _alloc(es, nc, "f0xT", [128, 8 * S], BF16)
    w1T = _alloc(es, nc, "f0w1", [128, 8 * 4096], BF16)
    w2b = [_alloc(es, nc, f"f0w2{i}", [128, 1024], BF16) for i in range(2)]
    hsb = _alloc(es, nc, "f0h", [128, 32 * 512], BF16)
    b1s = _alloc(es, nc, "f0b1s", [32, 128], F32)
    b1T = _alloc(es, nc, "f0b1T", [128, 32], F32)
    b2r = _alloc(es, nc, "f0b2r", [1, 1024], F32)
    b2bc = _alloc(es, nc, "f0b2bc", [128, 1024], F32)
    xt = [_alloc(es, nc, f"f0x{i}", [128, 1024], F32) for i in range(2)]
    zt = [_alloc(es, nc, f"f0z{i}", [128, 1024], F32) for i in range(2)]
    jt = [_alloc(es, nc, f"f0j{i}", [128, 1024], F32) for i in range(2)]
    _load_xT(tr, xT, T["xn0n"], 0, S)
    for c in range(8):
        tr.dma("sp", w1T[:, c * 4096:(c + 1) * 4096],
               T["wq1T"][c * 128:(c + 1) * 128, :])
    tr.dma("sp", b1s[:], T["b1f"].rearrange("(a p) -> a p", p=128))
    tr.trans(C.pb[6][:, 0:32], b1s[:], C.ident_f[0:32, 0:32])
    tr.ee("dve", "tensor_copy", b1T[:], C.pb[6][:, 0:32])
    tr.dma("sp", b2r[:], T["b2f"][None, :])
    for eh in range(2):
        tr.mm(C.pb[7][:, 0:512], C.ones[0:1, :], b2r[0:1, eh * 512:(eh + 1) * 512])
        tr.ee("dve", "tensor_copy", b2bc[:, eh * 512:(eh + 1) * 512],
              C.pb[7][:, 0:512])
    for sg in range(S // 512):
        for fs in range(32):
            bank = C.pb[fs % 2]
            for ci in range(8):
                tr.mm(bank[:, 0:512],
                      w1T[:, ci * 4096 + fs * 128: ci * 4096 + fs * 128 + 128],
                      xT[:, ci * S + sg * 512: ci * S + sg * 512 + 512],
                      start=(ci == 0), stop=(ci == 7))
            tr.act(hsb[:, fs * 512:(fs + 1) * 512], bank[:, 0:512], AF.Relu,
                   bias=b1T[:, fs:fs + 1])
        for pair in range(2):
            for cc in range(32):
                tr.dma("sp", w2b[cc % 2][:], T["wq2T"][cc * 128:(cc + 1) * 128, :])
                for sl in range(2):
                    stl = pair * 2 + sl
                    for eh in range(2):
                        tr.mm(C.pb[2 + sl * 2 + eh][:, 0:512],
                              hsb[:, cc * 512 + stl * 128: cc * 512 + stl * 128 + 128],
                              w2b[cc % 2][:, eh * 512:(eh + 1) * 512],
                              start=(cc == 0), stop=(cc == 31))
            for sl in range(2):
                stl = pair * 2 + sl
                t_glob = sg * 4 + stl
                b = stl % 2
                tr.dma("sp", xt[b][:], T["x"][t_glob * 128:(t_glob + 1) * 128, :])
                tr.dma("sp", zt[b][:], T["Z0"][t_glob * 128:(t_glob + 1) * 128, :])
                for eh in range(2):
                    o = jt[b][:, eh * 512:(eh + 1) * 512]
                    tr.ee("dve", "tensor_add", o, C.pb[2 + sl * 2 + eh][:, 0:512],
                          zt[b][:, eh * 512:(eh + 1) * 512])
                    tr.ee("dve", "tensor_add", o, o, xt[b][:, eh * 512:(eh + 1) * 512])
                    tr.ee("dve", "tensor_add", o, o, b2bc[:, eh * 512:(eh + 1) * 512])
                tr.dma("sp", T["x1"][t_glob * 128:(t_glob + 1) * 128, :], jt[b][:])


def ph_qkvr(tr, es, C, T):
    nc = tr.nc
    nt = S // 128
    xT = _alloc(es, nc, "qxT", [128, 8 * S], BF16)
    Wsb = {w: _alloc(es, nc, f"qW{w}", [128, 8 * 1024], BF16) for w in ("WQ", "WK", "WV")}
    bq8 = _alloc(es, nc, "qbq8", [128, 8], F32)
    bqs = _alloc(es, nc, "qbqs", [8, 128], F32)
    ktile = [_alloc(es, nc, f"qkt{i}", [128, 512], BF16) for i in range(2)]
    sqt = _alloc(es, nc, "qsqt", [128, 512], F32)
    sel = _alloc(es, nc, "qsel", [128, 2], F32)
    kn_g = _alloc(es, nc, "qkng", [1, 128], F32)
    qn_g = _alloc(es, nc, "qqng", [1, 128], F32)
    hrow = _alloc(es, nc, "qhrow", [1, 64], F32)
    vpv = _alloc(es, nc, "qvpv", [128, nt * 16 * 65], BF16)
    biasM = _alloc(es, nc, "qbM", [128, 16], F32)
    _load_xT(tr, xT, T["xn1n"], 0, S)
    for w in ("WQ", "WK", "WV"):
        tr.dma("pool", Wsb[w][:].rearrange("p (c e) -> p c e", c=8),
               T[w].rearrange("(c p) e -> p c e", p=128))
    tr.dma("sp", bqs[:], T["bQ"].rearrange("(a p) -> a p", p=128))
    tr.trans(C.pb[6][:, 0:8], bqs[:], C.ident_f[0:8, 0:8])
    tr.ee("dve", "tensor_scalar", bq8[:], C.pb[6][:, 0:8], 0.125, None, op0=ALU.mult)
    bkT = _alloc(es, nc, "qbkT", [128, 8], F32)
    tr.dma("sp", bqs[:], T["bK"].rearrange("(a p) -> a p", p=128))
    tr.trans(C.pb[6][:, 0:8], bqs[:], C.ident_f[0:8, 0:8])
    tr.ee("dve", "tensor_copy", bkT[:], C.pb[6][:, 0:8])
    _memset(tr, sel[:], 0.0, "dve")
    _memset(tr, sel[0:64, 0:1], 1.0, "dve")
    _memset(tr, sel[64:128, 1:2], 1.0, "dve")
    _memset(tr, vpv[:], 1.0, "pool")
    for (w, dstn, scale, do_bias, ng) in (("WK", "kT", 1.0, False, kn_g),
                                          ("WQ", "qT8", 0.125, True, qn_g)):
        for co in range(8):
            for sg in range(S // 512):
                bank = C.pb[(co + sg) % 2]
                for ci in range(8):
                    tr.mm(bank[:, 0:512],
                          Wsb[w][:, ci * 1024 + co * 128: ci * 1024 + co * 128 + 128],
                          xT[:, ci * S + sg * 512: ci * S + sg * 512 + 512],
                          start=(ci == 0), stop=(ci == 7))
                kb = ktile[(co + sg) % 2]
                if do_bias:
                    tr.ee("dve", "tensor_scalar", kb[:], bank[:, 0:512], scale,
                          bq8[:, co:co + 1], op0=ALU.mult, op1=ALU.add)
                else:
                    tr.ee("dve", "tensor_scalar", kb[:], bank[:, 0:512],
                          bkT[:, co:co + 1], None, op0=ALU.add)
                tr.dma("sp", T[dstn][co * 128:(co + 1) * 128, sg * 512:(sg + 1) * 512],
                       kb[:])
                tr.act(sqt[:], kb[:], AF.Square)
                for hh in range(2):
                    tr.mm(C.pb[4 + hh][0:1, 0:512], sel[:, hh:hh + 1], sqt[:])
                    h = co * 2 + hh
                    nsl = S // 512
                    tr.ee("dve", "reduce_max", ng[0:1, h * nsl + sg: h * nsl + sg + 1],
                          C.pb[4 + hh][0:1, 0:512], axis=AX)
    qmax = _alloc(es, nc, "qqmax", [1, 16], F32)
    kmax = _alloc(es, nc, "qkmax", [1, 16], F32)
    prod = _alloc(es, nc, "qprod", [1, 16], F32)
    nsl = S // 512
    for h in range(16):
        tr.ee("dve", "reduce_max", qmax[0:1, h:h + 1],
              qn_g[0:1, h * nsl:(h + 1) * nsl], axis=AX)
        tr.ee("dve", "reduce_max", kmax[0:1, h:h + 1],
              kn_g[0:1, h * nsl:(h + 1) * nsl], axis=AX)
    tr.ee("dve", "tensor_mul", prod[:], qmax[:], kmax[:])
    tr.act(prod[:], prod[:], AF.Sqrt)
    tr.ee("dve", "tensor_scalar", prod[:], prod[:], -1.0, None, op0=ALU.mult)
    tr.mm(C.pb[6][:, 0:16], C.ones[0:1, :], prod[:])
    tr.ee("dve", "tensor_copy", biasM[:], C.pb[6][:, 0:16])
    tr.dma("sp", T["biasM"], biasM[:])
    for st in range(nt):
        for dh in range(2):
            bank = C.pb[(st + dh) % 2]
            for ci in range(8):
                tr.mm(bank[:, 0:512],
                      xT[:, ci * S + st * 128: ci * S + st * 128 + 128],
                      Wsb["WV"][:, ci * 1024 + dh * 512: ci * 1024 + dh * 512 + 512],
                      start=(ci == 0), stop=(ci == 7))
            for hh in range(8):
                h = dh * 8 + hh
                tr.ee("dve", "tensor_copy",
                      vpv[:, (st * 16 + h) * 65:(st * 16 + h) * 65 + 64],
                      bank[:, hh * 64:(hh + 1) * 64])
    for g in range(4):
        w = nt * 16 * 65 // 4
        tr.dma("sp", T["vpv"][:, g * w:(g + 1) * w], vpv[:, g * w:(g + 1) * w])
    wof = _alloc(es, nc, "qwof", [64, 1024], F32)
    bv16 = _alloc(es, nc, "qbv16", [16, 64], F32)
    bv64 = _alloc(es, nc, "qbv64", [64, 16], F32)
    bvs = _alloc(es, nc, "qbvs", [64, 1], F32)
    bvwo = _alloc(es, nc, "qbvwo", [1, 1024], F32)
    bor = _alloc(es, nc, "qbor", [1, 1024], F32)
    jbr = _alloc(es, nc, "qjbr", [1, 1024], F32)
    tr.dma("sp", wof[:], T["WO"])
    tr.dma("sp", bv16[:], T["bV"].rearrange("(h d) -> h d", d=64))
    tr.trans(C.pb[6][0:64, 0:16], bv16[:], C.ident_f[0:16, 0:16])
    tr.ee("dve", "tensor_copy", bv64[:], C.pb[6][0:64, 0:16])
    tr.ee("dve", "reduce_sum", bvs[:], bv64[:], axis=AX)
    for eh in range(2):
        tr.mm(C.pb[7][0:1, 0:512], bvs[:], wof[:, eh * 512:(eh + 1) * 512])
        tr.ee("dve", "tensor_copy", bvwo[0:1, eh * 512:(eh + 1) * 512],
              C.pb[7][0:1, 0:512])
    tr.dma("sp", bor[:], T["bO"][None, :])
    tr.ee("dve", "scalar_tensor_tensor", jbr[:], bor[:], 16.0, bvwo[:],
          op0=ALU.mult, op1=ALU.add)
    tr.dma("sp", T["jbrow"], jbr[:])


def ph_router(tr, es, C, T):
    nc = tr.nc
    nt = S // 128
    xT = _alloc(es, nc, "rxT", [128, 8 * S], BF16)
    _load_xT(tr, xT, T["xn1n"], 0, S)
    rwsb = _alloc(es, nc, "qrw", [128, 8 * 16], BF16)
    rbsb = _alloc(es, nc, "qrb", [16, 1], F32)
    lgT = _alloc(es, nc, "qlgT", [16, S], F32)
    lnt = _alloc(es, nc, "qlnt", [128, 16], F32)
    lw = [_alloc(es, nc, f"qlw{i}", [128, 16], F32) for i in range(2)]
    eqb = _alloc(es, nc, "qeq", [128, 16], F32)
    dport = _alloc(es, nc, "qd", [128, 16], F32)
    indb = _alloc(es, nc, "qind", [128, 16], F32)
    expb = _alloc(es, nc, "qexp", [128, 16], F32)
    peb = _alloc(es, nc, "qpe", [128, 16], F32)
    cechunk = _alloc(es, nc, "qcec", [128, 16], F32)
    ce_sb = _alloc(es, nc, "qce", [128, nt * 16], F32)
    ceT_sb = _alloc(es, nc, "qceT", [16, S], F32)
    smr = _alloc(es, nc, "qsmr", [128, 8], F32)
    tr.dma("pool", rwsb[:].rearrange("p (c e) -> p c e", c=8),
           T["rw"].rearrange("(c p) e -> p c e", p=128))
    tr.dma("sp", rbsb[:], T["rb"][:, None])
    for sg in range(S // 512):
        bank = C.pb[sg % 2]
        for ci in range(8):
            tr.mm(bank[0:16, 0:512], rwsb[:, ci * 16:(ci + 1) * 16],
                  xT[:, ci * S + sg * 512: ci * S + sg * 512 + 512],
                  start=(ci == 0), stop=(ci == 7))
        tr.ee("dve", "tensor_scalar", lgT[0:16, sg * 512:(sg + 1) * 512],
              bank[0:16, 0:512], rbsb[:, 0:1], None, op0=ALU.add)
    for st in range(nt):
        tr.trans(C.pb[2][:, 0:16], lgT[0:16, st * 128:(st + 1) * 128],
                 C.ident_f[0:16, 0:16])
        tr.ee("dve", "tensor_copy", lnt[:], C.pb[2][:, 0:16])
        mx = smr[:, 0:1]
        m1n = smr[:, 1:2]
        cur = lnt
        for it in range(4):
            tr.ee("dve", "reduce_max", mx, cur[:], axis=AX)
            if it == 0:
                tr.ee("dve", "tensor_scalar", m1n, mx, -1.0, None, op0=ALU.mult)
            tr.ee("dve", "tensor_scalar", eqb[:], cur[:], mx, None, op0=ALU.is_equal)
            nxt = lw[it % 2]
            tr.ee("dve", "scalar_tensor_tensor", nxt[:], eqb[:], -1e30, cur[:],
                  op0=ALU.mult, op1=ALU.add)
            cur = nxt
        tr.ee("dve", "tensor_sub", dport[:], lnt[:], cur[:])
        tr.ee("dve", "tensor_scalar", indb[:], dport[:], 1e20, None, op0=ALU.is_gt)
        tr.act(expb[:], lnt[:], AF.Exp, bias=m1n)
        tr.ee("dve", "tensor_mul", peb[:], expb[:], indb[:])
        ssum = smr[:, 2:3]
        rr = smr[:, 3:4]
        tr.ee("dve", "reduce_sum", ssum, peb[:], axis=AX)
        tr.ee("dve", "reciprocal", rr, ssum)
        tr.ee("dve", "tensor_scalar", cechunk[:], peb[:], rr, None, op0=ALU.mult)
        tr.ee("dve", "tensor_copy", ce_sb[:, st * 16:(st + 1) * 16], cechunk[:])
        tr.trans(C.pb[2][0:16, 0:128], cechunk[:], C.ident_f[:])
        tr.ee("dve", "tensor_copy", ceT_sb[0:16, st * 128:(st + 1) * 128],
              C.pb[2][0:16, 0:128])
    tr.dma("sp", T["ce"], ce_sb[:])
    tr.dma("sp", T["ceT"], ceT_sb[:])




def ph_attn1(tr, es, C, T):
    nc = tr.nc
    nt = S // 128
    kT = _alloc(es, nc, "t1kT", [128, 8 * S], BF16)
    vpv = _alloc(es, nc, "t1v", [128, nt * 16 * 65], BF16)
    biasM = _alloc(es, nc, "t1bM", [128, 16], F32)
    wobf = _alloc(es, nc, "t1wo", [64, 1024], BF16)
    Zacc = _alloc(es, nc, "t1Z", [64, S], F32)
    Zbf = _alloc(es, nc, "t1Zb", [64, 512], BF16)
    qsl = [_alloc(es, nc, f"t1q{i}", [128, 8 * 512], BF16) for i in range(2)]
    pt = [_alloc(es, nc, f"t1p{i}", [128, 512], BF16) for i in range(2)]
    rp = _alloc(es, nc, "t1rp", [1, 512], F32)
    bcb = _alloc(es, nc, "t1bc", [64, 512], F32)
    ztmp = _alloc(es, nc, "t1zt", [64, 512], F32)
    att = [_alloc(es, nc, f"t1a{i}", [128, 1024], F32) for i in range(2)]
    for c in range(8):
        tr.dma("sp", kT[:, c * S:(c + 1) * S], T["kT"][c * 128:(c + 1) * 128, :])
    for g in range(4):
        w = nt * 16 * 65 // 4
        tr.dma("sp", vpv[:, g * w:(g + 1) * w], T["vpv"][:, g * w:(g + 1) * w])
    tr.dma("sp", biasM[:], T["biasM"])
    tr.dma("pool", wobf[:], T["WO"])
    for qg in range(S // 512):
        qb = qsl[qg % 2]
        tr.dma("sp", qb[:].rearrange("p (c s) -> p c s", c=8),
               T["qT8"].rearrange("(c p) s -> p c s", p=128)[:, :, qg * 512:(qg + 1) * 512])
        nk = 4 * (qg + 1)
        for pr in range(8):
            c = pr
            for j in range(nk):
                qlo = max(0, 128 * j - 512 * qg)
                for hh in range(2):
                    h = 2 * pr + hh
                    lo = hh * 64
                    sbank = C.pb[hh]
                    pvbank = C.pb[2 + hh]
                    ptb = pt[hh]
                    tr.mm(sbank[:, qlo:512],
                          kT[lo:lo + 64, c * S + j * 128: c * S + j * 128 + 128],
                          qb[lo:lo + 64, c * 512 + qlo: c * 512 + 512],
                          start=True, stop=True)
                    tr.act(ptb[:, qlo:512], sbank[:, qlo:512], AF.Exp,
                           bias=biasM[:, h:h + 1])
                    if j >= 4 * qg:
                        tr.op("pool",
                              lambda e, a=ptb[:, qlo:512], wd=512 - qlo: e.affine_select(
                                  out=a, in_=a, compare_op=ALU.is_ge, fill=0.0,
                                  base=0, pattern=[[1, wd]], channel_multiplier=-1),
                              reads=[ptb[:, qlo:512]], writes=[ptb[:, qlo:512]])
                    tr.mm(pvbank[0:65, qlo:512],
                          vpv[:, (j * 16 + h) * 65:(j * 16 + h) * 65 + 65],
                          ptb[:, qlo:512],
                          start=(j == 0), stop=(j == nk - 1))
            for hh in range(2):
                h = 2 * pr + hh
                pvbank = C.pb[2 + hh]
                tr.ee("dve", "reciprocal", rp[:], pvbank[64:65, 0:512])
                tr.mm(C.pb[4][0:64, 0:512], C.ones[0:1, 0:64], rp[:])
                tr.ee("dve", "tensor_copy", bcb[:], C.pb[4][0:64, 0:512])
                zsl = Zacc[:, qg * 512:(qg + 1) * 512]
                if h == 0:
                    tr.ee("dve", "tensor_mul", zsl, pvbank[0:64, 0:512], bcb[:])
                else:
                    tr.ee("dve", "tensor_mul", ztmp[:], pvbank[0:64, 0:512], bcb[:])
                    tr.ee("dve", "tensor_add", zsl, zsl, ztmp[:])
        tr.ee("dve", "tensor_copy", Zbf[:], Zacc[:, qg * 512:(qg + 1) * 512])
        for qt in range(4):
            ab = att[qt % 2]
            for eh in range(2):
                tr.mm(C.pb[5 + eh][:, 0:512], Zbf[:, qt * 128:(qt + 1) * 128],
                      wobf[:, eh * 512:(eh + 1) * 512])
                tr.ee("dve", "tensor_copy", ab[:, eh * 512:(eh + 1) * 512],
                      C.pb[5 + eh][:, 0:512])
            q = qg * 4 + qt
            tr.dma("sp", T["attn1"][q * 128:(q + 1) * 128, :], ab[:])


def ph_moe(tr, es, C, T):
    nc = tr.nc
    nt = S // 128
    SGT = min(2048, S)
    nsg = S // SGT
    stg = SGT // 128
    ssb = SGT // 512
    xTs = _alloc(es, nc, "mxT", [128, 8 * SGT], BF16)
    w1e = _alloc(es, nc, "mw1", [128, 8 * 2048], BF16)
    w2e = _alloc(es, nc, "mw2", [128, 16 * 1024], BF16)
    hsb = _alloc(es, nc, "mh", [128, 16 * 512], BF16)
    acc = _alloc(es, nc, "macc", [128, stg * 1024], F32)
    ce_sb = _alloc(es, nc, "mce", [128, stg * 16], F32)
    ceT_sb = _alloc(es, nc, "mceT", [16, SGT], BF16)
    b2sb = _alloc(es, nc, "mb2", [16, 1024], BF16)
    b1x = _alloc(es, nc, "mb1x", [16, 128], F32)
    b1T = _alloc(es, nc, "mb1T", [128, 16 * 16], F32)
    jbr = _alloc(es, nc, "mjbr", [1, 1024], F32)
    jbbc = _alloc(es, nc, "mjbbc", [128, 1024], F32)
    x1t = [_alloc(es, nc, "mx10", [128, 1024], F32)] * 2
    att = [_alloc(es, nc, "mat0", [128, 1024], F32)] * 2
    outt = [_alloc(es, nc, "mo0", [128, 1024], F32)] * 2
    tr.dma("pool", b2sb[:], T["b2e"])
    tr.dma("sp", jbr[:], T["jbrow"])
    for eh in range(2):
        tr.mm(C.pb[6][:, 0:512], C.ones[0:1, :], jbr[0:1, eh * 512:(eh + 1) * 512])
        tr.ee("dve", "tensor_copy", jbbc[:, eh * 512:(eh + 1) * 512],
              C.pb[6][:, 0:512])
    for e in range(16):
        tr.dma("sp", b1x[:], T["b1e"][e].rearrange("(a p) -> a p", p=128))
        tr.trans(C.pb[6][:, 0:16], b1x[:], C.ident_f[0:16, 0:16])
        tr.ee("dve", "tensor_copy", b1T[:, e * 16:(e + 1) * 16], C.pb[6][:, 0:16])
    for sg in range(nsg):
        _load_xT(tr, xTs, T["xn1n"], sg * SGT, SGT)
        tr.dma("sp", ce_sb[:], T["ce"][:, sg * stg * 16:(sg + 1) * stg * 16])
        tr.dma("pool", ceT_sb[:], T["ceT"][0:16, sg * SGT:(sg + 1) * SGT])
        for e in range(16):
            for half in range(2):
                tr.dma("pool",
                       w1e[:].rearrange("p (c i) -> p c i", c=8)[:, half * 4:(half + 1) * 4, :],
                       T["w1e"][e].rearrange("(c p) i -> p c i", p=128)[:, half * 4:(half + 1) * 4, :])
                tr.dma("pool",
                       w2e[:].rearrange("p (c i) -> p c i", c=16)[:, half * 8:(half + 1) * 8, :],
                       T["w2e"][e].rearrange("(c p) i -> p c i", p=128)[:, half * 8:(half + 1) * 8, :])
            for ssub in range(ssb):
                for isl in range(16):
                    bank = C.pb[isl % 2]
                    for ci in range(8):
                        tr.mm(bank[:, 0:512],
                              w1e[:, ci * 2048 + isl * 128: ci * 2048 + isl * 128 + 128],
                              xTs[:, ci * SGT + ssub * 512: ci * SGT + ssub * 512 + 512],
                              start=(ci == 0), stop=(ci == 7))
                    tr.act(hsb[:, isl * 512:(isl + 1) * 512], bank[:, 0:512],
                           AF.Gelu, bias=b1T[:, e * 16 + isl: e * 16 + isl + 1])
                for pairi in range(2):
                    for cc in range(16):
                        for sl in range(2):
                            stl = pairi * 2 + sl
                            for eh in range(2):
                                tr.mm(C.pb[2 + sl * 2 + eh][:, 0:512],
                                      hsb[:, cc * 512 + stl * 128: cc * 512 + stl * 128 + 128],
                                      w2e[:, cc * 1024 + eh * 512: cc * 1024 + eh * 512 + 512],
                                      start=(cc == 0), stop=(cc == 15))
                    for sl in range(2):
                        stl = pairi * 2 + sl
                        st_l = ssub * 4 + stl
                        cec = ce_sb[:, st_l * 16 + e: st_l * 16 + e + 1]
                        for eh in range(2):
                            a = acc[:, st_l * 1024 + eh * 512: st_l * 1024 + eh * 512 + 512]
                            bank = C.pb[2 + sl * 2 + eh]
                            if e == 0:
                                tr.ee("dve", "tensor_scalar", a, bank[:, 0:512],
                                      cec, None, op0=ALU.mult)
                            else:
                                tr.ee("dve", "scalar_tensor_tensor", a, bank[:, 0:512],
                                      cec, a, op0=ALU.mult, op1=ALU.add)
        for stl16 in range(stg):
            st_g = sg * stg + stl16
            b = stl16 % 2
            tr.dma("sp", x1t[b][:], T["x1"][st_g * 128:(st_g + 1) * 128, :])
            tr.dma("sp", att[b][:], T["attn1"][st_g * 128:(st_g + 1) * 128, :])
            for eh in range(2):
                bank = C.pb[6 + eh]
                tr.mm(bank[0:128, 0:512], ceT_sb[0:16, stl16 * 128:(stl16 + 1) * 128],
                      b2sb[0:16, eh * 512:(eh + 1) * 512])
                o = outt[b][:, eh * 512:(eh + 1) * 512]
                tr.ee("dve", "tensor_add", o, bank[:, 0:512],
                      acc[:, stl16 * 1024 + eh * 512: stl16 * 1024 + eh * 512 + 512])
                tr.ee("dve", "tensor_add", o, o, x1t[b][:, eh * 512:(eh + 1) * 512])
                tr.ee("dve", "tensor_add", o, o, att[b][:, eh * 512:(eh + 1) * 512])
                tr.ee("dve", "tensor_add", o, o, jbbc[:, eh * 512:(eh + 1) * 512])
            tr.dma("sp", T["out"][st_g * 128:(st_g + 1) * 128, :], outt[b][:])



INPUT_TENSORS = {
    "x": lambda: ([S, 1024], F32),
    "R": lambda: ([1024, 1024], F32),
    "W1": lambda: ([4096, 1024], F32),
    "b1f": lambda: ([4096], F32),
    "W2": lambda: ([1024, 4096], F32),
    "b2f": lambda: ([1024], F32),
    "WQ": lambda: ([1024, 1024], F32),
    "bQ": lambda: ([1024], F32),
    "bK": lambda: ([1024], F32),
    "WK": lambda: ([1024, 1024], F32),
    "WV": lambda: ([1024, 1024], F32),
    "WO": lambda: ([64, 1024], F32),
    "bO": lambda: ([1024], F32),
    "bV": lambda: ([1024], F32),
    "rw": lambda: ([1024, 16], F32),
    "rb": lambda: ([16], F32),
    "w1e": lambda: ([16, 1024, 2048], F32),
    "b1e": lambda: ([16, 2048], F32),
    "w2e": lambda: ([16, 2048, 1024], F32),
    "b2e": lambda: ([16, 1024], F32),
}
SCRATCH_TENSORS = {
    "xn0n": lambda: ([S, 1024], BF16),
    "Z0": lambda: ([S, 1024], F32),
    "wq1T": lambda: ([1024, 4096], BF16),
    "wq2T": lambda: ([4096, 1024], BF16),
    "x1": lambda: ([S, 1024], F32),
    "xn1n": lambda: ([S, 1024], BF16),
    "qT8": lambda: ([1024, S], BF16),
    "kT": lambda: ([1024, S], BF16),
    "vpv": lambda: ([128, (S // 128) * 16 * 65], BF16),
    "biasM": lambda: ([128, 16], F32),
    "jbrow": lambda: ([1, 1024], F32),
    "attn1": lambda: ([S, 1024], F32),
    "ce": lambda: ([128, (S // 128) * 16], F32),
    "ceT": lambda: ([16, S], F32),
    "out": lambda: ([S, 1024], F32),
}

PHASES = {
    "ln0": (lambda tr, es, C, T: ph_ln(tr, es, C, T, "x", "xn0n", "l0"),
            ["x"], ["xn0n"]),
    "quant": (ph_quant, ["W1", "W2"], ["wq1T", "wq2T"]),
    "attn0": (ph_attn0, ["xn0n", "R"], ["Z0"]),
    "ffn0": (ph_ffn0, ["xn0n", "wq1T", "wq2T", "b1f", "b2f", "x", "Z0"], ["x1"]),
    "ln1": (lambda tr, es, C, T: ph_ln(tr, es, C, T, "x1", "xn1n", "l1"),
            ["x1"], ["xn1n"]),
    "qkvr": (ph_qkvr, ["xn1n", "WQ", "bQ", "WK", "bK", "WV", "WO", "bO", "bV"],
             ["qT8", "kT", "vpv", "biasM", "jbrow"]),
    "router": (ph_router, ["xn1n", "rw", "rb"], ["ce", "ceT"]),
    "attn1": (ph_attn1, ["qT8", "kT", "vpv", "biasM", "WO"], ["attn1"]),
    "moe": (ph_moe, ["xn1n", "w1e", "b1e", "w2e", "b2e", "ce", "ceT", "x1",
                     "attn1", "jbrow"], ["out"]),
}

GROUPS = [["ln0"], ["quant"], ["attn0"], ["ffn0"], ["ln1"], ["qkvr"],
          ["router"], ["attn1"], ["moe"]]


def build_program(phase_names, need_later):
    reads, writes = set(), set()
    for p in phase_names:
        _, r, w = PHASES[p]
        reads.update(r)
        writes.update(w)
    nc = bass.Bass("TRN2", target_bir_lowering=False)
    T = {}
    in_names, out_names = [], []
    for name in sorted(reads | writes):
        if name in INPUT_TENSORS:
            shape, dt = INPUT_TENSORS[name]()
            kind = "ExternalInput"
        else:
            shape, dt = SCRATCH_TENSORS[name]()
            if name in writes:
                kind = "ExternalOutput" if (name in need_later or name == "out") else "Internal"
            else:
                kind = "ExternalInput"
        if kind == "ExternalInput":
            in_names.append(name)
        elif kind == "ExternalOutput":
            out_names.append(name)
        T[name] = nc.dram_tensor(name, shape, dt, kind=kind).ap()
    tr = Tr(nc)
    with contextlib.ExitStack() as es_ctx:
        C = Ctx(tr, es_ctx)
        for p in phase_names:
            fn, _, _ = PHASES[p]
            with contextlib.ExitStack() as es:
                fn(tr, es, C, T)
            tr.barrier()
        tr.finalize()
        tr.emit()
    return nc, in_names, out_names


def _values_from_inputs(inputs):
    c = np.ascontiguousarray
    xs = [c(inputs["L0_attn_encoding"][i]) for i in range(8)]
    shared = {
        "R": inputs["L0_attn_R"], "W1": inputs["L0_ffn_W1"], "b1f": inputs["L0_ffn_b1"],
        "W2": inputs["L0_ffn_W2"], "b2f": inputs["L0_ffn_b2"],
        "WQ": inputs["L1_attn_WQ"], "bQ": inputs["L1_attn_bQ"], "bK": inputs["L1_attn_bK"],
        "WK": inputs["L1_attn_WK"], "WV": inputs["L1_attn_WV"],
        "WO": inputs["L1_attn_WO"], "bO": inputs["L1_attn_bO"],
        "bV": inputs["L1_attn_bV"],
        "rw": inputs["L1_ffn_router_w"], "rb": inputs["L1_ffn_router_b"],
        "w1e": inputs["L1_ffn_w1"], "b1e": inputs["L1_ffn_b1"],
        "w2e": inputs["L1_ffn_w2"], "b2e": inputs["L1_ffn_b2"],
    }
    vals = {"x": xs}
    for k, v in shared.items():
        a = c(np.asarray(v))
        vals[k] = [a] * 8
    return vals


FUSED = True


def kernel(**inputs):
    vals = _values_from_inputs(inputs)
    groups = [[p for g in GROUPS for p in g]] if FUSED else GROUPS
    for gi, g in enumerate(groups):
        later = set()
        for g2 in groups[gi + 1:]:
            for p in g2:
                later.update(PHASES[p][1])
        nc, in_names, out_names = build_program(g, later)
        in_maps = [{n: vals[n][i] for n in in_names} for i in range(8)]
        res = run_bass_kernel_spmd(nc, in_maps, core_ids=list(range(8)))
        for n in out_names:
            vals[n] = [np.asarray(res.results[i][n]) for i in range(8)]
    return np.stack([vals["out"][i] for i in range(8)], axis=0)


# version 06
# speedup vs baseline: 1.1522x; 1.1522x over previous
import math
import contextlib
import numpy as np
import ml_dtypes
import concourse.bass as bass
import concourse.mybir as mybir
from concourse.bass_utils import run_bass_kernel_spmd

F32 = mybir.dt.float32
BF16 = mybir.dt.bfloat16
AF = mybir.ActivationFunctionType
ALU = mybir.AluOpType
AX = mybir.AxisListType.X

S = 4096
D = 1024
NT = S // 128
NC = D // 128
NEG = -1e9

ENGINES = ("pe", "dve", "act", "pool", "sp")
_ENG_ATTR = {"pe": "tensor", "dve": "vector", "act": "scalar", "pool": "gpsimd", "sp": "sync"}
_QUEUES = ("sp", "act", "pool")
_NROT = 4


class Op:
    __slots__ = ("build", "waits", "incs")

    def __init__(self, build):
        self.build = build
        self.waits = []
        self.incs = []


def _key(ap):
    t = getattr(ap, "tensor", ap)
    return t.name


class Tr:

    def __init__(self, nc):
        self.nc = nc
        self.ops = {e: [] for e in ENGINES}
        self.counts = {}
        self.mem = {}
        self.marks = {e: {} for e in ENGINES}
        self.sems = set()
        for e in ("pe", "dve", "act", "pool"):
            self._sem("S_" + e)
        self.qrot = {}
        for q in _QUEUES:
            for i in range(_NROT):
                self._sem(f"D_{q}{i}")
            self.qrot[q] = {"i": 0, "cum": [0] * _NROT}
        self.pe_pending_reads = []
        self.pe_open = set()
        self._last_op = {e: None for e in ENGINES}

    def _sem(self, name):
        self.sems.add(name)
        self.counts.setdefault(name, 0)

    def _rec(self, key):
        return self.mem.setdefault(key, {"w": None, "r": []})

    def _wait(self, op, eng, sem, val):
        if val <= 0:
            return
        wm = self.marks[eng]
        if wm.get(sem, 0) >= val:
            return
        op.waits.append((sem, val))
        wm[sem] = val

    def _engine_done_token(self, eng):
        return ("S_" + eng, self.counts["S_" + eng])

    def _flush_pe_reads(self, val):
        for key in self.pe_pending_reads:
            self._rec(key)["r"].append(("S_pe", val))
        self.pe_pending_reads = []

    def _force_pe_inc(self):
        if not self.pe_pending_reads:
            return
        last = self._last_op["pe"]
        assert last is not None
        self.counts["S_pe"] += 1
        v = self.counts["S_pe"]
        last.incs.append(("S_pe", 1))
        self._flush_pe_reads(v)

    def op(self, eng, build, reads=(), writes=(), inc=True,
           pe_start=True, pe_stop=True, is_mm=False):
        o = Op(build)
        for ap in reads:
            assert _key(ap) not in self.pe_open or eng == "pe", _key(ap)
        for ap in reads:
            k = _key(ap)
            r = self._rec(k)
            if r["w"] is not None:
                sem, val = r["w"]
                if not (eng == "pe" and sem == "S_pe"):
                    self._wait(o, eng, sem, val)
        for ap in writes:
            k = _key(ap)
            if k in self.pe_pending_reads and eng != "pe":
                self._force_pe_inc()
            r = self._rec(k)
            if r["w"] is not None:
                sem, val = r["w"]
                if not (eng == "pe" and sem == "S_pe"):
                    self._wait(o, eng, sem, val)
            for sem, val in r["r"]:
                if not (eng == "pe" and sem == "S_pe"):
                    self._wait(o, eng, sem, val)
        if eng == "pe":
            for ap in reads:
                self.pe_pending_reads.append(_key(ap))
            do_inc = inc and pe_stop
            if do_inc:
                self.counts["S_pe"] += 1
                v = self.counts["S_pe"]
                o.incs.append(("S_pe", 1))
                self._flush_pe_reads(v)
                for ap in writes:
                    rr = self._rec(_key(ap))
                    rr["w"] = ("S_pe", v)
                    rr["r"] = []
            else:
                pass
        else:
            sem = "S_" + eng
            self.counts[sem] += 1
            v = self.counts[sem]
            o.incs.append((sem, 1))
            for ap in writes:
                rr = self._rec(_key(ap))
                rr["w"] = (sem, v)
                rr["r"] = []
            for ap in reads:
                self._rec(_key(ap))["r"].append((sem, v))
        self.ops[eng].append(o)
        self._last_op[eng] = o
        return o

    def dma(self, q, dst, src, **kw):
        rot = self.qrot[q]
        slot = rot["i"] % _NROT
        rot["i"] += 1
        sem = f"D_{q}{slot}"
        o = Op(lambda e, dst=dst, src=src, kw=kw: e.dma_start(dst, src, **kw))
        if rot["cum"][slot] > 0:
            self._wait(o, q, sem, rot["cum"][slot])
        for ap in (src,):
            k = _key(ap)
            r = self._rec(k)
            if r["w"] is not None:
                self._wait(o, q, *r["w"])
        kd = _key(dst)
        if kd in self.pe_pending_reads:
            self._force_pe_inc()
        r = self._rec(kd)
        if r["w"] is not None:
            self._wait(o, q, *r["w"])
        for s_, v_ in r["r"]:
            self._wait(o, q, s_, v_)
        rot["cum"][slot] += 16
        self.counts[sem] = rot["cum"][slot]
        v = rot["cum"][slot]
        o.incs.append((sem, 16))
        rr = self._rec(kd)
        rr["w"] = (sem, v)
        rr["r"] = []
        self._rec(_key(src))["r"].append((sem, v))
        self.ops[q].append(o)
        self._last_op[q] = o
        return o

    def mm(self, out, lhsT, rhs, start=True, stop=True, **kw):
        k = _key(out)
        if start:
            assert k not in self.pe_open, k
        else:
            assert k in self.pe_open, k
        if stop:
            self.pe_open.discard(k)
        else:
            self.pe_open.add(k)
        return self.op(
            "pe",
            lambda e, o=out, l=lhsT, r=rhs, s=start, t=stop, kw=kw:
                e.matmul(o, l, r, start=s, stop=t, **kw),
            reads=[lhsT, rhs], writes=[out], pe_stop=stop)

    def trans(self, out, in_, ident, **kw):
        assert _key(out) not in self.pe_open
        return self.op(
            "pe",
            lambda e, o=out, i=in_, d=ident, kw=kw: e.transpose(o, i, d, **kw),
            reads=[in_, ident], writes=[out])

    def act(self, out, in_, func, bias=0.0, scale=1.0, accum_out=None):
        reads = [in_]
        writes = [out]
        if isinstance(bias, bass.AP):
            reads.append(bias)
        if isinstance(scale, bass.AP):
            reads.append(scale)
        if accum_out is not None:
            writes.append(accum_out)
        return self.op(
            "act",
            lambda e, o=out, i=in_, f=func, b=bias, s=scale, a=accum_out:
                e.activation(o, i, f, bias=b, scale=s, accum_out=a),
            reads=reads, writes=writes)

    def ee(self, eng, meth, out, *ins, extra_reads=(), **kw):
        reads = [a for a in ins if isinstance(a, bass.AP)] + list(extra_reads)
        for v in kw.values():
            if isinstance(v, bass.AP):
                reads.append(v)
        return self.op(
            eng,
            lambda e, m=meth, o=out, i=ins, kw=kw: getattr(e, m)(o, *i, **kw),
            reads=reads, writes=[out])

    def barrier(self):
        self._force_pe_inc()
        targets = [(s, v) for s, v in sorted(self.counts.items()) if v > 0]
        for e in ENGINES:
            o = Op(lambda en: en.nop())
            wm = self.marks[e]
            for s_, v_ in targets:
                if wm.get(s_, 0) < v_:
                    o.waits.append((s_, v_))
                    wm[s_] = v_
            self.ops[e].append(o)
            self._last_op[e] = o
        for rec in self.mem.values():
            rec["r"] = []

    def finalize(self):
        o = Op(lambda e: e.nop())
        for q in _QUEUES:
            rot = self.qrot[q]
            for i in range(_NROT):
                sem = f"D_{q}{i}"
                if rot["cum"][i] > 0:
                    o.waits.append((sem, rot["cum"][i]))
        for e in ("pe", "dve", "act", "pool"):
            sem = "S_" + e
            if self.counts[sem] > 0:
                o.waits.append((sem, self.counts[sem]))
        self.ops["sp"].append(o)

    def emit(self):
        nc = self.nc
        with contextlib.ExitStack() as es:
            handles = {}
            for name in sorted(self.sems):
                handles[name] = es.enter_context(nc.semaphore(name))
            blk = es.enter_context(nc.Block())

            def emit_engine(engobj, eng_name):
                for o in self.ops[eng_name]:
                    for sem, val in o.waits:
                        engobj.wait_ge(handles[sem], val)
                    inst = o.build(engobj)
                    for sem, n in o.incs:
                        inst.then_inc(handles[sem], n)

            blk.tensor(lambda e: emit_engine(e, "pe"))
            blk.vector(lambda e: emit_engine(e, "dve"))
            blk.scalar(lambda e: emit_engine(e, "act"))
            blk.gpsimd(lambda e: emit_engine(e, "pool"))
            blk.sync(lambda e: emit_engine(e, "sp"))



class Ctx:

    def __init__(self, tr, es):
        nc = tr.nc
        self.nc = nc
        self._bn = 0
        self.ident_bf = es.enter_context(nc.sbuf_tensor("identbf", [128, 128], BF16))
        self.ident_f = es.enter_context(nc.sbuf_tensor("identf", [128, 128], F32))
        self.mask = es.enter_context(nc.sbuf_tensor("maskdg", [128, 128], F32))
        self.ones = es.enter_context(nc.sbuf_tensor("onesr", [1, 128], F32))
        self.eps = es.enter_context(nc.sbuf_tensor("epsc", [128, 1], F32))
        t = tr
        t.ee("pool", "memset", self.eps[:], 1e-5)
        t.ee("pool", "memset", self.ones[:], 1.0)
        t.ee("pool", "memset", self.ident_bf[:], 0.0)
        t.op("pool", lambda e, a=self.ident_bf: e.affine_select(
            out=a[:], in_=a[:], compare_op=ALU.not_equal, fill=1.0, base=0,
            pattern=[[-1, 128]], channel_multiplier=1),
            reads=[self.ident_bf[:]], writes=[self.ident_bf[:]])
        t.ee("pool", "memset", self.ident_f[:], 0.0)
        t.op("pool", lambda e, a=self.ident_f: e.affine_select(
            out=a[:], in_=a[:], compare_op=ALU.not_equal, fill=1.0, base=0,
            pattern=[[-1, 128]], channel_multiplier=1),
            reads=[self.ident_f[:]], writes=[self.ident_f[:]])
        t.ee("pool", "memset", self.mask[:], 0.0)
        t.op("pool", lambda e, a=self.mask: e.affine_select(
            out=a[:], in_=a[:], compare_op=ALU.is_ge, fill=NEG, base=0,
            pattern=[[-1, 128]], channel_multiplier=1),
            reads=[self.mask[:]], writes=[self.mask[:]])

    def banks(self, es, n=8, width=512):
        self._bn += 1
        return [es.enter_context(self.nc.psum_tensor(f"pb{self._bn}_{i}", [128, width], F32))
                for i in range(n)]


def _alloc(es, nc, name, shape, dtype):
    return es.enter_context(nc.sbuf_tensor(name, shape, dtype))


def _memset(tr, ap, val, eng="pool"):
    tr.op(eng, lambda e, a=ap, v=val: e.memset(a, v), reads=[], writes=[ap])



def ph_ln(tr, es, C, T, src, dst, pref):
    nc = tr.nc
    nt = S // 128
    xt = [_alloc(es, nc, f"{pref}xt{i}", [128, 1024], F32) for i in range(2)]
    xnb = [_alloc(es, nc, f"{pref}xnb{i}", [128, 1024], BF16) for i in range(2)]
    sq = [_alloc(es, nc, f"{pref}sq{i}", [128, 1024], F32) for i in range(2)]
    sm = [_alloc(es, nc, f"{pref}sm{i}", [128, 8], F32) for i in range(2)]
    for t in range(nt):
        b = t % 2
        negm = sm[b][:, 0:1]
        ssq = sm[b][:, 1:2]
        var = sm[b][:, 2:3]
        std = sm[b][:, 3:4]
        rstd = sm[b][:, 4:5]
        nb = sm[b][:, 5:6]
        tr.dma("sp", xt[b][:], T[src][t * 128:(t + 1) * 128, :])
        tr.ee("dve", "reduce_sum", negm, xt[b][:], axis=AX, negate=True)
        tr.ee("dve", "tensor_scalar", negm, negm, 1.0 / 1024, None, op0=ALU.mult)
        tr.act(sq[b][:], xt[b][:], AF.Square, bias=negm, accum_out=ssq)
        tr.act(std, ssq, AF.Sqrt, bias=C.eps[:, 0:1], scale=1.0 / 1024)
        tr.ee("dve", "reciprocal", rstd, std)
        tr.ee("dve", "tensor_mul", nb, negm, rstd)
        tr.ee("dve", "tensor_scalar", xnb[b][:], xt[b][:], rstd, nb,
              op0=ALU.mult, op1=ALU.add)
        tr.dma("sp", T[dst][t * 128:(t + 1) * 128, :], xnb[b][:])


def ph_quant(tr, es, C, T):
    nc = tr.nc
    pb = C.banks(es, 8)
    pbb = [p.bitcast(BF16) for p in pb]
    wt = [_alloc(es, nc, f"qwt{i}", [128, 4096], F32) for i in range(2)]
    sgn = _alloc(es, nc, "qsgn", [128, 4096], BF16)
    absw = _alloc(es, nc, "qabs", [128, 4096], F32)
    mbit = _alloc(es, nc, "qmb", [128, 4096], BF16)
    wq = _alloc(es, nc, "qwq", [128, 4096], BF16)
    colacc = _alloc(es, nc, "qcol", [128, 1], F32)
    tmp1 = _alloc(es, nc, "qtmp", [128, 1], F32)
    oncol = _alloc(es, nc, "qonc", [128, 1], F32)
    thr = _alloc(es, nc, "qthr", [1, 1], F32)
    cbc = _alloc(es, nc, "qcbc", [128, 1], F32)
    wqT = _alloc(es, nc, "qwqT", [128, 8 * 4096], BF16)

    for (wname, oname, ntile, width) in (("W1", "wq1T", 32, 1024),
                                         ("W2", "wq2T", 8, 4096)):
        numel = ntile * 128 * width
        _memset(tr, colacc[:], 0.0, "dve")
        _memset(tr, oncol[:], 1.0, "dve")
        for t in range(ntile):
            b = t % 2
            tr.dma("sp", wt[b][:, 0:width], T[wname][t * 128:(t + 1) * 128, :])
            tr.ee("dve", "reduce_sum", tmp1[:], wt[b][:, 0:width], axis=AX,
                  apply_absolute_value=True)
            tr.ee("dve", "tensor_add", colacc[:], colacc[:], tmp1[:])
        tr.mm(pb[0][0:1, 0:1], colacc[:], oncol[:])
        tr.ee("dve", "tensor_scalar", thr[:], pb[0][0:1, 0:1], 0.5 / numel,
              0.5e-5, op0=ALU.mult, op1=ALU.add)
        tr.mm(pb[1][:, 0:1], C.ones[0:1, :], thr[:])
        tr.ee("dve", "tensor_copy", cbc[:], pb[1][:, 0:1])
        nchunk = width // 128
        for t in range(ntile):
            b = t % 2
            tr.dma("sp", wt[b][:, 0:width], T[wname][t * 128:(t + 1) * 128, :])
            tr.op("act", lambda e, o=sgn[:, 0:width], i=wt[b][:, 0:width]:
                  e.sign(o, i), reads=[wt[b][:, 0:width]], writes=[sgn[:, 0:width]])
            tr.act(absw[:, 0:width], wt[b][:, 0:width], AF.Abs)
            tr.ee("dve", "tensor_scalar", mbit[:, 0:width], absw[:, 0:width],
                  cbc[:, 0:1], None, op0=ALU.is_gt)
            tr.ee("dve", "tensor_mul", wq[:, 0:width], mbit[:, 0:width],
                  sgn[:, 0:width])
            for c0 in range(0, nchunk, 4):
                cg = min(4, nchunk - c0)
                for q in range(cg):
                    tr.trans(pbb[2][:, q * 128:(q + 1) * 128],
                             wq[:, (c0 + q) * 128:(c0 + q + 1) * 128],
                             C.ident_bf[:])
                dst = wqT[:].rearrange("p (c n) -> p c n", c=8 if wname == "W1" else 32)[
                    :, c0:c0 + cg, t * 128:t * 128 + 128]
                src = pbb[2][:].rearrange("p (q n) -> p q n", q=8)[:, 0:cg, 0:128]
                tr.ee("dve", "tensor_copy", dst, src)
        for c in range(nchunk):
            tr.dma("sp", T[oname][c * 128:(c + 1) * 128, :],
                   wqT[:, c * (32768 // nchunk):(c * (32768 // nchunk)) + (32768 // nchunk)])


def _load_xT(tr, dst_sb, src_dram, scol0, swidth):
    for c in range(8):
        tr.dma("sp", dst_sb[:, c * swidth:(c + 1) * swidth],
               src_dram[scol0:scol0 + swidth, c * 128:(c + 1) * 128],
               transpose=True)


def _load_nat(tr, dst_sb, src_dram, width=1024):
    nt = S // 128
    for g in range(4):
        t0, t1 = g * nt // 4, (g + 1) * nt // 4
        dst = dst_sb[:].rearrange("p (t d) -> p t d", d=width)[:, t0:t1, :]
        src = src_dram.rearrange("(t p) d -> p t d", p=128)[:, t0:t1, :]
        tr.dma("sp", dst, src)


def ph_attn0(tr, es, C, T):
    nc = tr.nc
    pb = C.banks(es, 8)
    pbb = [p.bitcast(BF16) for p in pb]
    nt = S // 128
    xnat = _alloc(es, nc, "a0nat", [128, nt * 1024], BF16)
    xT = _alloc(es, nc, "a0xT", [128, 8 * S], BF16)
    Rsb = _alloc(es, nc, "a0R", [128, 8 * 1024], BF16)
    encR = _alloc(es, nc, "a0enc", [128, 8 * 512], BF16)
    Sr0 = _alloc(es, nc, "a0S", [128, S], F32)
    Srow = [Sr0, Sr0]
    Prow = [_alloc(es, nc, f"a0P{i}", [128, S], BF16) for i in range(2)]
    PT0 = _alloc(es, nc, "a0PT", [128, S], BF16)
    PT = [PT0, PT0]
    Zs0 = _alloc(es, nc, "a0Z", [128, 1024], F32)
    Zsb = [Zs0, Zs0]
    sm = [_alloc(es, nc, f"a0sm{i}", [128, 8], F32) for i in range(2)]
    _load_nat(tr, xnat, T["xn0n"])
    _load_xT(tr, xT, T["xn0n"], 0, S)
    tr.dma("pool", Rsb[:].rearrange("p (c e) -> p c e", c=8),
           T["R"].rearrange("(c p) e -> p c e", p=128))

    def enc_part(qg):
        for co in range(8):
            for ci in range(8):
                tr.mm(pb[6][:, 0:512],
                      Rsb[:, ci * 1024 + co * 128: ci * 1024 + co * 128 + 128],
                      xT[:, ci * S + qg * 512: ci * S + qg * 512 + 512],
                      start=(ci == 0), stop=(ci == 7))
            tr.ee("dve", "tensor_copy", encR[:, co * 512:(co + 1) * 512],
                  pb[6][:, 0:512])

    def s_part(q):
        b = q % 2
        qt = q % 4
        klen = 128 * (q + 1)
        ng = (klen + 511) // 512
        for kg in range(ng):
            w = min(512, klen - kg * 512)
            bank = pb[kg % 3]
            for ci in range(8):
                tr.mm(bank[:, 0:w],
                      encR[:, ci * 512 + qt * 128: ci * 512 + qt * 128 + 128],
                      xT[:, ci * S + kg * 512: ci * S + kg * 512 + w],
                      start=(ci == 0), stop=(ci == 7))
            if kg == ng - 1:
                if w > 128:
                    tr.ee("dve", "tensor_copy",
                          Srow[b][:, kg * 512: kg * 512 + w - 128],
                          bank[:, 0:w - 128])
                tr.ee("dve", "tensor_add",
                      Srow[b][:, kg * 512 + w - 128: kg * 512 + w],
                      bank[:, w - 128:w], C.mask[:])
            else:
                tr.ee("dve", "tensor_copy", Srow[b][:, kg * 512: kg * 512 + w],
                      bank[:, 0:w])
        negmax = sm[b][:, 0:1]
        sumexp = sm[b][:, 1:2]
        tr.ee("dve", "reduce_max", negmax, Srow[b][:, 0:klen], axis=AX, negate=True)
        tr.act(Prow[b][:, 0:klen], Srow[b][:, 0:klen], AF.Exp, bias=negmax,
               accum_out=sumexp)
        tr.ee("dve", "reciprocal", sm[b][:, 2:3], sumexp)

    def pv_part(q):
        b = q % 2
        klen = 128 * (q + 1)
        nk = klen // 128
        for j0 in range(0, nk, 4):
            jg = min(4, nk - j0)
            for q_ in range(jg):
                tr.trans(pbb[7][:, q_ * 128:(q_ + 1) * 128],
                         Prow[b][:, (j0 + q_) * 128:(j0 + q_ + 1) * 128],
                         C.ident_bf[:])
            tr.ee("dve", "tensor_copy", PT[b][:, j0 * 128: j0 * 128 + jg * 128],
                  pbb[7][:, 0:jg * 128])
        for j in range(nk):
            for h in range(2):
                tr.mm(pb[4 + h][:, 0:512],
                      PT[b][:, j * 128:(j + 1) * 128],
                      xnat[:, j * 1024 + h * 512: j * 1024 + h * 512 + 512],
                      start=(j == 0), stop=(j == nk - 1))
        zb = Zsb[b]
        for h in range(2):
            tr.ee("dve", "tensor_scalar", zb[:, h * 512:(h + 1) * 512],
                  pb[4 + h][:, 0:512], sm[b][:, 2:3], None, op0=ALU.mult)
        tr.dma("sp", T["Z0"][q * 128:(q + 1) * 128, :], zb[:])

    enc_part(0)
    s_part(0)
    for q in range(nt):
        if q + 1 < nt:
            if (q + 1) % 4 == 0:
                enc_part((q + 1) // 4)
            s_part(q + 1)
        pv_part(q)


def ph_ffn0(tr, es, C, T):
    nc = tr.nc
    pb = C.banks(es, 8)
    pbb = [p.bitcast(BF16) for p in pb]
    xT = _alloc(es, nc, "f0xT", [128, 8 * S], BF16)
    w1T = _alloc(es, nc, "f0w1", [128, 8 * 4096], BF16)
    w2b = [_alloc(es, nc, f"f0w2{i}", [128, 1024], BF16) for i in range(2)]
    hsb = _alloc(es, nc, "f0h", [128, 32 * 512], BF16)
    b1s = _alloc(es, nc, "f0b1s", [32, 128], F32)
    b1T = _alloc(es, nc, "f0b1T", [128, 32], F32)
    b2r = _alloc(es, nc, "f0b2r", [1, 1024], F32)
    b2bc = _alloc(es, nc, "f0b2bc", [128, 1024], F32)
    xt = [_alloc(es, nc, f"f0x{i}", [128, 1024], F32) for i in range(2)]
    zt = [_alloc(es, nc, f"f0z{i}", [128, 1024], F32) for i in range(2)]
    jt = [_alloc(es, nc, f"f0j{i}", [128, 1024], F32) for i in range(2)]
    _load_xT(tr, xT, T["xn0n"], 0, S)
    for c in range(8):
        tr.dma("sp", w1T[:, c * 4096:(c + 1) * 4096],
               T["wq1T"][c * 128:(c + 1) * 128, :])
    tr.dma("sp", b1s[:], T["b1f"].rearrange("(a p) -> a p", p=128))
    tr.trans(pb[6][:, 0:32], b1s[:], C.ident_f[0:32, 0:32])
    tr.ee("dve", "tensor_copy", b1T[:], pb[6][:, 0:32])
    tr.dma("sp", b2r[:], T["b2f"][None, :])
    for eh in range(2):
        tr.mm(pb[7][:, 0:512], C.ones[0:1, :], b2r[0:1, eh * 512:(eh + 1) * 512])
        tr.ee("dve", "tensor_copy", b2bc[:, eh * 512:(eh + 1) * 512],
              pb[7][:, 0:512])
    for sg in range(S // 512):
        for fs in range(32):
            bank = pb[fs % 2]
            for ci in range(8):
                tr.mm(bank[:, 0:512],
                      w1T[:, ci * 4096 + fs * 128: ci * 4096 + fs * 128 + 128],
                      xT[:, ci * S + sg * 512: ci * S + sg * 512 + 512],
                      start=(ci == 0), stop=(ci == 7))
            tr.act(hsb[:, fs * 512:(fs + 1) * 512], bank[:, 0:512], AF.Relu,
                   bias=b1T[:, fs:fs + 1])
        for pair in range(2):
            for cc in range(32):
                tr.dma("sp", w2b[cc % 2][:], T["wq2T"][cc * 128:(cc + 1) * 128, :])
                for sl in range(2):
                    stl = pair * 2 + sl
                    for eh in range(2):
                        tr.mm(pb[2 + sl * 2 + eh][:, 0:512],
                              hsb[:, cc * 512 + stl * 128: cc * 512 + stl * 128 + 128],
                              w2b[cc % 2][:, eh * 512:(eh + 1) * 512],
                              start=(cc == 0), stop=(cc == 31))
            for sl in range(2):
                stl = pair * 2 + sl
                t_glob = sg * 4 + stl
                b = stl % 2
                tr.dma("sp", xt[b][:], T["x"][t_glob * 128:(t_glob + 1) * 128, :])
                tr.dma("sp", zt[b][:], T["Z0"][t_glob * 128:(t_glob + 1) * 128, :])
                for eh in range(2):
                    o = jt[b][:, eh * 512:(eh + 1) * 512]
                    tr.ee("dve", "tensor_add", o, pb[2 + sl * 2 + eh][:, 0:512],
                          zt[b][:, eh * 512:(eh + 1) * 512])
                    tr.ee("dve", "tensor_add", o, o, xt[b][:, eh * 512:(eh + 1) * 512])
                    tr.ee("dve", "tensor_add", o, o, b2bc[:, eh * 512:(eh + 1) * 512])
                tr.dma("sp", T["x1"][t_glob * 128:(t_glob + 1) * 128, :], jt[b][:])


def ph_qkvr(tr, es, C, T):
    nc = tr.nc
    pb = C.banks(es, 8)
    pbb = [p.bitcast(BF16) for p in pb]
    nt = S // 128
    xT = _alloc(es, nc, "qxT", [128, 8 * S], BF16)
    Wsb = {w: _alloc(es, nc, f"qW{w}", [128, 8 * 1024], BF16) for w in ("WQ", "WK", "WV")}
    bq8 = _alloc(es, nc, "qbq8", [128, 8], F32)
    bqs = _alloc(es, nc, "qbqs", [8, 128], F32)
    ktile = [_alloc(es, nc, f"qkt{i}", [128, 512], BF16) for i in range(2)]
    sqt = _alloc(es, nc, "qsqt", [128, 512], F32)
    sel = _alloc(es, nc, "qsel", [128, 2], F32)
    kn_g = _alloc(es, nc, "qkng", [1, 128], F32)
    qn_g = _alloc(es, nc, "qqng", [1, 128], F32)
    hrow = _alloc(es, nc, "qhrow", [1, 64], F32)
    vpv = _alloc(es, nc, "qvpv", [128, nt * 16 * 65], BF16)
    biasM = _alloc(es, nc, "qbM", [128, 16], F32)
    _load_xT(tr, xT, T["xn1n"], 0, S)
    for w in ("WQ", "WK", "WV"):
        tr.dma("pool", Wsb[w][:].rearrange("p (c e) -> p c e", c=8),
               T[w].rearrange("(c p) e -> p c e", p=128))
    tr.dma("sp", bqs[:], T["bQ"].rearrange("(a p) -> a p", p=128))
    tr.trans(pb[6][:, 0:8], bqs[:], C.ident_f[0:8, 0:8])
    tr.ee("dve", "tensor_scalar", bq8[:], pb[6][:, 0:8], 0.125, None, op0=ALU.mult)
    bkT = _alloc(es, nc, "qbkT", [128, 8], F32)
    tr.dma("sp", bqs[:], T["bK"].rearrange("(a p) -> a p", p=128))
    tr.trans(pb[6][:, 0:8], bqs[:], C.ident_f[0:8, 0:8])
    tr.ee("dve", "tensor_copy", bkT[:], pb[6][:, 0:8])
    _memset(tr, sel[:], 0.0, "dve")
    _memset(tr, sel[0:64, 0:1], 1.0, "dve")
    _memset(tr, sel[64:128, 1:2], 1.0, "dve")
    _memset(tr, vpv[:], 1.0, "pool")
    for (w, dstn, scale, do_bias, ng) in (("WK", "kT", 1.0, False, kn_g),
                                          ("WQ", "qT8", 0.125, True, qn_g)):
        for co in range(8):
            for sg in range(S // 512):
                bank = pb[(co + sg) % 2]
                for ci in range(8):
                    tr.mm(bank[:, 0:512],
                          Wsb[w][:, ci * 1024 + co * 128: ci * 1024 + co * 128 + 128],
                          xT[:, ci * S + sg * 512: ci * S + sg * 512 + 512],
                          start=(ci == 0), stop=(ci == 7))
                kb = ktile[(co + sg) % 2]
                if do_bias:
                    tr.ee("dve", "tensor_scalar", kb[:], bank[:, 0:512], scale,
                          bq8[:, co:co + 1], op0=ALU.mult, op1=ALU.add)
                else:
                    tr.ee("dve", "tensor_scalar", kb[:], bank[:, 0:512],
                          bkT[:, co:co + 1], None, op0=ALU.add)
                tr.dma("sp", T[dstn][co * 128:(co + 1) * 128, sg * 512:(sg + 1) * 512],
                       kb[:])
                tr.act(sqt[:], kb[:], AF.Square)
                for hh in range(2):
                    tr.mm(pb[4 + hh][0:1, 0:512], sel[:, hh:hh + 1], sqt[:])
                    h = co * 2 + hh
                    nsl = S // 512
                    tr.ee("dve", "reduce_max", ng[0:1, h * nsl + sg: h * nsl + sg + 1],
                          pb[4 + hh][0:1, 0:512], axis=AX)
    qmax = _alloc(es, nc, "qqmax", [1, 16], F32)
    kmax = _alloc(es, nc, "qkmax", [1, 16], F32)
    prod = _alloc(es, nc, "qprod", [1, 16], F32)
    nsl = S // 512
    for h in range(16):
        tr.ee("dve", "reduce_max", qmax[0:1, h:h + 1],
              qn_g[0:1, h * nsl:(h + 1) * nsl], axis=AX)
        tr.ee("dve", "reduce_max", kmax[0:1, h:h + 1],
              kn_g[0:1, h * nsl:(h + 1) * nsl], axis=AX)
    tr.ee("dve", "tensor_mul", prod[:], qmax[:], kmax[:])
    tr.act(prod[:], prod[:], AF.Sqrt)
    tr.ee("dve", "tensor_scalar", prod[:], prod[:], -1.0, None, op0=ALU.mult)
    tr.mm(pb[6][:, 0:16], C.ones[0:1, :], prod[:])
    tr.ee("dve", "tensor_copy", biasM[:], pb[6][:, 0:16])
    tr.dma("sp", T["biasM"], biasM[:])
    for st in range(nt):
        for dh in range(2):
            bank = pb[(st + dh) % 2]
            for ci in range(8):
                tr.mm(bank[:, 0:512],
                      xT[:, ci * S + st * 128: ci * S + st * 128 + 128],
                      Wsb["WV"][:, ci * 1024 + dh * 512: ci * 1024 + dh * 512 + 512],
                      start=(ci == 0), stop=(ci == 7))
            for hh in range(8):
                h = dh * 8 + hh
                tr.ee("dve", "tensor_copy",
                      vpv[:, (st * 16 + h) * 65:(st * 16 + h) * 65 + 64],
                      bank[:, hh * 64:(hh + 1) * 64])
    for g in range(4):
        w = nt * 16 * 65 // 4
        tr.dma("sp", T["vpv"][:, g * w:(g + 1) * w], vpv[:, g * w:(g + 1) * w])
    wof = _alloc(es, nc, "qwof", [64, 1024], F32)
    bv16 = _alloc(es, nc, "qbv16", [16, 64], F32)
    bv64 = _alloc(es, nc, "qbv64", [64, 16], F32)
    bvs = _alloc(es, nc, "qbvs", [64, 1], F32)
    bvwo = _alloc(es, nc, "qbvwo", [1, 1024], F32)
    bor = _alloc(es, nc, "qbor", [1, 1024], F32)
    jbr = _alloc(es, nc, "qjbr", [1, 1024], F32)
    tr.dma("sp", wof[:], T["WO"])
    tr.dma("sp", bv16[:], T["bV"].rearrange("(h d) -> h d", d=64))
    tr.trans(pb[6][0:64, 0:16], bv16[:], C.ident_f[0:16, 0:16])
    tr.ee("dve", "tensor_copy", bv64[:], pb[6][0:64, 0:16])
    tr.ee("dve", "reduce_sum", bvs[:], bv64[:], axis=AX)
    for eh in range(2):
        tr.mm(pb[7][0:1, 0:512], bvs[:], wof[:, eh * 512:(eh + 1) * 512])
        tr.ee("dve", "tensor_copy", bvwo[0:1, eh * 512:(eh + 1) * 512],
              pb[7][0:1, 0:512])
    tr.dma("sp", bor[:], T["bO"][None, :])
    tr.ee("dve", "scalar_tensor_tensor", jbr[:], bor[:], 16.0, bvwo[:],
          op0=ALU.mult, op1=ALU.add)
    tr.dma("sp", T["jbrow"], jbr[:])


def ph_router(tr, es, C, T):
    nc = tr.nc
    pb = C.banks(es, 8)
    pbb = [p.bitcast(BF16) for p in pb]
    nt = S // 128
    xT = _alloc(es, nc, "rxT", [128, 8 * S], BF16)
    _load_xT(tr, xT, T["xn1n"], 0, S)
    rwsb = _alloc(es, nc, "qrw", [128, 8 * 16], BF16)
    rbsb = _alloc(es, nc, "qrb", [16, 1], F32)
    lgT = _alloc(es, nc, "qlgT", [16, S], F32)
    lnt = _alloc(es, nc, "qlnt", [128, 16], F32)
    lw = [_alloc(es, nc, f"qlw{i}", [128, 16], F32) for i in range(2)]
    eqb = _alloc(es, nc, "qeq", [128, 16], F32)
    dport = _alloc(es, nc, "qd", [128, 16], F32)
    indb = _alloc(es, nc, "qind", [128, 16], F32)
    expb = _alloc(es, nc, "qexp", [128, 16], F32)
    peb = _alloc(es, nc, "qpe", [128, 16], F32)
    cechunk = _alloc(es, nc, "qcec", [128, 16], F32)
    ce_sb = _alloc(es, nc, "qce", [128, nt * 16], F32)
    ceT_sb = _alloc(es, nc, "qceT", [16, S], F32)
    smr = _alloc(es, nc, "qsmr", [128, 8], F32)
    tr.dma("pool", rwsb[:].rearrange("p (c e) -> p c e", c=8),
           T["rw"].rearrange("(c p) e -> p c e", p=128))
    tr.dma("sp", rbsb[:], T["rb"][:, None])
    for sg in range(S // 512):
        bank = pb[sg % 2]
        for ci in range(8):
            tr.mm(bank[0:16, 0:512], rwsb[:, ci * 16:(ci + 1) * 16],
                  xT[:, ci * S + sg * 512: ci * S + sg * 512 + 512],
                  start=(ci == 0), stop=(ci == 7))
        tr.ee("dve", "tensor_scalar", lgT[0:16, sg * 512:(sg + 1) * 512],
              bank[0:16, 0:512], rbsb[:, 0:1], None, op0=ALU.add)
    for st in range(nt):
        tr.trans(pb[2][:, 0:16], lgT[0:16, st * 128:(st + 1) * 128],
                 C.ident_f[0:16, 0:16])
        tr.ee("dve", "tensor_copy", lnt[:], pb[2][:, 0:16])
        mx = smr[:, 0:1]
        m1n = smr[:, 1:2]
        cur = lnt
        for it in range(4):
            tr.ee("dve", "reduce_max", mx, cur[:], axis=AX)
            if it == 0:
                tr.ee("dve", "tensor_scalar", m1n, mx, -1.0, None, op0=ALU.mult)
            tr.ee("dve", "tensor_scalar", eqb[:], cur[:], mx, None, op0=ALU.is_equal)
            nxt = lw[it % 2]
            tr.ee("dve", "scalar_tensor_tensor", nxt[:], eqb[:], -1e30, cur[:],
                  op0=ALU.mult, op1=ALU.add)
            cur = nxt
        tr.ee("dve", "tensor_sub", dport[:], lnt[:], cur[:])
        tr.ee("dve", "tensor_scalar", indb[:], dport[:], 1e20, None, op0=ALU.is_gt)
        tr.act(expb[:], lnt[:], AF.Exp, bias=m1n)
        tr.ee("dve", "tensor_mul", peb[:], expb[:], indb[:])
        ssum = smr[:, 2:3]
        rr = smr[:, 3:4]
        tr.ee("dve", "reduce_sum", ssum, peb[:], axis=AX)
        tr.ee("dve", "reciprocal", rr, ssum)
        tr.ee("dve", "tensor_scalar", cechunk[:], peb[:], rr, None, op0=ALU.mult)
        tr.ee("dve", "tensor_copy", ce_sb[:, st * 16:(st + 1) * 16], cechunk[:])
        tr.trans(pb[2][0:16, 0:128], cechunk[:], C.ident_f[:])
        tr.ee("dve", "tensor_copy", ceT_sb[0:16, st * 128:(st + 1) * 128],
              pb[2][0:16, 0:128])
    tr.dma("sp", T["ce"], ce_sb[:])
    tr.dma("sp", T["ceT"], ceT_sb[:])




def ph_attn1(tr, es, C, T):
    nc = tr.nc
    sp2 = [es.enter_context(nc.psum_tensor(f"t1sp{i}", [128, 1024], F32)) for i in range(2)]
    pv = [es.enter_context(nc.psum_tensor(f"t1pv{i}", [128, 512], F32)) for i in range(2)]
    pbc = es.enter_context(nc.psum_tensor("t1bc", [128, 512], F32))
    pwo = es.enter_context(nc.psum_tensor("t1wo", [128, 512], F32))
    nt = S // 128
    kT = _alloc(es, nc, "t1kT", [128, 8 * S], BF16)
    vpv = _alloc(es, nc, "t1v", [128, nt * 16 * 65], BF16)
    biasM = _alloc(es, nc, "t1bM", [128, 16], F32)
    biasMg = _alloc(es, nc, "t1bMg", [128, 1], F32)
    wobf = _alloc(es, nc, "t1wo_w", [64, 1024], BF16)
    Zacc = _alloc(es, nc, "t1Z", [64, S], F32)
    Zbf = _alloc(es, nc, "t1Zb", [64, 512], BF16)
    qsl = [_alloc(es, nc, f"t1q{i}", [128, 8 * 512], BF16) for i in range(2)]
    pt = [_alloc(es, nc, f"t1p{i}", [128, 1024], BF16) for i in range(2)]
    rp = _alloc(es, nc, "t1rp", [1, 512], F32)
    bcb = _alloc(es, nc, "t1bcs", [64, 512], F32)
    ztmp = _alloc(es, nc, "t1zt", [64, 512], F32)
    att = [_alloc(es, nc, f"t1a{i}", [128, 1024], F32) for i in range(2)]
    for c in range(8):
        tr.dma("sp", kT[:, c * S:(c + 1) * S], T["kT"][c * 128:(c + 1) * 128, :])
    for g in range(4):
        w = nt * 16 * 65 // 4
        tr.dma("sp", vpv[:, g * w:(g + 1) * w], T["vpv"][:, g * w:(g + 1) * w])
    tr.dma("sp", biasM[:], T["biasM"])
    tr.op("dve", lambda e, o=biasMg[:], i=biasM[:]: e.tensor_reduce(
        o, i, AX, ALU.min), reads=[biasM[:]], writes=[biasMg[:]])
    tr.dma("pool", wobf[:], T["WO"])

    def s_step(qg, pr, j):
        sp = sp2[j % 2]
        qlo = max(0, 128 * j - 512 * qg)
        qb = qsl[qg % 2]
        c = pr
        for hh in range(2):
            lo = hh * 64
            tr.mm(sp[:, hh * 512 + qlo: hh * 512 + 512],
                  kT[lo:lo + 64, c * S + j * 128: c * S + j * 128 + 128],
                  qb[lo:lo + 64, c * 512 + qlo: c * 512 + 512],
                  start=True, stop=True)

    def pv_step(qg, pr, j, nk):
        sp = sp2[j % 2]
        ptb = pt[j % 2]
        qlo = max(0, 128 * j - 512 * qg)
        wd = 512 - qlo
        src3 = sp[:].rearrange("p (h n) -> p h n", h=2)[:, :, qlo:512]
        dst3 = ptb[:].rearrange("p (h n) -> p h n", h=2)[:, :, qlo:512]
        tr.act(dst3, src3, AF.Exp, bias=biasMg[:, 0:1])
        if j >= 4 * qg:
            for hh in range(2):
                a = ptb[:, hh * 512 + qlo: hh * 512 + 512]
                tr.op("pool",
                      lambda e, a=a, wd=wd: e.affine_select(
                          out=a, in_=a, compare_op=ALU.is_ge, fill=0.0,
                          base=0, pattern=[[1, wd]], channel_multiplier=-1),
                      reads=[a], writes=[a])
        for hh in range(2):
            h = 2 * pr + hh
            tr.mm(pv[hh][0:65, qlo:512],
                  vpv[:, (j * 16 + h) * 65:(j * 16 + h) * 65 + 65],
                  ptb[:, hh * 512 + qlo: hh * 512 + 512],
                  start=(j == 0), stop=(j == nk - 1))

    for qg in range(S // 512):
        qb = qsl[qg % 2]
        tr.dma("sp", qb[:].rearrange("p (c s) -> p c s", c=8),
               T["qT8"].rearrange("(c p) s -> p c s", p=128)[:, :, qg * 512:(qg + 1) * 512])
        nk = 4 * (qg + 1)
        for pr in range(8):
            s_step(qg, pr, 0)
            for j in range(1, nk):
                s_step(qg, pr, j)
                pv_step(qg, pr, j - 1, nk)
            pv_step(qg, pr, nk - 1, nk)
            for hh in range(2):
                h = 2 * pr + hh
                tr.ee("dve", "reciprocal", rp[:], pv[hh][64:65, 0:512])
                tr.mm(pbc[0:64, 0:512], C.ones[0:1, 0:64], rp[:])
                tr.ee("dve", "tensor_copy", bcb[:], pbc[0:64, 0:512])
                zsl = Zacc[:, qg * 512:(qg + 1) * 512]
                if h == 0:
                    tr.ee("dve", "tensor_mul", zsl, pv[hh][0:64, 0:512], bcb[:])
                else:
                    tr.ee("dve", "tensor_mul", ztmp[:], pv[hh][0:64, 0:512], bcb[:])
                    tr.ee("dve", "tensor_add", zsl, zsl, ztmp[:])
        tr.ee("dve", "tensor_copy", Zbf[:], Zacc[:, qg * 512:(qg + 1) * 512])
        for qt in range(4):
            ab = att[qt % 2]
            for eh in range(2):
                tr.mm(pwo[:, 0:512], Zbf[:, qt * 128:(qt + 1) * 128],
                      wobf[:, eh * 512:(eh + 1) * 512])
                tr.ee("dve", "tensor_copy", ab[:, eh * 512:(eh + 1) * 512],
                      pwo[:, 0:512])
            q = qg * 4 + qt
            tr.dma("sp", T["attn1"][q * 128:(q + 1) * 128, :], ab[:])


def ph_moe(tr, es, C, T):
    nc = tr.nc
    pb = C.banks(es, 8)
    pbb = [p.bitcast(BF16) for p in pb]
    nt = S // 128
    SGT = min(2048, S)
    nsg = S // SGT
    stg = SGT // 128
    ssb = SGT // 512
    xTs = _alloc(es, nc, "mxT", [128, 8 * SGT], BF16)
    w1e = _alloc(es, nc, "mw1", [128, 8 * 2048], BF16)
    w2e = _alloc(es, nc, "mw2", [128, 16 * 1024], BF16)
    hsb = _alloc(es, nc, "mh", [128, 16 * 512], BF16)
    acc = _alloc(es, nc, "macc", [128, stg * 1024], F32)
    ce_sb = _alloc(es, nc, "mce", [128, stg * 16], F32)
    ceT_sb = _alloc(es, nc, "mceT", [16, SGT], BF16)
    b2sb = _alloc(es, nc, "mb2", [16, 1024], BF16)
    b1x = _alloc(es, nc, "mb1x", [16, 128], F32)
    b1T = _alloc(es, nc, "mb1T", [128, 16 * 16], F32)
    jbr = _alloc(es, nc, "mjbr", [1, 1024], F32)
    jbbc = _alloc(es, nc, "mjbbc", [128, 1024], F32)
    x1t = [_alloc(es, nc, "mx10", [128, 1024], F32)] * 2
    att = [_alloc(es, nc, "mat0", [128, 1024], F32)] * 2
    outt = [_alloc(es, nc, "mo0", [128, 1024], F32)] * 2
    tr.dma("pool", b2sb[:], T["b2e"])
    tr.dma("sp", jbr[:], T["jbrow"])
    for eh in range(2):
        tr.mm(pb[6][:, 0:512], C.ones[0:1, :], jbr[0:1, eh * 512:(eh + 1) * 512])
        tr.ee("dve", "tensor_copy", jbbc[:, eh * 512:(eh + 1) * 512],
              pb[6][:, 0:512])
    for e in range(16):
        tr.dma("sp", b1x[:], T["b1e"][e].rearrange("(a p) -> a p", p=128))
        tr.trans(pb[6][:, 0:16], b1x[:], C.ident_f[0:16, 0:16])
        tr.ee("dve", "tensor_copy", b1T[:, e * 16:(e + 1) * 16], pb[6][:, 0:16])
    for sg in range(nsg):
        _load_xT(tr, xTs, T["xn1n"], sg * SGT, SGT)
        tr.dma("sp", ce_sb[:], T["ce"][:, sg * stg * 16:(sg + 1) * stg * 16])
        tr.dma("pool", ceT_sb[:], T["ceT"][0:16, sg * SGT:(sg + 1) * SGT])
        for e in range(16):
            for half in range(2):
                tr.dma("pool",
                       w1e[:].rearrange("p (c i) -> p c i", c=8)[:, half * 4:(half + 1) * 4, :],
                       T["w1e"][e].rearrange("(c p) i -> p c i", p=128)[:, half * 4:(half + 1) * 4, :])
                tr.dma("pool",
                       w2e[:].rearrange("p (c i) -> p c i", c=16)[:, half * 8:(half + 1) * 8, :],
                       T["w2e"][e].rearrange("(c p) i -> p c i", p=128)[:, half * 8:(half + 1) * 8, :])
            for ssub in range(ssb):
                for isl in range(16):
                    bank = pb[isl % 2]
                    for ci in range(8):
                        tr.mm(bank[:, 0:512],
                              w1e[:, ci * 2048 + isl * 128: ci * 2048 + isl * 128 + 128],
                              xTs[:, ci * SGT + ssub * 512: ci * SGT + ssub * 512 + 512],
                              start=(ci == 0), stop=(ci == 7))
                    tr.act(hsb[:, isl * 512:(isl + 1) * 512], bank[:, 0:512],
                           AF.Gelu, bias=b1T[:, e * 16 + isl: e * 16 + isl + 1])
                for pairi in range(2):
                    for cc in range(16):
                        for sl in range(2):
                            stl = pairi * 2 + sl
                            for eh in range(2):
                                tr.mm(pb[2 + sl * 2 + eh][:, 0:512],
                                      hsb[:, cc * 512 + stl * 128: cc * 512 + stl * 128 + 128],
                                      w2e[:, cc * 1024 + eh * 512: cc * 1024 + eh * 512 + 512],
                                      start=(cc == 0), stop=(cc == 15))
                    for sl in range(2):
                        stl = pairi * 2 + sl
                        st_l = ssub * 4 + stl
                        cec = ce_sb[:, st_l * 16 + e: st_l * 16 + e + 1]
                        for eh in range(2):
                            a = acc[:, st_l * 1024 + eh * 512: st_l * 1024 + eh * 512 + 512]
                            bank = pb[2 + sl * 2 + eh]
                            if e == 0:
                                tr.ee("dve", "tensor_scalar", a, bank[:, 0:512],
                                      cec, None, op0=ALU.mult)
                            else:
                                tr.ee("dve", "scalar_tensor_tensor", a, bank[:, 0:512],
                                      cec, a, op0=ALU.mult, op1=ALU.add)
        for stl16 in range(stg):
            st_g = sg * stg + stl16
            b = stl16 % 2
            tr.dma("sp", x1t[b][:], T["x1"][st_g * 128:(st_g + 1) * 128, :])
            tr.dma("sp", att[b][:], T["attn1"][st_g * 128:(st_g + 1) * 128, :])
            for eh in range(2):
                bank = pb[6 + eh]
                tr.mm(bank[0:128, 0:512], ceT_sb[0:16, stl16 * 128:(stl16 + 1) * 128],
                      b2sb[0:16, eh * 512:(eh + 1) * 512])
                o = outt[b][:, eh * 512:(eh + 1) * 512]
                tr.ee("dve", "tensor_add", o, bank[:, 0:512],
                      acc[:, stl16 * 1024 + eh * 512: stl16 * 1024 + eh * 512 + 512])
                tr.ee("dve", "tensor_add", o, o, x1t[b][:, eh * 512:(eh + 1) * 512])
                tr.ee("dve", "tensor_add", o, o, att[b][:, eh * 512:(eh + 1) * 512])
                tr.ee("dve", "tensor_add", o, o, jbbc[:, eh * 512:(eh + 1) * 512])
            tr.dma("sp", T["out"][st_g * 128:(st_g + 1) * 128, :], outt[b][:])



INPUT_TENSORS = {
    "x": lambda: ([S, 1024], F32),
    "R": lambda: ([1024, 1024], F32),
    "W1": lambda: ([4096, 1024], F32),
    "b1f": lambda: ([4096], F32),
    "W2": lambda: ([1024, 4096], F32),
    "b2f": lambda: ([1024], F32),
    "WQ": lambda: ([1024, 1024], F32),
    "bQ": lambda: ([1024], F32),
    "bK": lambda: ([1024], F32),
    "WK": lambda: ([1024, 1024], F32),
    "WV": lambda: ([1024, 1024], F32),
    "WO": lambda: ([64, 1024], F32),
    "bO": lambda: ([1024], F32),
    "bV": lambda: ([1024], F32),
    "rw": lambda: ([1024, 16], F32),
    "rb": lambda: ([16], F32),
    "w1e": lambda: ([16, 1024, 2048], F32),
    "b1e": lambda: ([16, 2048], F32),
    "w2e": lambda: ([16, 2048, 1024], F32),
    "b2e": lambda: ([16, 1024], F32),
}
SCRATCH_TENSORS = {
    "xn0n": lambda: ([S, 1024], BF16),
    "Z0": lambda: ([S, 1024], F32),
    "wq1T": lambda: ([1024, 4096], BF16),
    "wq2T": lambda: ([4096, 1024], BF16),
    "x1": lambda: ([S, 1024], F32),
    "xn1n": lambda: ([S, 1024], BF16),
    "qT8": lambda: ([1024, S], BF16),
    "kT": lambda: ([1024, S], BF16),
    "vpv": lambda: ([128, (S // 128) * 16 * 65], BF16),
    "biasM": lambda: ([128, 16], F32),
    "jbrow": lambda: ([1, 1024], F32),
    "attn1": lambda: ([S, 1024], F32),
    "ce": lambda: ([128, (S // 128) * 16], F32),
    "ceT": lambda: ([16, S], F32),
    "out": lambda: ([S, 1024], F32),
}

PHASES = {
    "ln0": (lambda tr, es, C, T: ph_ln(tr, es, C, T, "x", "xn0n", "l0"),
            ["x"], ["xn0n"]),
    "quant": (ph_quant, ["W1", "W2"], ["wq1T", "wq2T"]),
    "attn0": (ph_attn0, ["xn0n", "R"], ["Z0"]),
    "ffn0": (ph_ffn0, ["xn0n", "wq1T", "wq2T", "b1f", "b2f", "x", "Z0"], ["x1"]),
    "ln1": (lambda tr, es, C, T: ph_ln(tr, es, C, T, "x1", "xn1n", "l1"),
            ["x1"], ["xn1n"]),
    "qkvr": (ph_qkvr, ["xn1n", "WQ", "bQ", "WK", "bK", "WV", "WO", "bO", "bV"],
             ["qT8", "kT", "vpv", "biasM", "jbrow"]),
    "router": (ph_router, ["xn1n", "rw", "rb"], ["ce", "ceT"]),
    "attn1": (ph_attn1, ["qT8", "kT", "vpv", "biasM", "WO"], ["attn1"]),
    "moe": (ph_moe, ["xn1n", "w1e", "b1e", "w2e", "b2e", "ce", "ceT", "x1",
                     "attn1", "jbrow"], ["out"]),
}

GROUPS = [["ln0"], ["quant"], ["attn0"], ["ffn0"], ["ln1"], ["qkvr"],
          ["router"], ["attn1"], ["moe"]]


def build_program(phase_names, need_later):
    reads, writes = set(), set()
    for p in phase_names:
        _, r, w = PHASES[p]
        reads.update(r)
        writes.update(w)
    nc = bass.Bass("TRN2", target_bir_lowering=False)
    T = {}
    in_names, out_names = [], []
    for name in sorted(reads | writes):
        if name in INPUT_TENSORS:
            shape, dt = INPUT_TENSORS[name]()
            kind = "ExternalInput"
        else:
            shape, dt = SCRATCH_TENSORS[name]()
            if name in writes:
                kind = "ExternalOutput" if (name in need_later or name == "out") else "Internal"
            else:
                kind = "ExternalInput"
        if kind == "ExternalInput":
            in_names.append(name)
        elif kind == "ExternalOutput":
            out_names.append(name)
        T[name] = nc.dram_tensor(name, shape, dt, kind=kind).ap()
    tr = Tr(nc)
    with contextlib.ExitStack() as es_ctx:
        C = Ctx(tr, es_ctx)
        for p in phase_names:
            fn, _, _ = PHASES[p]
            with contextlib.ExitStack() as es:
                fn(tr, es, C, T)
            tr.barrier()
        tr.finalize()
        tr.emit()
    return nc, in_names, out_names


def _values_from_inputs(inputs):
    c = np.ascontiguousarray
    xs = [c(inputs["L0_attn_encoding"][i]) for i in range(8)]
    shared = {
        "R": inputs["L0_attn_R"], "W1": inputs["L0_ffn_W1"], "b1f": inputs["L0_ffn_b1"],
        "W2": inputs["L0_ffn_W2"], "b2f": inputs["L0_ffn_b2"],
        "WQ": inputs["L1_attn_WQ"], "bQ": inputs["L1_attn_bQ"], "bK": inputs["L1_attn_bK"],
        "WK": inputs["L1_attn_WK"], "WV": inputs["L1_attn_WV"],
        "WO": inputs["L1_attn_WO"], "bO": inputs["L1_attn_bO"],
        "bV": inputs["L1_attn_bV"],
        "rw": inputs["L1_ffn_router_w"], "rb": inputs["L1_ffn_router_b"],
        "w1e": inputs["L1_ffn_w1"], "b1e": inputs["L1_ffn_b1"],
        "w2e": inputs["L1_ffn_w2"], "b2e": inputs["L1_ffn_b2"],
    }
    vals = {"x": xs}
    for k, v in shared.items():
        a = c(np.asarray(v))
        vals[k] = [a] * 8
    return vals


FUSED = True


def kernel(**inputs):
    vals = _values_from_inputs(inputs)
    groups = [[p for g in GROUPS for p in g]] if FUSED else GROUPS
    for gi, g in enumerate(groups):
        later = set()
        for g2 in groups[gi + 1:]:
            for p in g2:
                later.update(PHASES[p][1])
        nc, in_names, out_names = build_program(g, later)
        in_maps = [{n: vals[n][i] for n in in_names} for i in range(8)]
        res = run_bass_kernel_spmd(nc, in_maps, core_ids=list(range(8)))
        for n in out_names:
            vals[n] = [np.asarray(res.results[i][n]) for i in range(8)]
    return np.stack([vals["out"][i] for i in range(8)], axis=0)


# version 07
# speedup vs baseline: 1.2344x; 1.0713x over previous
import math
import contextlib
import numpy as np
import ml_dtypes
import concourse.bass as bass
import concourse.mybir as mybir
from concourse.bass_utils import run_bass_kernel_spmd

F32 = mybir.dt.float32
BF16 = mybir.dt.bfloat16
AF = mybir.ActivationFunctionType
ALU = mybir.AluOpType
AX = mybir.AxisListType.X

S = 4096
D = 1024
NT = S // 128
NC = D // 128
NEG = -1e9

ENGINES = ("pe", "dve", "act", "pool", "sp")
_ENG_ATTR = {"pe": "tensor", "dve": "vector", "act": "scalar", "pool": "gpsimd", "sp": "sync"}
_QUEUES = ("sp", "act", "pool")
_NROT = 4


class Op:
    __slots__ = ("build", "waits", "incs")

    def __init__(self, build):
        self.build = build
        self.waits = []
        self.incs = []


def _key(ap):
    t = getattr(ap, "tensor", ap)
    return t.name


class Tr:

    def __init__(self, nc):
        self.nc = nc
        self.ops = {e: [] for e in ENGINES}
        self.counts = {}
        self.mem = {}
        self.marks = {e: {} for e in ENGINES}
        self.sems = set()
        for e in ("pe", "dve", "act", "pool"):
            self._sem("S_" + e)
        self.qrot = {}
        for q in _QUEUES:
            for i in range(_NROT):
                self._sem(f"D_{q}{i}")
            self.qrot[q] = {"i": 0, "cum": [0] * _NROT}
        self.pe_pending_reads = []
        self.pe_open = set()
        self._last_op = {e: None for e in ENGINES}

    def _sem(self, name):
        self.sems.add(name)
        self.counts.setdefault(name, 0)

    def _rec(self, key):
        return self.mem.setdefault(key, {"w": None, "r": []})

    def _wait(self, op, eng, sem, val):
        if val <= 0:
            return
        wm = self.marks[eng]
        if wm.get(sem, 0) >= val:
            return
        op.waits.append((sem, val))
        wm[sem] = val

    def _engine_done_token(self, eng):
        return ("S_" + eng, self.counts["S_" + eng])

    def _flush_pe_reads(self, val):
        for key in self.pe_pending_reads:
            self._rec(key)["r"].append(("S_pe", val))
        self.pe_pending_reads = []

    def _force_pe_inc(self):
        if not self.pe_pending_reads:
            return
        last = self._last_op["pe"]
        assert last is not None
        self.counts["S_pe"] += 1
        v = self.counts["S_pe"]
        last.incs.append(("S_pe", 1))
        self._flush_pe_reads(v)

    def op(self, eng, build, reads=(), writes=(), inc=True,
           pe_start=True, pe_stop=True, is_mm=False):
        o = Op(build)
        for ap in reads:
            assert _key(ap) not in self.pe_open or eng == "pe", _key(ap)
        for ap in reads:
            k = _key(ap)
            r = self._rec(k)
            if r["w"] is not None:
                sem, val = r["w"]
                if not (eng == "pe" and sem == "S_pe"):
                    self._wait(o, eng, sem, val)
        for ap in writes:
            k = _key(ap)
            if k in self.pe_pending_reads and eng != "pe":
                self._force_pe_inc()
            r = self._rec(k)
            if r["w"] is not None:
                sem, val = r["w"]
                if not (eng == "pe" and sem == "S_pe"):
                    self._wait(o, eng, sem, val)
            for sem, val in r["r"]:
                if not (eng == "pe" and sem == "S_pe"):
                    self._wait(o, eng, sem, val)
        if eng == "pe":
            for ap in reads:
                self.pe_pending_reads.append(_key(ap))
            do_inc = inc and pe_stop
            if do_inc:
                self.counts["S_pe"] += 1
                v = self.counts["S_pe"]
                o.incs.append(("S_pe", 1))
                self._flush_pe_reads(v)
                for ap in writes:
                    rr = self._rec(_key(ap))
                    rr["w"] = ("S_pe", v)
                    rr["r"] = []
            else:
                pass
        else:
            sem = "S_" + eng
            self.counts[sem] += 1
            v = self.counts[sem]
            o.incs.append((sem, 1))
            for ap in writes:
                rr = self._rec(_key(ap))
                rr["w"] = (sem, v)
                rr["r"] = []
            for ap in reads:
                self._rec(_key(ap))["r"].append((sem, v))
        self.ops[eng].append(o)
        self._last_op[eng] = o
        return o

    def dma(self, q, dst, src, **kw):
        rot = self.qrot[q]
        slot = rot["i"] % _NROT
        rot["i"] += 1
        sem = f"D_{q}{slot}"
        o = Op(lambda e, dst=dst, src=src, kw=kw: e.dma_start(dst, src, **kw))
        if rot["cum"][slot] > 0:
            self._wait(o, q, sem, rot["cum"][slot])
        for ap in (src,):
            k = _key(ap)
            r = self._rec(k)
            if r["w"] is not None:
                self._wait(o, q, *r["w"])
        kd = _key(dst)
        if kd in self.pe_pending_reads:
            self._force_pe_inc()
        r = self._rec(kd)
        if r["w"] is not None:
            self._wait(o, q, *r["w"])
        for s_, v_ in r["r"]:
            self._wait(o, q, s_, v_)
        rot["cum"][slot] += 16
        self.counts[sem] = rot["cum"][slot]
        v = rot["cum"][slot]
        o.incs.append((sem, 16))
        rr = self._rec(kd)
        rr["w"] = (sem, v)
        rr["r"] = []
        self._rec(_key(src))["r"].append((sem, v))
        self.ops[q].append(o)
        self._last_op[q] = o
        return o

    def mm(self, out, lhsT, rhs, start=True, stop=True, **kw):
        k = _key(out)
        if start:
            assert k not in self.pe_open, k
        else:
            assert k in self.pe_open, k
        if stop:
            self.pe_open.discard(k)
        else:
            self.pe_open.add(k)
        return self.op(
            "pe",
            lambda e, o=out, l=lhsT, r=rhs, s=start, t=stop, kw=kw:
                e.matmul(o, l, r, start=s, stop=t, **kw),
            reads=[lhsT, rhs], writes=[out], pe_stop=stop)

    def trans(self, out, in_, ident, **kw):
        assert _key(out) not in self.pe_open
        return self.op(
            "pe",
            lambda e, o=out, i=in_, d=ident, kw=kw: e.transpose(o, i, d, **kw),
            reads=[in_, ident], writes=[out])

    def act(self, out, in_, func, bias=0.0, scale=1.0, accum_out=None):
        reads = [in_]
        writes = [out]
        if isinstance(bias, bass.AP):
            reads.append(bias)
        if isinstance(scale, bass.AP):
            reads.append(scale)
        if accum_out is not None:
            writes.append(accum_out)
        return self.op(
            "act",
            lambda e, o=out, i=in_, f=func, b=bias, s=scale, a=accum_out:
                e.activation(o, i, f, bias=b, scale=s, accum_out=a),
            reads=reads, writes=writes)

    def ee(self, eng, meth, out, *ins, extra_reads=(), **kw):
        reads = [a for a in ins if isinstance(a, bass.AP)] + list(extra_reads)
        for v in kw.values():
            if isinstance(v, bass.AP):
                reads.append(v)
        return self.op(
            eng,
            lambda e, m=meth, o=out, i=ins, kw=kw: getattr(e, m)(o, *i, **kw),
            reads=reads, writes=[out])

    def barrier(self):
        self._force_pe_inc()
        targets = [(s, v) for s, v in sorted(self.counts.items()) if v > 0]
        for e in ENGINES:
            o = Op(lambda en: en.nop())
            wm = self.marks[e]
            for s_, v_ in targets:
                if wm.get(s_, 0) < v_:
                    o.waits.append((s_, v_))
                    wm[s_] = v_
            self.ops[e].append(o)
            self._last_op[e] = o
        for rec in self.mem.values():
            rec["r"] = []

    def finalize(self):
        o = Op(lambda e: e.nop())
        for q in _QUEUES:
            rot = self.qrot[q]
            for i in range(_NROT):
                sem = f"D_{q}{i}"
                if rot["cum"][i] > 0:
                    o.waits.append((sem, rot["cum"][i]))
        for e in ("pe", "dve", "act", "pool"):
            sem = "S_" + e
            if self.counts[sem] > 0:
                o.waits.append((sem, self.counts[sem]))
        self.ops["sp"].append(o)

    def emit(self):
        nc = self.nc
        with contextlib.ExitStack() as es:
            handles = {}
            for name in sorted(self.sems):
                handles[name] = es.enter_context(nc.semaphore(name))
            blk = es.enter_context(nc.Block())

            def emit_engine(engobj, eng_name):
                for o in self.ops[eng_name]:
                    for sem, val in o.waits:
                        engobj.wait_ge(handles[sem], val)
                    inst = o.build(engobj)
                    for sem, n in o.incs:
                        inst.then_inc(handles[sem], n)

            blk.tensor(lambda e: emit_engine(e, "pe"))
            blk.vector(lambda e: emit_engine(e, "dve"))
            blk.scalar(lambda e: emit_engine(e, "act"))
            blk.gpsimd(lambda e: emit_engine(e, "pool"))
            blk.sync(lambda e: emit_engine(e, "sp"))



class Ctx:

    def __init__(self, tr, es):
        nc = tr.nc
        self.nc = nc
        self._bn = 0
        self.ident_bf = es.enter_context(nc.sbuf_tensor("identbf", [128, 128], BF16))
        self.ident_f = es.enter_context(nc.sbuf_tensor("identf", [128, 128], F32))
        self.mask = es.enter_context(nc.sbuf_tensor("maskdg", [128, 128], F32))
        self.ones = es.enter_context(nc.sbuf_tensor("onesr", [1, 128], F32))
        self.eps = es.enter_context(nc.sbuf_tensor("epsc", [128, 1], F32))
        t = tr
        t.ee("pool", "memset", self.eps[:], 1e-5)
        t.ee("pool", "memset", self.ones[:], 1.0)
        t.ee("pool", "memset", self.ident_bf[:], 0.0)
        t.op("pool", lambda e, a=self.ident_bf: e.affine_select(
            out=a[:], in_=a[:], compare_op=ALU.not_equal, fill=1.0, base=0,
            pattern=[[-1, 128]], channel_multiplier=1),
            reads=[self.ident_bf[:]], writes=[self.ident_bf[:]])
        t.ee("pool", "memset", self.ident_f[:], 0.0)
        t.op("pool", lambda e, a=self.ident_f: e.affine_select(
            out=a[:], in_=a[:], compare_op=ALU.not_equal, fill=1.0, base=0,
            pattern=[[-1, 128]], channel_multiplier=1),
            reads=[self.ident_f[:]], writes=[self.ident_f[:]])
        t.ee("pool", "memset", self.mask[:], 0.0)
        t.op("pool", lambda e, a=self.mask: e.affine_select(
            out=a[:], in_=a[:], compare_op=ALU.is_ge, fill=NEG, base=0,
            pattern=[[-1, 128]], channel_multiplier=1),
            reads=[self.mask[:]], writes=[self.mask[:]])

    def banks(self, es, n=8, width=512):
        self._bn += 1
        return [es.enter_context(self.nc.psum_tensor(f"pb{self._bn}_{i}", [128, width], F32))
                for i in range(n)]


def _alloc(es, nc, name, shape, dtype):
    return es.enter_context(nc.sbuf_tensor(name, shape, dtype))


def _memset(tr, ap, val, eng="pool"):
    tr.op(eng, lambda e, a=ap, v=val: e.memset(a, v), reads=[], writes=[ap])



def ph_ln(tr, es, C, T, src, dst, pref):
    nc = tr.nc
    nt = S // 128
    xt = [_alloc(es, nc, f"{pref}xt{i}", [128, 1024], F32) for i in range(2)]
    xnb = [_alloc(es, nc, f"{pref}xnb{i}", [128, 1024], BF16) for i in range(2)]
    sq = [_alloc(es, nc, f"{pref}sq{i}", [128, 1024], F32) for i in range(2)]
    sm = [_alloc(es, nc, f"{pref}sm{i}", [128, 8], F32) for i in range(2)]
    for t in range(nt):
        b = t % 2
        negm = sm[b][:, 0:1]
        ssq = sm[b][:, 1:2]
        var = sm[b][:, 2:3]
        std = sm[b][:, 3:4]
        rstd = sm[b][:, 4:5]
        nb = sm[b][:, 5:6]
        tr.dma("sp", xt[b][:], T[src][t * 128:(t + 1) * 128, :])
        tr.ee("dve", "reduce_sum", negm, xt[b][:], axis=AX, negate=True)
        tr.ee("dve", "tensor_scalar", negm, negm, 1.0 / 1024, None, op0=ALU.mult)
        tr.act(sq[b][:], xt[b][:], AF.Square, bias=negm, accum_out=ssq)
        tr.act(std, ssq, AF.Sqrt, bias=C.eps[:, 0:1], scale=1.0 / 1024)
        tr.ee("dve", "reciprocal", rstd, std)
        tr.ee("dve", "tensor_mul", nb, negm, rstd)
        tr.ee("dve", "tensor_scalar", xnb[b][:], xt[b][:], rstd, nb,
              op0=ALU.mult, op1=ALU.add)
        tr.dma("sp", T[dst][t * 128:(t + 1) * 128, :], xnb[b][:])


def ph_quant(tr, es, C, T):
    nc = tr.nc
    pb = C.banks(es, 8)
    pbb = [p.bitcast(BF16) for p in pb]
    wt = [_alloc(es, nc, f"qwt{i}", [128, 4096], F32) for i in range(2)]
    sgn = _alloc(es, nc, "qsgn", [128, 4096], BF16)
    absw = _alloc(es, nc, "qabs", [128, 4096], F32)
    mbit = _alloc(es, nc, "qmb", [128, 4096], BF16)
    wq = _alloc(es, nc, "qwq", [128, 4096], BF16)
    colacc = _alloc(es, nc, "qcol", [128, 1], F32)
    tmp1 = _alloc(es, nc, "qtmp", [128, 1], F32)
    oncol = _alloc(es, nc, "qonc", [128, 1], F32)
    thr = _alloc(es, nc, "qthr", [1, 1], F32)
    cbc = _alloc(es, nc, "qcbc", [128, 1], F32)
    wqT = _alloc(es, nc, "qwqT", [128, 8 * 4096], BF16)

    for (wname, oname, ntile, width) in (("W1", "wq1T", 32, 1024),
                                         ("W2", "wq2T", 8, 4096)):
        numel = ntile * 128 * width
        _memset(tr, colacc[:], 0.0, "dve")
        _memset(tr, oncol[:], 1.0, "dve")
        for t in range(ntile):
            b = t % 2
            tr.dma("sp", wt[b][:, 0:width], T[wname][t * 128:(t + 1) * 128, :])
            tr.ee("dve", "reduce_sum", tmp1[:], wt[b][:, 0:width], axis=AX,
                  apply_absolute_value=True)
            tr.ee("dve", "tensor_add", colacc[:], colacc[:], tmp1[:])
        tr.mm(pb[0][0:1, 0:1], colacc[:], oncol[:])
        tr.ee("dve", "tensor_scalar", thr[:], pb[0][0:1, 0:1], 0.5 / numel,
              0.5e-5, op0=ALU.mult, op1=ALU.add)
        tr.mm(pb[1][:, 0:1], C.ones[0:1, :], thr[:])
        tr.ee("dve", "tensor_copy", cbc[:], pb[1][:, 0:1])
        nchunk = width // 128
        for t in range(ntile):
            b = t % 2
            tr.dma("sp", wt[b][:, 0:width], T[wname][t * 128:(t + 1) * 128, :])
            tr.op("act", lambda e, o=sgn[:, 0:width], i=wt[b][:, 0:width]:
                  e.sign(o, i), reads=[wt[b][:, 0:width]], writes=[sgn[:, 0:width]])
            tr.act(absw[:, 0:width], wt[b][:, 0:width], AF.Abs)
            tr.ee("dve", "tensor_scalar", mbit[:, 0:width], absw[:, 0:width],
                  cbc[:, 0:1], None, op0=ALU.is_gt)
            tr.ee("dve", "tensor_mul", wq[:, 0:width], mbit[:, 0:width],
                  sgn[:, 0:width])
            for c0 in range(0, nchunk, 4):
                cg = min(4, nchunk - c0)
                for q in range(cg):
                    tr.trans(pbb[2][:, q * 128:(q + 1) * 128],
                             wq[:, (c0 + q) * 128:(c0 + q + 1) * 128],
                             C.ident_bf[:])
                dst = wqT[:].rearrange("p (c n) -> p c n", c=8 if wname == "W1" else 32)[
                    :, c0:c0 + cg, t * 128:t * 128 + 128]
                src = pbb[2][:].rearrange("p (q n) -> p q n", q=8)[:, 0:cg, 0:128]
                tr.ee("dve", "tensor_copy", dst, src)
        for c in range(nchunk):
            tr.dma("sp", T[oname][c * 128:(c + 1) * 128, :],
                   wqT[:, c * (32768 // nchunk):(c * (32768 // nchunk)) + (32768 // nchunk)])


def _load_xT(tr, dst_sb, src_dram, scol0, swidth):
    for c in range(8):
        tr.dma("sp", dst_sb[:, c * swidth:(c + 1) * swidth],
               src_dram[scol0:scol0 + swidth, c * 128:(c + 1) * 128],
               transpose=True)


def _load_nat(tr, dst_sb, src_dram, width=1024):
    nt = S // 128
    for g in range(4):
        t0, t1 = g * nt // 4, (g + 1) * nt // 4
        dst = dst_sb[:].rearrange("p (t d) -> p t d", d=width)[:, t0:t1, :]
        src = src_dram.rearrange("(t p) d -> p t d", p=128)[:, t0:t1, :]
        tr.dma("sp", dst, src)


def ph_attn0(tr, es, C, T):
    nc = tr.nc
    pb = C.banks(es, 8)
    pbb = [p.bitcast(BF16) for p in pb]
    nt = S // 128
    xnat = _alloc(es, nc, "a0nat", [128, nt * 1024], BF16)
    xT = _alloc(es, nc, "a0xT", [128, 8 * S], BF16)
    Rsb = _alloc(es, nc, "a0R", [128, 8 * 1024], BF16)
    encR = _alloc(es, nc, "a0enc", [128, 8 * 512], BF16)
    Sr0 = _alloc(es, nc, "a0S", [128, S], F32)
    Srow = [Sr0, Sr0]
    Prow = [_alloc(es, nc, f"a0P{i}", [128, S], BF16) for i in range(2)]
    PT0 = _alloc(es, nc, "a0PT", [128, S], BF16)
    PT = [PT0, PT0]
    Zs0 = _alloc(es, nc, "a0Z", [128, 1024], F32)
    Zsb = [Zs0, Zs0]
    sm = [_alloc(es, nc, f"a0sm{i}", [128, 8], F32) for i in range(2)]
    _load_nat(tr, xnat, T["xn0n"])
    _load_xT(tr, xT, T["xn0n"], 0, S)
    tr.dma("pool", Rsb[:].rearrange("p (c e) -> p c e", c=8),
           T["R"].rearrange("(c p) e -> p c e", p=128))

    def enc_part(qg):
        for co in range(8):
            for ci in range(8):
                tr.mm(pb[6][:, 0:512],
                      Rsb[:, ci * 1024 + co * 128: ci * 1024 + co * 128 + 128],
                      xT[:, ci * S + qg * 512: ci * S + qg * 512 + 512],
                      start=(ci == 0), stop=(ci == 7))
            tr.ee("dve", "tensor_copy", encR[:, co * 512:(co + 1) * 512],
                  pb[6][:, 0:512])

    def s_part(q):
        b = q % 2
        qt = q % 4
        klen = 128 * (q + 1)
        ng = (klen + 511) // 512
        for kg in range(ng):
            w = min(512, klen - kg * 512)
            bank = pb[kg % 3]
            for ci in range(8):
                tr.mm(bank[:, 0:w],
                      encR[:, ci * 512 + qt * 128: ci * 512 + qt * 128 + 128],
                      xT[:, ci * S + kg * 512: ci * S + kg * 512 + w],
                      start=(ci == 0), stop=(ci == 7))
            if kg == ng - 1:
                if w > 128:
                    tr.ee("dve", "tensor_copy",
                          Srow[b][:, kg * 512: kg * 512 + w - 128],
                          bank[:, 0:w - 128])
                tr.ee("dve", "tensor_add",
                      Srow[b][:, kg * 512 + w - 128: kg * 512 + w],
                      bank[:, w - 128:w], C.mask[:])
            else:
                tr.ee("dve", "tensor_copy", Srow[b][:, kg * 512: kg * 512 + w],
                      bank[:, 0:w])
        negmax = sm[b][:, 0:1]
        sumexp = sm[b][:, 1:2]
        tr.ee("dve", "reduce_max", negmax, Srow[b][:, 0:klen], axis=AX, negate=True)
        tr.act(Prow[b][:, 0:klen], Srow[b][:, 0:klen], AF.Exp, bias=negmax,
               accum_out=sumexp)
        tr.ee("dve", "reciprocal", sm[b][:, 2:3], sumexp)

    def pv_part(q):
        b = q % 2
        klen = 128 * (q + 1)
        nk = klen // 128
        for j0 in range(0, nk, 4):
            jg = min(4, nk - j0)
            for q_ in range(jg):
                tr.trans(pbb[7][:, q_ * 128:(q_ + 1) * 128],
                         Prow[b][:, (j0 + q_) * 128:(j0 + q_ + 1) * 128],
                         C.ident_bf[:])
            tr.ee("dve", "tensor_copy", PT[b][:, j0 * 128: j0 * 128 + jg * 128],
                  pbb[7][:, 0:jg * 128])
        for j in range(nk):
            for h in range(2):
                tr.mm(pb[4 + h][:, 0:512],
                      PT[b][:, j * 128:(j + 1) * 128],
                      xnat[:, j * 1024 + h * 512: j * 1024 + h * 512 + 512],
                      start=(j == 0), stop=(j == nk - 1))
        zb = Zsb[b]
        for h in range(2):
            tr.ee("dve", "tensor_scalar", zb[:, h * 512:(h + 1) * 512],
                  pb[4 + h][:, 0:512], sm[b][:, 2:3], None, op0=ALU.mult)
        tr.dma("sp", T["Z0"][q * 128:(q + 1) * 128, :], zb[:])

    enc_part(0)
    s_part(0)
    for q in range(nt):
        if q + 1 < nt:
            if (q + 1) % 4 == 0:
                enc_part((q + 1) // 4)
            s_part(q + 1)
        pv_part(q)


def ph_ffn0(tr, es, C, T):
    nc = tr.nc
    pb = C.banks(es, 8)
    pbb = [p.bitcast(BF16) for p in pb]
    xT = _alloc(es, nc, "f0xT", [128, 8 * S], BF16)
    w1T = _alloc(es, nc, "f0w1", [128, 8 * 4096], BF16)
    w2b = [_alloc(es, nc, f"f0w2{i}", [128, 1024], BF16) for i in range(2)]
    hsp = [_alloc(es, nc, f"f0h{i}", [128, 8 * 512], BF16) for i in range(4)]
    b1s = _alloc(es, nc, "f0b1s", [32, 128], F32)
    b1T = _alloc(es, nc, "f0b1T", [128, 32], F32)
    b2r = _alloc(es, nc, "f0b2r", [1, 1024], F32)
    b2bc = _alloc(es, nc, "f0b2bc", [128, 1024], F32)
    xt0 = _alloc(es, nc, "f0x", [128, 1024], F32)
    xt = [xt0, xt0]
    zt0 = _alloc(es, nc, "f0z", [128, 1024], F32)
    zt = [zt0, zt0]
    jt0 = _alloc(es, nc, "f0j", [128, 1024], F32)
    jt = [jt0, jt0]
    _load_xT(tr, xT, T["xn0n"], 0, S)
    for c in range(8):
        tr.dma("sp", w1T[:, c * 4096:(c + 1) * 4096],
               T["wq1T"][c * 128:(c + 1) * 128, :])
    tr.dma("sp", b1s[:], T["b1f"].rearrange("(a p) -> a p", p=128))
    tr.trans(pb[6][:, 0:32], b1s[:], C.ident_f[0:32, 0:32])
    tr.ee("dve", "tensor_copy", b1T[:], pb[6][:, 0:32])
    tr.dma("sp", b2r[:], T["b2f"][None, :])
    for eh in range(2):
        tr.mm(pb[7][:, 0:512], C.ones[0:1, :], b2r[0:1, eh * 512:(eh + 1) * 512])
        tr.ee("dve", "tensor_copy", b2bc[:, eh * 512:(eh + 1) * 512],
              pb[7][:, 0:512])
    for sg in range(S // 512):
        for fs in range(32):
            bank = pb[fs % 2]
            for ci in range(8):
                tr.mm(bank[:, 0:512],
                      w1T[:, ci * 4096 + fs * 128: ci * 4096 + fs * 128 + 128],
                      xT[:, ci * S + sg * 512: ci * S + sg * 512 + 512],
                      start=(ci == 0), stop=(ci == 7))
            tr.act(hsp[fs // 8][:, (fs % 8) * 512:(fs % 8 + 1) * 512],
                   bank[:, 0:512], AF.Relu, bias=b1T[:, fs:fs + 1])
        for pair in range(2):
            for cc in range(32):
                tr.dma("sp", w2b[cc % 2][:], T["wq2T"][cc * 128:(cc + 1) * 128, :])
                for sl in range(2):
                    stl = pair * 2 + sl
                    for eh in range(2):
                        tr.mm(pb[2 + sl * 2 + eh][:, 0:512],
                              hsp[cc // 8][:, (cc % 8) * 512 + stl * 128: (cc % 8) * 512 + stl * 128 + 128],
                              w2b[cc % 2][:, eh * 512:(eh + 1) * 512],
                              start=(cc == 0), stop=(cc == 31))
            for sl in range(2):
                stl = pair * 2 + sl
                t_glob = sg * 4 + stl
                b = stl % 2
                tr.dma("sp", xt[b][:], T["x"][t_glob * 128:(t_glob + 1) * 128, :])
                tr.dma("sp", zt[b][:], T["Z0"][t_glob * 128:(t_glob + 1) * 128, :])
                for eh in range(2):
                    o = jt[b][:, eh * 512:(eh + 1) * 512]
                    tr.ee("dve", "tensor_add", o, pb[2 + sl * 2 + eh][:, 0:512],
                          zt[b][:, eh * 512:(eh + 1) * 512])
                    tr.ee("dve", "tensor_add", o, o, xt[b][:, eh * 512:(eh + 1) * 512])
                    tr.ee("dve", "tensor_add", o, o, b2bc[:, eh * 512:(eh + 1) * 512])
                tr.dma("sp", T["x1"][t_glob * 128:(t_glob + 1) * 128, :], jt[b][:])


def ph_qkvr(tr, es, C, T):
    nc = tr.nc
    pb = C.banks(es, 8)
    pbb = [p.bitcast(BF16) for p in pb]
    nt = S // 128
    xT = _alloc(es, nc, "qxT", [128, 8 * S], BF16)
    Wsb = {w: _alloc(es, nc, f"qW{w}", [128, 8 * 1024], BF16) for w in ("WQ", "WK", "WV")}
    bq8 = _alloc(es, nc, "qbq8", [128, 8], F32)
    bqs = _alloc(es, nc, "qbqs", [8, 128], F32)
    ktile = [_alloc(es, nc, f"qkt{i}", [128, 512], BF16) for i in range(4)]
    sqt = [_alloc(es, nc, f"qsqt{i}", [128, 512], F32) for i in range(2)]
    sel = _alloc(es, nc, "qsel", [128, 2], F32)
    kn_g = _alloc(es, nc, "qkng", [1, 128], F32)
    qn_g = _alloc(es, nc, "qqng", [1, 128], F32)
    hrow = _alloc(es, nc, "qhrow", [1, 64], F32)
    vpv = _alloc(es, nc, "qvpv", [128, nt * 16 * 65], BF16)
    biasM = _alloc(es, nc, "qbM", [128, 16], F32)
    _load_xT(tr, xT, T["xn1n"], 0, S)
    for w in ("WQ", "WK", "WV"):
        tr.dma("pool", Wsb[w][:].rearrange("p (c e) -> p c e", c=8),
               T[w].rearrange("(c p) e -> p c e", p=128))
    tr.dma("sp", bqs[:], T["bQ"].rearrange("(a p) -> a p", p=128))
    tr.trans(pb[6][:, 0:8], bqs[:], C.ident_f[0:8, 0:8])
    tr.ee("dve", "tensor_scalar", bq8[:], pb[6][:, 0:8], 0.125, None, op0=ALU.mult)
    bkT = _alloc(es, nc, "qbkT", [128, 8], F32)
    tr.dma("sp", bqs[:], T["bK"].rearrange("(a p) -> a p", p=128))
    tr.trans(pb[6][:, 0:8], bqs[:], C.ident_f[0:8, 0:8])
    tr.ee("dve", "tensor_copy", bkT[:], pb[6][:, 0:8])
    _memset(tr, sel[:], 0.0, "dve")
    _memset(tr, sel[0:64, 0:1], 1.0, "dve")
    _memset(tr, sel[64:128, 1:2], 1.0, "dve")
    _memset(tr, vpv[:], 1.0, "pool")
    for (w, dstn, scale, do_bias, ng) in (("WK", "kT", 1.0, False, kn_g),
                                          ("WQ", "qT8", 0.125, True, qn_g)):
        for co in range(8):
            for sg in range(S // 512):
                bank = pb[(co + sg) % 2]
                for ci in range(8):
                    tr.mm(bank[:, 0:512],
                          Wsb[w][:, ci * 1024 + co * 128: ci * 1024 + co * 128 + 128],
                          xT[:, ci * S + sg * 512: ci * S + sg * 512 + 512],
                          start=(ci == 0), stop=(ci == 7))
                kb = ktile[(co * (S // 512) + sg) % 4]
                if do_bias:
                    tr.ee("dve", "tensor_scalar", kb[:], bank[:, 0:512], scale,
                          bq8[:, co:co + 1], op0=ALU.mult, op1=ALU.add)
                else:
                    tr.ee("dve", "tensor_scalar", kb[:], bank[:, 0:512],
                          bkT[:, co:co + 1], None, op0=ALU.add)
                tr.dma("sp", T[dstn][co * 128:(co + 1) * 128, sg * 512:(sg + 1) * 512],
                       kb[:])
                sq_ = sqt[(co * (S // 512) + sg) % 2]
                tr.act(sq_[:], kb[:], AF.Square)
                for hh in range(2):
                    tr.mm(pb[4 + hh][0:1, 0:512], sel[:, hh:hh + 1], sq_[:])
                    h = co * 2 + hh
                    nsl = S // 512
                    tr.ee("dve", "reduce_max", ng[0:1, h * nsl + sg: h * nsl + sg + 1],
                          pb[4 + hh][0:1, 0:512], axis=AX)
    qmax = _alloc(es, nc, "qqmax", [1, 16], F32)
    kmax = _alloc(es, nc, "qkmax", [1, 16], F32)
    prod = _alloc(es, nc, "qprod", [1, 16], F32)
    nsl = S // 512
    for h in range(16):
        tr.ee("dve", "reduce_max", qmax[0:1, h:h + 1],
              qn_g[0:1, h * nsl:(h + 1) * nsl], axis=AX)
        tr.ee("dve", "reduce_max", kmax[0:1, h:h + 1],
              kn_g[0:1, h * nsl:(h + 1) * nsl], axis=AX)
    tr.ee("dve", "tensor_mul", prod[:], qmax[:], kmax[:])
    tr.act(prod[:], prod[:], AF.Sqrt)
    tr.ee("dve", "tensor_scalar", prod[:], prod[:], -1.0, None, op0=ALU.mult)
    tr.mm(pb[6][:, 0:16], C.ones[0:1, :], prod[:])
    tr.ee("dve", "tensor_copy", biasM[:], pb[6][:, 0:16])
    tr.dma("sp", T["biasM"], biasM[:])
    for st in range(nt):
        for dh in range(2):
            bank = pb[(st + dh) % 2]
            for ci in range(8):
                tr.mm(bank[:, 0:512],
                      xT[:, ci * S + st * 128: ci * S + st * 128 + 128],
                      Wsb["WV"][:, ci * 1024 + dh * 512: ci * 1024 + dh * 512 + 512],
                      start=(ci == 0), stop=(ci == 7))
            for hh in range(8):
                h = dh * 8 + hh
                tr.ee("dve", "tensor_copy",
                      vpv[:, (st * 16 + h) * 65:(st * 16 + h) * 65 + 64],
                      bank[:, hh * 64:(hh + 1) * 64])
    for g in range(4):
        w = nt * 16 * 65 // 4
        tr.dma("sp", T["vpv"][:, g * w:(g + 1) * w], vpv[:, g * w:(g + 1) * w])
    wof = _alloc(es, nc, "qwof", [64, 1024], F32)
    bv16 = _alloc(es, nc, "qbv16", [16, 64], F32)
    bv64 = _alloc(es, nc, "qbv64", [64, 16], F32)
    bvs = _alloc(es, nc, "qbvs", [64, 1], F32)
    bvwo = _alloc(es, nc, "qbvwo", [1, 1024], F32)
    bor = _alloc(es, nc, "qbor", [1, 1024], F32)
    jbr = _alloc(es, nc, "qjbr", [1, 1024], F32)
    tr.dma("sp", wof[:], T["WO"])
    tr.dma("sp", bv16[:], T["bV"].rearrange("(h d) -> h d", d=64))
    tr.trans(pb[6][0:64, 0:16], bv16[:], C.ident_f[0:16, 0:16])
    tr.ee("dve", "tensor_copy", bv64[:], pb[6][0:64, 0:16])
    tr.ee("dve", "reduce_sum", bvs[:], bv64[:], axis=AX)
    for eh in range(2):
        tr.mm(pb[7][0:1, 0:512], bvs[:], wof[:, eh * 512:(eh + 1) * 512])
        tr.ee("dve", "tensor_copy", bvwo[0:1, eh * 512:(eh + 1) * 512],
              pb[7][0:1, 0:512])
    tr.dma("sp", bor[:], T["bO"][None, :])
    tr.ee("dve", "scalar_tensor_tensor", jbr[:], bor[:], 16.0, bvwo[:],
          op0=ALU.mult, op1=ALU.add)
    tr.dma("sp", T["jbrow"], jbr[:])


def ph_router(tr, es, C, T):
    nc = tr.nc
    pb = C.banks(es, 8)
    pbb = [p.bitcast(BF16) for p in pb]
    nt = S // 128
    xT = _alloc(es, nc, "rxT", [128, 8 * S], BF16)
    _load_xT(tr, xT, T["xn1n"], 0, S)
    rwsb = _alloc(es, nc, "qrw", [128, 8 * 16], BF16)
    rbsb = _alloc(es, nc, "qrb", [16, 1], F32)
    lgT = _alloc(es, nc, "qlgT", [16, S], F32)
    lnt = _alloc(es, nc, "qlnt", [128, 16], F32)
    lw = [_alloc(es, nc, f"qlw{i}", [128, 16], F32) for i in range(2)]
    eqb = _alloc(es, nc, "qeq", [128, 16], F32)
    dport = _alloc(es, nc, "qd", [128, 16], F32)
    indb = _alloc(es, nc, "qind", [128, 16], F32)
    expb = _alloc(es, nc, "qexp", [128, 16], F32)
    peb = _alloc(es, nc, "qpe", [128, 16], F32)
    cechunk = _alloc(es, nc, "qcec", [128, 16], F32)
    ce_sb = _alloc(es, nc, "qce", [128, nt * 16], F32)
    ceT_sb = _alloc(es, nc, "qceT", [16, S], F32)
    smr = _alloc(es, nc, "qsmr", [128, 8], F32)
    tr.dma("pool", rwsb[:].rearrange("p (c e) -> p c e", c=8),
           T["rw"].rearrange("(c p) e -> p c e", p=128))
    tr.dma("sp", rbsb[:], T["rb"][:, None])
    for sg in range(S // 512):
        bank = pb[sg % 2]
        for ci in range(8):
            tr.mm(bank[0:16, 0:512], rwsb[:, ci * 16:(ci + 1) * 16],
                  xT[:, ci * S + sg * 512: ci * S + sg * 512 + 512],
                  start=(ci == 0), stop=(ci == 7))
        tr.ee("dve", "tensor_scalar", lgT[0:16, sg * 512:(sg + 1) * 512],
              bank[0:16, 0:512], rbsb[:, 0:1], None, op0=ALU.add)
    for st in range(nt):
        tr.trans(pb[2][:, 0:16], lgT[0:16, st * 128:(st + 1) * 128],
                 C.ident_f[0:16, 0:16])
        tr.ee("dve", "tensor_copy", lnt[:], pb[2][:, 0:16])
        mx = smr[:, 0:1]
        m1n = smr[:, 1:2]
        cur = lnt
        for it in range(4):
            tr.ee("dve", "reduce_max", mx, cur[:], axis=AX)
            if it == 0:
                tr.ee("dve", "tensor_scalar", m1n, mx, -1.0, None, op0=ALU.mult)
            tr.ee("dve", "tensor_scalar", eqb[:], cur[:], mx, None, op0=ALU.is_equal)
            nxt = lw[it % 2]
            tr.ee("dve", "scalar_tensor_tensor", nxt[:], eqb[:], -1e30, cur[:],
                  op0=ALU.mult, op1=ALU.add)
            cur = nxt
        tr.ee("dve", "tensor_sub", dport[:], lnt[:], cur[:])
        tr.ee("dve", "tensor_scalar", indb[:], dport[:], 1e20, None, op0=ALU.is_gt)
        tr.act(expb[:], lnt[:], AF.Exp, bias=m1n)
        tr.ee("dve", "tensor_mul", peb[:], expb[:], indb[:])
        ssum = smr[:, 2:3]
        rr = smr[:, 3:4]
        tr.ee("dve", "reduce_sum", ssum, peb[:], axis=AX)
        tr.ee("dve", "reciprocal", rr, ssum)
        tr.ee("dve", "tensor_scalar", cechunk[:], peb[:], rr, None, op0=ALU.mult)
        tr.ee("dve", "tensor_copy", ce_sb[:, st * 16:(st + 1) * 16], cechunk[:])
        tr.trans(pb[2][0:16, 0:128], cechunk[:], C.ident_f[:])
        tr.ee("dve", "tensor_copy", ceT_sb[0:16, st * 128:(st + 1) * 128],
              pb[2][0:16, 0:128])
    tr.dma("sp", T["ce"], ce_sb[:])
    tr.dma("sp", T["ceT"], ceT_sb[:])




def ph_attn1(tr, es, C, T):
    nc = tr.nc
    sp2 = [es.enter_context(nc.psum_tensor(f"t1sp{i}", [128, 1024], F32)) for i in range(2)]
    pv = [es.enter_context(nc.psum_tensor(f"t1pv{i}", [128, 512], F32)) for i in range(4)]
    nt = S // 128
    kT = _alloc(es, nc, "t1kT", [128, 8 * S], BF16)
    vpv = _alloc(es, nc, "t1v", [128, nt * 16 * 65], BF16)
    biasM = _alloc(es, nc, "t1bM", [128, 16], F32)
    biasMg = _alloc(es, nc, "t1bMg", [128, 1], F32)
    wobf = _alloc(es, nc, "t1wo_w", [64, 1024], BF16)
    Zacc = _alloc(es, nc, "t1Z", [64, S], F32)
    Zbf = _alloc(es, nc, "t1Zb", [64, 512], BF16)
    qsl = [_alloc(es, nc, f"t1q{i}", [128, 8 * 512], BF16) for i in range(2)]
    pt = [_alloc(es, nc, f"t1p{i}", [128, 1024], BF16) for i in range(2)]
    rp = _alloc(es, nc, "t1rp", [1, 512], F32)
    rtmp = _alloc(es, nc, "t1rt", [1, 512], F32)
    bcb = [_alloc(es, nc, f"t1bcs{i}", [64, 512], F32) for i in range(2)]
    ztmp = _alloc(es, nc, "t1zt", [64, 512], F32)
    att = [_alloc(es, nc, f"t1a{i}", [128, 1024], F32) for i in range(2)]
    for c in range(8):
        tr.dma("sp", kT[:, c * S:(c + 1) * S], T["kT"][c * 128:(c + 1) * 128, :])
    for g in range(4):
        w = nt * 16 * 65 // 4
        tr.dma("sp", vpv[:, g * w:(g + 1) * w], T["vpv"][:, g * w:(g + 1) * w])
    tr.dma("sp", biasM[:], T["biasM"])
    tr.op("dve", lambda e, o=biasMg[:], i=biasM[:]: e.tensor_reduce(
        o, i, AX, ALU.min), reads=[biasM[:]], writes=[biasMg[:]])
    tr.dma("pool", wobf[:], T["WO"])

    def s_step(qg, pr, j):
        sp = sp2[j % 2]
        qlo = max(0, 128 * j - 512 * qg)
        qb = qsl[qg % 2]
        c = pr
        for hh in range(2):
            lo = hh * 64
            tr.mm(sp[:, hh * 512 + qlo: hh * 512 + 512],
                  kT[lo:lo + 64, c * S + j * 128: c * S + j * 128 + 128],
                  qb[lo:lo + 64, c * 512 + qlo: c * 512 + 512],
                  start=True, stop=True)

    def pv_step(qg, pr, j, nk):
        sp = sp2[j % 2]
        ptb = pt[j % 2]
        qlo = max(0, 128 * j - 512 * qg)
        wd = 512 - qlo
        src3 = sp[:].rearrange("p (h n) -> p h n", h=2)[:, :, qlo:512]
        dst3 = ptb[:].rearrange("p (h n) -> p h n", h=2)[:, :, qlo:512]
        tr.act(dst3, src3, AF.Exp, bias=biasMg[:, 0:1])
        if j >= 4 * qg:
            for hh in range(2):
                a = ptb[:, hh * 512 + qlo: hh * 512 + 512]
                tr.op("pool",
                      lambda e, a=a, wd=wd: e.affine_select(
                          out=a, in_=a, compare_op=ALU.is_ge, fill=0.0,
                          base=0, pattern=[[1, wd]], channel_multiplier=-1),
                      reads=[a], writes=[a])
        for hh in range(2):
            h = 2 * pr + hh
            tr.mm(pv[2 * (pr % 2) + hh][0:65, qlo:512],
                  vpv[:, (j * 16 + h) * 65:(j * 16 + h) * 65 + 65],
                  ptb[:, hh * 512 + qlo: hh * 512 + 512],
                  start=(j == 0), stop=(j == nk - 1))

    def tail(qg, pr):
        for hh in range(2):
            h = 2 * pr + hh
            pvb = pv[2 * (pr % 2) + hh]
            bco = sp2[hh]
            tr.act(rtmp[:], pvb[64:65, 0:512], AF.Ln)
            tr.act(rp[:], rtmp[:], AF.Exp, scale=-1.0)
            tr.mm(bco[0:64, 0:512], C.ones[0:1, 0:64], rp[:])
            tr.ee("dve", "tensor_copy", bcb[hh][:], bco[0:64, 0:512])
            zsl = Zacc[:, qg * 512:(qg + 1) * 512]
            if h == 0:
                tr.ee("dve", "tensor_mul", zsl, pvb[0:64, 0:512], bcb[hh][:])
            else:
                tr.ee("dve", "tensor_mul", ztmp[:], pvb[0:64, 0:512], bcb[hh][:])
                tr.ee("dve", "tensor_add", zsl, zsl, ztmp[:])

    for qg in range(S // 512):
        qb = qsl[qg % 2]
        tr.dma("sp", qb[:].rearrange("p (c s) -> p c s", c=8),
               T["qT8"].rearrange("(c p) s -> p c s", p=128)[:, :, qg * 512:(qg + 1) * 512])
        nk = 4 * (qg + 1)
        for pr in range(8):
            s_step(qg, pr, 0)
            for j in range(1, nk):
                s_step(qg, pr, j)
                pv_step(qg, pr, j - 1, nk)
            pv_step(qg, pr, nk - 1, nk)
            if pr > 0:
                tail(qg, pr - 1)
        tail(qg, 7)
        tr.ee("dve", "tensor_copy", Zbf[:], Zacc[:, qg * 512:(qg + 1) * 512])
        for qt in range(4):
            ab = att[qt % 2]
            for eh in range(2):
                pwo = pv[qt % 2]
                tr.mm(pwo[:, 0:512], Zbf[:, qt * 128:(qt + 1) * 128],
                      wobf[:, eh * 512:(eh + 1) * 512])
                tr.ee("dve", "tensor_copy", ab[:, eh * 512:(eh + 1) * 512],
                      pwo[:, 0:512])
            q = qg * 4 + qt
            tr.dma("sp", T["attn1"][q * 128:(q + 1) * 128, :], ab[:])


def ph_moe(tr, es, C, T):
    nc = tr.nc
    pb = C.banks(es, 8)
    pbb = [p.bitcast(BF16) for p in pb]
    nt = S // 128
    SGT = min(2048, S)
    nsg = S // SGT
    stg = SGT // 128
    ssb = SGT // 512
    xTs = _alloc(es, nc, "mxT", [128, 8 * SGT], BF16)
    w1e = _alloc(es, nc, "mw1", [128, 8 * 2048], BF16)
    w2e = _alloc(es, nc, "mw2", [128, 16 * 1024], BF16)
    hspm = [_alloc(es, nc, f"mh{i}", [128, 4 * 512], BF16) for i in range(4)]
    acc = _alloc(es, nc, "macc", [128, stg * 1024], F32)
    ce_sb = _alloc(es, nc, "mce", [128, stg * 16], F32)
    ceT_sb = _alloc(es, nc, "mceT", [16, SGT], BF16)
    b2sb = _alloc(es, nc, "mb2", [16, 1024], BF16)
    b1x = _alloc(es, nc, "mb1x", [16, 128], F32)
    b1T = _alloc(es, nc, "mb1T", [128, 16 * 16], F32)
    jbr = _alloc(es, nc, "mjbr", [1, 1024], F32)
    jbbc = _alloc(es, nc, "mjbbc", [128, 1024], F32)
    x1t = [_alloc(es, nc, "mx10", [128, 1024], F32)] * 2
    att = [_alloc(es, nc, "mat0", [128, 1024], F32)] * 2
    outt = [_alloc(es, nc, "mo0", [128, 1024], F32)] * 2
    tr.dma("pool", b2sb[:], T["b2e"])
    tr.dma("sp", jbr[:], T["jbrow"])
    for eh in range(2):
        tr.mm(pb[6][:, 0:512], C.ones[0:1, :], jbr[0:1, eh * 512:(eh + 1) * 512])
        tr.ee("dve", "tensor_copy", jbbc[:, eh * 512:(eh + 1) * 512],
              pb[6][:, 0:512])
    for e in range(16):
        tr.dma("sp", b1x[:], T["b1e"][e].rearrange("(a p) -> a p", p=128))
        tr.trans(pb[6][:, 0:16], b1x[:], C.ident_f[0:16, 0:16])
        tr.ee("dve", "tensor_copy", b1T[:, e * 16:(e + 1) * 16], pb[6][:, 0:16])
    for sg in range(nsg):
        _load_xT(tr, xTs, T["xn1n"], sg * SGT, SGT)
        tr.dma("sp", ce_sb[:], T["ce"][:, sg * stg * 16:(sg + 1) * stg * 16])
        tr.dma("pool", ceT_sb[:], T["ceT"][0:16, sg * SGT:(sg + 1) * SGT])
        for e in range(16):
            for half in range(2):
                tr.dma("pool",
                       w1e[:].rearrange("p (c i) -> p c i", c=8)[:, half * 4:(half + 1) * 4, :],
                       T["w1e"][e].rearrange("(c p) i -> p c i", p=128)[:, half * 4:(half + 1) * 4, :])
                tr.dma("pool",
                       w2e[:].rearrange("p (c i) -> p c i", c=16)[:, half * 8:(half + 1) * 8, :],
                       T["w2e"][e].rearrange("(c p) i -> p c i", p=128)[:, half * 8:(half + 1) * 8, :])
            for ssub in range(ssb):
                for isl in range(16):
                    bank = pb[isl % 2]
                    for ci in range(8):
                        tr.mm(bank[:, 0:512],
                              w1e[:, ci * 2048 + isl * 128: ci * 2048 + isl * 128 + 128],
                              xTs[:, ci * SGT + ssub * 512: ci * SGT + ssub * 512 + 512],
                              start=(ci == 0), stop=(ci == 7))
                    tr.act(hspm[isl // 4][:, (isl % 4) * 512:(isl % 4 + 1) * 512],
                           bank[:, 0:512],
                           AF.Gelu, bias=b1T[:, e * 16 + isl: e * 16 + isl + 1])
                for pairi in range(2):
                    for cc in range(16):
                        for sl in range(2):
                            stl = pairi * 2 + sl
                            for eh in range(2):
                                tr.mm(pb[2 + sl * 2 + eh][:, 0:512],
                                      hspm[cc // 4][:, (cc % 4) * 512 + stl * 128: (cc % 4) * 512 + stl * 128 + 128],
                                      w2e[:, cc * 1024 + eh * 512: cc * 1024 + eh * 512 + 512],
                                      start=(cc == 0), stop=(cc == 15))
                    for sl in range(2):
                        stl = pairi * 2 + sl
                        st_l = ssub * 4 + stl
                        cec = ce_sb[:, st_l * 16 + e: st_l * 16 + e + 1]
                        for eh in range(2):
                            a = acc[:, st_l * 1024 + eh * 512: st_l * 1024 + eh * 512 + 512]
                            bank = pb[2 + sl * 2 + eh]
                            if e == 0:
                                tr.ee("dve", "tensor_scalar", a, bank[:, 0:512],
                                      cec, None, op0=ALU.mult)
                            else:
                                tr.ee("dve", "scalar_tensor_tensor", a, bank[:, 0:512],
                                      cec, a, op0=ALU.mult, op1=ALU.add)
        for stl16 in range(stg):
            st_g = sg * stg + stl16
            b = stl16 % 2
            tr.dma("sp", x1t[b][:], T["x1"][st_g * 128:(st_g + 1) * 128, :])
            tr.dma("sp", att[b][:], T["attn1"][st_g * 128:(st_g + 1) * 128, :])
            for eh in range(2):
                bank = pb[6 + eh]
                tr.mm(bank[0:128, 0:512], ceT_sb[0:16, stl16 * 128:(stl16 + 1) * 128],
                      b2sb[0:16, eh * 512:(eh + 1) * 512])
                o = outt[b][:, eh * 512:(eh + 1) * 512]
                tr.ee("dve", "tensor_add", o, bank[:, 0:512],
                      acc[:, stl16 * 1024 + eh * 512: stl16 * 1024 + eh * 512 + 512])
                tr.ee("dve", "tensor_add", o, o, x1t[b][:, eh * 512:(eh + 1) * 512])
                tr.ee("dve", "tensor_add", o, o, att[b][:, eh * 512:(eh + 1) * 512])
                tr.ee("dve", "tensor_add", o, o, jbbc[:, eh * 512:(eh + 1) * 512])
            tr.dma("sp", T["out"][st_g * 128:(st_g + 1) * 128, :], outt[b][:])



INPUT_TENSORS = {
    "x": lambda: ([S, 1024], F32),
    "R": lambda: ([1024, 1024], F32),
    "W1": lambda: ([4096, 1024], F32),
    "b1f": lambda: ([4096], F32),
    "W2": lambda: ([1024, 4096], F32),
    "b2f": lambda: ([1024], F32),
    "WQ": lambda: ([1024, 1024], F32),
    "bQ": lambda: ([1024], F32),
    "bK": lambda: ([1024], F32),
    "WK": lambda: ([1024, 1024], F32),
    "WV": lambda: ([1024, 1024], F32),
    "WO": lambda: ([64, 1024], F32),
    "bO": lambda: ([1024], F32),
    "bV": lambda: ([1024], F32),
    "rw": lambda: ([1024, 16], F32),
    "rb": lambda: ([16], F32),
    "w1e": lambda: ([16, 1024, 2048], F32),
    "b1e": lambda: ([16, 2048], F32),
    "w2e": lambda: ([16, 2048, 1024], F32),
    "b2e": lambda: ([16, 1024], F32),
}
SCRATCH_TENSORS = {
    "xn0n": lambda: ([S, 1024], BF16),
    "Z0": lambda: ([S, 1024], F32),
    "wq1T": lambda: ([1024, 4096], BF16),
    "wq2T": lambda: ([4096, 1024], BF16),
    "x1": lambda: ([S, 1024], F32),
    "xn1n": lambda: ([S, 1024], BF16),
    "qT8": lambda: ([1024, S], BF16),
    "kT": lambda: ([1024, S], BF16),
    "vpv": lambda: ([128, (S // 128) * 16 * 65], BF16),
    "biasM": lambda: ([128, 16], F32),
    "jbrow": lambda: ([1, 1024], F32),
    "attn1": lambda: ([S, 1024], F32),
    "ce": lambda: ([128, (S // 128) * 16], F32),
    "ceT": lambda: ([16, S], F32),
    "out": lambda: ([S, 1024], F32),
}

PHASES = {
    "ln0": (lambda tr, es, C, T: ph_ln(tr, es, C, T, "x", "xn0n", "l0"),
            ["x"], ["xn0n"]),
    "quant": (ph_quant, ["W1", "W2"], ["wq1T", "wq2T"]),
    "attn0": (ph_attn0, ["xn0n", "R"], ["Z0"]),
    "ffn0": (ph_ffn0, ["xn0n", "wq1T", "wq2T", "b1f", "b2f", "x", "Z0"], ["x1"]),
    "ln1": (lambda tr, es, C, T: ph_ln(tr, es, C, T, "x1", "xn1n", "l1"),
            ["x1"], ["xn1n"]),
    "qkvr": (ph_qkvr, ["xn1n", "WQ", "bQ", "WK", "bK", "WV", "WO", "bO", "bV"],
             ["qT8", "kT", "vpv", "biasM", "jbrow"]),
    "router": (ph_router, ["xn1n", "rw", "rb"], ["ce", "ceT"]),
    "attn1": (ph_attn1, ["qT8", "kT", "vpv", "biasM", "WO"], ["attn1"]),
    "moe": (ph_moe, ["xn1n", "w1e", "b1e", "w2e", "b2e", "ce", "ceT", "x1",
                     "attn1", "jbrow"], ["out"]),
}

GROUPS = [["ln0"], ["quant"], ["attn0"], ["ffn0"], ["ln1"], ["qkvr"],
          ["router"], ["attn1"], ["moe"]]


def build_program(phase_names, need_later):
    reads, writes = set(), set()
    for p in phase_names:
        _, r, w = PHASES[p]
        reads.update(r)
        writes.update(w)
    nc = bass.Bass("TRN2", target_bir_lowering=False)
    T = {}
    in_names, out_names = [], []
    for name in sorted(reads | writes):
        if name in INPUT_TENSORS:
            shape, dt = INPUT_TENSORS[name]()
            kind = "ExternalInput"
        else:
            shape, dt = SCRATCH_TENSORS[name]()
            if name in writes:
                kind = "ExternalOutput" if (name in need_later or name == "out") else "Internal"
            else:
                kind = "ExternalInput"
        if kind == "ExternalInput":
            in_names.append(name)
        elif kind == "ExternalOutput":
            out_names.append(name)
        T[name] = nc.dram_tensor(name, shape, dt, kind=kind).ap()
    tr = Tr(nc)
    with contextlib.ExitStack() as es_ctx:
        C = Ctx(tr, es_ctx)
        for p in phase_names:
            fn, _, _ = PHASES[p]
            with contextlib.ExitStack() as es:
                fn(tr, es, C, T)
            tr.barrier()
        tr.finalize()
        tr.emit()
    return nc, in_names, out_names


def _values_from_inputs(inputs):
    c = np.ascontiguousarray
    xs = [c(inputs["L0_attn_encoding"][i]) for i in range(8)]
    shared = {
        "R": inputs["L0_attn_R"], "W1": inputs["L0_ffn_W1"], "b1f": inputs["L0_ffn_b1"],
        "W2": inputs["L0_ffn_W2"], "b2f": inputs["L0_ffn_b2"],
        "WQ": inputs["L1_attn_WQ"], "bQ": inputs["L1_attn_bQ"], "bK": inputs["L1_attn_bK"],
        "WK": inputs["L1_attn_WK"], "WV": inputs["L1_attn_WV"],
        "WO": inputs["L1_attn_WO"], "bO": inputs["L1_attn_bO"],
        "bV": inputs["L1_attn_bV"],
        "rw": inputs["L1_ffn_router_w"], "rb": inputs["L1_ffn_router_b"],
        "w1e": inputs["L1_ffn_w1"], "b1e": inputs["L1_ffn_b1"],
        "w2e": inputs["L1_ffn_w2"], "b2e": inputs["L1_ffn_b2"],
    }
    vals = {"x": xs}
    for k, v in shared.items():
        a = c(np.asarray(v))
        vals[k] = [a] * 8
    return vals


FUSED = True


def kernel(**inputs):
    vals = _values_from_inputs(inputs)
    groups = [[p for g in GROUPS for p in g]] if FUSED else GROUPS
    for gi, g in enumerate(groups):
        later = set()
        for g2 in groups[gi + 1:]:
            for p in g2:
                later.update(PHASES[p][1])
        nc, in_names, out_names = build_program(g, later)
        in_maps = [{n: vals[n][i] for n in in_names} for i in range(8)]
        res = run_bass_kernel_spmd(nc, in_maps, core_ids=list(range(8)))
        for n in out_names:
            vals[n] = [np.asarray(res.results[i][n]) for i in range(8)]
    return np.stack([vals["out"][i] for i in range(8)], axis=0)


# version 10
# speedup vs baseline: 1.5934x; 1.2908x over previous
import math
import contextlib
import numpy as np
import ml_dtypes
import concourse.bass as bass
import concourse.mybir as mybir
from concourse.bass_core_rust import IndirectOffsetOnAxis
from concourse.bass_utils import run_bass_kernel_spmd

F32 = mybir.dt.float32
BF16 = mybir.dt.bfloat16
I32 = mybir.dt.int32
AF = mybir.ActivationFunctionType
ALU = mybir.AluOpType
AX = mybir.AxisListType.X

S = 4096
D = 1024
NT = S // 128
NC = D // 128
NEG = -1e9

ENGINES = ("pe", "dve", "act", "pool", "sp")
_ENG_ATTR = {"pe": "tensor", "dve": "vector", "act": "scalar", "pool": "gpsimd", "sp": "sync"}
_QUEUES = ("sp", "act", "pool")
_NROT = 4


class Op:
    __slots__ = ("build", "waits", "incs")

    def __init__(self, build):
        self.build = build
        self.waits = []
        self.incs = []


def _key(ap):
    t = getattr(ap, "tensor", ap)
    return t.name


class Tr:

    def __init__(self, nc):
        self.nc = nc
        self.ops = {e: [] for e in ENGINES}
        self.counts = {}
        self.mem = {}
        self.marks = {e: {} for e in ENGINES}
        self.sems = set()
        for e in ("pe", "dve", "act", "pool"):
            self._sem("S_" + e)
        self.qrot = {}
        for q in _QUEUES:
            for i in range(_NROT):
                self._sem(f"D_{q}{i}")
            self.qrot[q] = {"i": 0, "cum": [0] * _NROT}
        self.pe_pending_reads = []
        self.pe_open = set()
        self._last_op = {e: None for e in ENGINES}

    def _sem(self, name):
        self.sems.add(name)
        self.counts.setdefault(name, 0)

    def _rec(self, key):
        return self.mem.setdefault(key, {"w": None, "r": []})

    def _wait(self, op, eng, sem, val):
        if val <= 0:
            return
        wm = self.marks[eng]
        if wm.get(sem, 0) >= val:
            return
        op.waits.append((sem, val))
        wm[sem] = val

    def _engine_done_token(self, eng):
        return ("S_" + eng, self.counts["S_" + eng])

    def _flush_pe_reads(self, val):
        for key in self.pe_pending_reads:
            self._rec(key)["r"].append(("S_pe", val))
        self.pe_pending_reads = []

    def _force_pe_inc(self):
        if not self.pe_pending_reads:
            return
        last = self._last_op["pe"]
        assert last is not None
        self.counts["S_pe"] += 1
        v = self.counts["S_pe"]
        last.incs.append(("S_pe", 1))
        self._flush_pe_reads(v)

    def op(self, eng, build, reads=(), writes=(), inc=True,
           pe_start=True, pe_stop=True, is_mm=False):
        o = Op(build)
        for ap in reads:
            assert _key(ap) not in self.pe_open or eng == "pe", _key(ap)
        for ap in reads:
            k = _key(ap)
            r = self._rec(k)
            if r["w"] is not None:
                sem, val = r["w"]
                if not (eng == "pe" and sem == "S_pe"):
                    self._wait(o, eng, sem, val)
        for ap in writes:
            k = _key(ap)
            if k in self.pe_pending_reads and eng != "pe":
                self._force_pe_inc()
            r = self._rec(k)
            if r["w"] is not None:
                sem, val = r["w"]
                if not (eng == "pe" and sem == "S_pe"):
                    self._wait(o, eng, sem, val)
            for sem, val in r["r"]:
                if not (eng == "pe" and sem == "S_pe"):
                    self._wait(o, eng, sem, val)
        if eng == "pe":
            for ap in reads:
                self.pe_pending_reads.append(_key(ap))
            do_inc = inc and pe_stop
            if do_inc:
                self.counts["S_pe"] += 1
                v = self.counts["S_pe"]
                o.incs.append(("S_pe", 1))
                self._flush_pe_reads(v)
                for ap in writes:
                    rr = self._rec(_key(ap))
                    rr["w"] = ("S_pe", v)
                    rr["r"] = []
            else:
                pass
        else:
            sem = "S_" + eng
            self.counts[sem] += 1
            v = self.counts[sem]
            o.incs.append((sem, 1))
            for ap in writes:
                rr = self._rec(_key(ap))
                rr["w"] = (sem, v)
                rr["r"] = []
            for ap in reads:
                self._rec(_key(ap))["r"].append((sem, v))
        self.ops[eng].append(o)
        self._last_op[eng] = o
        return o

    def dma(self, q, dst, src, waw_group=None, **kw):
        rot = self.qrot[q]
        slot = rot["i"] % _NROT
        rot["i"] += 1
        sem = f"D_{q}{slot}"
        o = Op(lambda e, dst=dst, src=src, kw=kw: e.dma_start(dst, src, **kw))
        if rot["cum"][slot] > 0:
            self._wait(o, q, sem, rot["cum"][slot])
        for ap in (src,):
            k = _key(ap)
            r = self._rec(k)
            if r["w"] is not None:
                self._wait(o, q, *r["w"])
        kd = _key(dst)
        if kd in self.pe_pending_reads:
            self._force_pe_inc()
        r = self._rec(kd)
        skip_waw = waw_group is not None and r.get("wg") == waw_group
        if not skip_waw:
            if r["w"] is not None:
                self._wait(o, q, *r["w"])
            for s_, v_ in r["r"]:
                self._wait(o, q, s_, v_)
        rot["cum"][slot] += 16
        self.counts[sem] = rot["cum"][slot]
        v = rot["cum"][slot]
        o.incs.append((sem, 16))
        rr = self._rec(kd)
        rr["w"] = (sem, v)
        rr["r"] = []
        rr["wg"] = waw_group
        self._rec(_key(src))["r"].append((sem, v))
        self.ops[q].append(o)
        self._last_op[q] = o
        return o

    def idma(self, q, dst, src, off, gather, waw_group=None):
        rot = self.qrot[q]
        slot = rot["i"] % _NROT
        rot["i"] += 1
        sem = f"D_{q}{slot}"

        def build(e, dst=dst, src=src, off=off, g=gather):
            ooa = IndirectOffsetOnAxis(off, 0)
            return e.indirect_dma_start(
                dst, None if g else ooa, src, ooa if g else None,
                oob_is_err=False)

        o = Op(build)
        if rot["cum"][slot] > 0:
            self._wait(o, q, sem, rot["cum"][slot])
        for ap in (src, off):
            r = self._rec(_key(ap))
            if r["w"] is not None:
                self._wait(o, q, *r["w"])
        kd = _key(dst)
        if kd in self.pe_pending_reads:
            self._force_pe_inc()
        r = self._rec(kd)
        skip_waw = waw_group is not None and r.get("wg") == waw_group
        if not skip_waw:
            if r["w"] is not None:
                self._wait(o, q, *r["w"])
            for s_, v_ in r["r"]:
                self._wait(o, q, s_, v_)
        rot["cum"][slot] += 16
        self.counts[sem] = rot["cum"][slot]
        v = rot["cum"][slot]
        o.incs.append((sem, 16))
        rr = self._rec(kd)
        rr["w"] = (sem, v)
        rr["r"] = []
        rr["wg"] = waw_group
        self._rec(_key(src))["r"].append((sem, v))
        self._rec(_key(off))["r"].append((sem, v))
        self.ops[q].append(o)
        self._last_op[q] = o
        return o

    def queue_fence(self, q, keys):
        o = Op(lambda e: e.nop())
        rot = self.qrot[q]
        wm = self.marks[q]
        for i in range(_NROT):
            sem = f"D_{q}{i}"
            if rot["cum"][i] > 0 and wm.get(sem, 0) < rot["cum"][i]:
                o.waits.append((sem, rot["cum"][i]))
                wm[sem] = rot["cum"][i]
        self._sem("F_" + q)
        self.counts["F_" + q] += 1
        v = self.counts["F_" + q]
        o.incs.append(("F_" + q, 1))
        for k in keys:
            rr = self._rec(k)
            rr["w"] = ("F_" + q, v)
            rr["r"] = []
            rr["wg"] = None
        self.ops[q].append(o)
        self._last_op[q] = o
        return o

    def mm(self, out, lhsT, rhs, start=True, stop=True, **kw):
        k = _key(out)
        if start:
            assert k not in self.pe_open, k
        else:
            assert k in self.pe_open, k
        if stop:
            self.pe_open.discard(k)
        else:
            self.pe_open.add(k)
        return self.op(
            "pe",
            lambda e, o=out, l=lhsT, r=rhs, s=start, t=stop, kw=kw:
                e.matmul(o, l, r, start=s, stop=t, **kw),
            reads=[lhsT, rhs], writes=[out], pe_stop=stop)

    def trans(self, out, in_, ident, **kw):
        assert _key(out) not in self.pe_open
        return self.op(
            "pe",
            lambda e, o=out, i=in_, d=ident, kw=kw: e.transpose(o, i, d, **kw),
            reads=[in_, ident], writes=[out])

    def act(self, out, in_, func, bias=0.0, scale=1.0, accum_out=None):
        reads = [in_]
        writes = [out]
        if isinstance(bias, bass.AP):
            reads.append(bias)
        if isinstance(scale, bass.AP):
            reads.append(scale)
        if accum_out is not None:
            writes.append(accum_out)
        return self.op(
            "act",
            lambda e, o=out, i=in_, f=func, b=bias, s=scale, a=accum_out:
                e.activation(o, i, f, bias=b, scale=s, accum_out=a),
            reads=reads, writes=writes)

    def ee(self, eng, meth, out, *ins, extra_reads=(), **kw):
        reads = [a for a in ins if isinstance(a, bass.AP)] + list(extra_reads)
        for v in kw.values():
            if isinstance(v, bass.AP):
                reads.append(v)
        return self.op(
            eng,
            lambda e, m=meth, o=out, i=ins, kw=kw: getattr(e, m)(o, *i, **kw),
            reads=reads, writes=[out])

    def barrier(self):
        self._force_pe_inc()
        targets = [(s, v) for s, v in sorted(self.counts.items()) if v > 0]
        for e in ENGINES:
            o = Op(lambda en: en.nop())
            wm = self.marks[e]
            for s_, v_ in targets:
                if wm.get(s_, 0) < v_:
                    o.waits.append((s_, v_))
                    wm[s_] = v_
            self.ops[e].append(o)
            self._last_op[e] = o
        for rec in self.mem.values():
            rec["r"] = []
            rec["wg"] = None

    def finalize(self):
        o = Op(lambda e: e.nop())
        for q in _QUEUES:
            rot = self.qrot[q]
            for i in range(_NROT):
                sem = f"D_{q}{i}"
                if rot["cum"][i] > 0:
                    o.waits.append((sem, rot["cum"][i]))
        for e in ("pe", "dve", "act", "pool"):
            sem = "S_" + e
            if self.counts[sem] > 0:
                o.waits.append((sem, self.counts[sem]))
        self.ops["sp"].append(o)

    def emit(self):
        nc = self.nc
        with contextlib.ExitStack() as es:
            handles = {}
            for name in sorted(self.sems):
                handles[name] = es.enter_context(nc.semaphore(name))
            blk = es.enter_context(nc.Block())

            def emit_engine(engobj, eng_name):
                for o in self.ops[eng_name]:
                    for sem, val in o.waits:
                        engobj.wait_ge(handles[sem], val)
                    inst = o.build(engobj)
                    for sem, n in o.incs:
                        inst.then_inc(handles[sem], n)

            blk.tensor(lambda e: emit_engine(e, "pe"))
            blk.vector(lambda e: emit_engine(e, "dve"))
            blk.scalar(lambda e: emit_engine(e, "act"))
            blk.gpsimd(lambda e: emit_engine(e, "pool"))
            blk.sync(lambda e: emit_engine(e, "sp"))



class Ctx:

    def __init__(self, tr, es):
        nc = tr.nc
        self.nc = nc
        self._bn = 0
        self.ident_bf = es.enter_context(nc.sbuf_tensor("identbf", [128, 128], BF16))
        self.ident_f = es.enter_context(nc.sbuf_tensor("identf", [128, 128], F32))
        self.mask = es.enter_context(nc.sbuf_tensor("maskdg", [128, 128], F32))
        self.ones = es.enter_context(nc.sbuf_tensor("onesr", [1, 128], F32))
        self.eps = es.enter_context(nc.sbuf_tensor("epsc", [128, 1], F32))
        t = tr
        t.ee("pool", "memset", self.eps[:], 1e-5)
        t.ee("pool", "memset", self.ones[:], 1.0)
        t.ee("pool", "memset", self.ident_bf[:], 0.0)
        t.op("pool", lambda e, a=self.ident_bf: e.affine_select(
            out=a[:], in_=a[:], compare_op=ALU.not_equal, fill=1.0, base=0,
            pattern=[[-1, 128]], channel_multiplier=1),
            reads=[self.ident_bf[:]], writes=[self.ident_bf[:]])
        t.ee("pool", "memset", self.ident_f[:], 0.0)
        t.op("pool", lambda e, a=self.ident_f: e.affine_select(
            out=a[:], in_=a[:], compare_op=ALU.not_equal, fill=1.0, base=0,
            pattern=[[-1, 128]], channel_multiplier=1),
            reads=[self.ident_f[:]], writes=[self.ident_f[:]])
        t.ee("pool", "memset", self.mask[:], 0.0)
        t.op("pool", lambda e, a=self.mask: e.affine_select(
            out=a[:], in_=a[:], compare_op=ALU.is_ge, fill=NEG, base=0,
            pattern=[[-1, 128]], channel_multiplier=1),
            reads=[self.mask[:]], writes=[self.mask[:]])
        self.lstrict = es.enter_context(nc.sbuf_tensor("lstrict", [128, 128], BF16))
        t.ee("pool", "memset", self.lstrict[:], 0.0)
        t.op("pool", lambda e, a=self.lstrict: e.affine_select(
            out=a[:], in_=a[:], compare_op=ALU.is_ge, fill=1.0, base=0,
            pattern=[[-1, 128]], channel_multiplier=1),
            reads=[self.lstrict[:]], writes=[self.lstrict[:]])
        self.onescol_bf = es.enter_context(nc.sbuf_tensor("onescbf", [128, 1], BF16))
        t.ee("pool", "memset", self.onescol_bf[:], 1.0)

    def banks(self, es, n=8, width=512):
        self._bn += 1
        return [es.enter_context(self.nc.psum_tensor(f"pb{self._bn}_{i}", [128, width], F32))
                for i in range(n)]


def _iota(tr, ap, base, chmul, pat):
    tr.op("pool", lambda e, a=ap, b=int(base), c=chmul, p=pat: e.iota(
        a, pattern=p, base=b, channel_multiplier=c,
        allow_small_or_imprecise_dtypes=True),
        reads=[], writes=[ap])


def _alloc(es, nc, name, shape, dtype):
    return es.enter_context(nc.sbuf_tensor(name, shape, dtype))


def _memset(tr, ap, val, eng="pool"):
    tr.op(eng, lambda e, a=ap, v=val: e.memset(a, v), reads=[], writes=[ap])



def ph_ln(tr, es, C, T, src, dst, pref):
    nc = tr.nc
    nt = S // 128
    xt = [_alloc(es, nc, f"{pref}xt{i}", [128, 1024], F32) for i in range(2)]
    xnb = [_alloc(es, nc, f"{pref}xnb{i}", [128, 1024], BF16) for i in range(2)]
    sq = [_alloc(es, nc, f"{pref}sq{i}", [128, 1024], F32) for i in range(2)]
    sm = [_alloc(es, nc, f"{pref}sm{i}", [128, 8], F32) for i in range(2)]
    for t in range(nt):
        b = t % 2
        negm = sm[b][:, 0:1]
        ssq = sm[b][:, 1:2]
        var = sm[b][:, 2:3]
        std = sm[b][:, 3:4]
        rstd = sm[b][:, 4:5]
        nb = sm[b][:, 5:6]
        tr.dma("sp", xt[b][:], T[src][t * 128:(t + 1) * 128, :])
        tr.ee("dve", "reduce_sum", negm, xt[b][:], axis=AX, negate=True)
        tr.ee("dve", "tensor_scalar", negm, negm, 1.0 / 1024, None, op0=ALU.mult)
        tr.act(sq[b][:], xt[b][:], AF.Square, bias=negm, accum_out=ssq)
        tr.act(std, ssq, AF.Sqrt, bias=C.eps[:, 0:1], scale=1.0 / 1024)
        tr.ee("dve", "reciprocal", rstd, std)
        tr.ee("dve", "tensor_mul", nb, negm, rstd)
        tr.ee("dve", "tensor_scalar", xnb[b][:], xt[b][:], rstd, nb,
              op0=ALU.mult, op1=ALU.add)
        tr.dma("sp", T[dst][t * 128:(t + 1) * 128, :], xnb[b][:])


def ph_quant(tr, es, C, T):
    nc = tr.nc
    pb = C.banks(es, 8)
    pbb = [p.bitcast(BF16) for p in pb]
    wt = [_alloc(es, nc, f"qwt{i}", [128, 4096], F32) for i in range(2)]
    sgn = _alloc(es, nc, "qsgn", [128, 4096], BF16)
    absw = _alloc(es, nc, "qabs", [128, 4096], F32)
    mbit = _alloc(es, nc, "qmb", [128, 4096], BF16)
    wq = _alloc(es, nc, "qwq", [128, 4096], BF16)
    colacc = _alloc(es, nc, "qcol", [128, 1], F32)
    tmp1 = _alloc(es, nc, "qtmp", [128, 1], F32)
    oncol = _alloc(es, nc, "qonc", [128, 1], F32)
    thr = _alloc(es, nc, "qthr", [1, 1], F32)
    cbc = _alloc(es, nc, "qcbc", [128, 1], F32)
    wqT = _alloc(es, nc, "qwqT", [128, 8 * 4096], BF16)

    for (wname, oname, ntile, width) in (("W1", "wq1T", 32, 1024),
                                         ("W2", "wq2T", 8, 4096)):
        numel = ntile * 128 * width
        _memset(tr, colacc[:], 0.0, "dve")
        _memset(tr, oncol[:], 1.0, "dve")
        for t in range(ntile):
            b = t % 2
            tr.dma("sp", wt[b][:, 0:width], T[wname][t * 128:(t + 1) * 128, :])
            tr.ee("dve", "reduce_sum", tmp1[:], wt[b][:, 0:width], axis=AX,
                  apply_absolute_value=True)
            tr.ee("dve", "tensor_add", colacc[:], colacc[:], tmp1[:])
        tr.mm(pb[0][0:1, 0:1], colacc[:], oncol[:])
        tr.ee("dve", "tensor_scalar", thr[:], pb[0][0:1, 0:1], 0.5 / numel,
              0.5e-5, op0=ALU.mult, op1=ALU.add)
        tr.mm(pb[1][:, 0:1], C.ones[0:1, :], thr[:])
        tr.ee("dve", "tensor_copy", cbc[:], pb[1][:, 0:1])
        nchunk = width // 128
        for t in range(ntile):
            b = t % 2
            tr.dma("sp", wt[b][:, 0:width], T[wname][t * 128:(t + 1) * 128, :])
            tr.op("act", lambda e, o=sgn[:, 0:width], i=wt[b][:, 0:width]:
                  e.sign(o, i), reads=[wt[b][:, 0:width]], writes=[sgn[:, 0:width]])
            tr.act(absw[:, 0:width], wt[b][:, 0:width], AF.Abs)
            tr.ee("dve", "tensor_scalar", mbit[:, 0:width], absw[:, 0:width],
                  cbc[:, 0:1], None, op0=ALU.is_gt)
            tr.ee("dve", "tensor_mul", wq[:, 0:width], mbit[:, 0:width],
                  sgn[:, 0:width])
            for c0 in range(0, nchunk, 4):
                cg = min(4, nchunk - c0)
                for q in range(cg):
                    tr.trans(pbb[2][:, q * 128:(q + 1) * 128],
                             wq[:, (c0 + q) * 128:(c0 + q + 1) * 128],
                             C.ident_bf[:])
                dst = wqT[:].rearrange("p (c n) -> p c n", c=8 if wname == "W1" else 32)[
                    :, c0:c0 + cg, t * 128:t * 128 + 128]
                src = pbb[2][:].rearrange("p (q n) -> p q n", q=8)[:, 0:cg, 0:128]
                tr.ee("dve", "tensor_copy", dst, src)
        for c in range(nchunk):
            tr.dma("sp", T[oname][c * 128:(c + 1) * 128, :],
                   wqT[:, c * (32768 // nchunk):(c * (32768 // nchunk)) + (32768 // nchunk)])


def _load_xT(tr, dst_sb, src_dram, scol0, swidth):
    for c in range(8):
        tr.dma("sp", dst_sb[:, c * swidth:(c + 1) * swidth],
               src_dram[scol0:scol0 + swidth, c * 128:(c + 1) * 128],
               transpose=True)


def _load_nat(tr, dst_sb, src_dram, width=1024):
    nt = S // 128
    for g in range(4):
        t0, t1 = g * nt // 4, (g + 1) * nt // 4
        dst = dst_sb[:].rearrange("p (t d) -> p t d", d=width)[:, t0:t1, :]
        src = src_dram.rearrange("(t p) d -> p t d", p=128)[:, t0:t1, :]
        tr.dma("sp", dst, src)


def ph_attn0(tr, es, C, T):
    nc = tr.nc
    pb = C.banks(es, 8)
    pbb = [p.bitcast(BF16) for p in pb]
    nt = S // 128
    xnat = _alloc(es, nc, "a0nat", [128, nt * 1024], BF16)
    xT = _alloc(es, nc, "a0xT", [128, 8 * S], BF16)
    Rsb = _alloc(es, nc, "a0R", [128, 8 * 1024], BF16)
    encR = _alloc(es, nc, "a0enc", [128, 8 * 512], BF16)
    Sr0 = _alloc(es, nc, "a0S", [128, S], F32)
    Srow = [Sr0, Sr0]
    Prow = [_alloc(es, nc, f"a0P{i}", [128, S], BF16) for i in range(2)]
    PT0 = _alloc(es, nc, "a0PT", [128, S], BF16)
    PT = [PT0, PT0]
    Zs0 = _alloc(es, nc, "a0Z", [128, 1024], F32)
    Zsb = [Zs0, Zs0]
    sm = [_alloc(es, nc, f"a0sm{i}", [128, 8], F32) for i in range(2)]
    _load_nat(tr, xnat, T["xn0n"])
    _load_xT(tr, xT, T["xn0n"], 0, S)
    tr.dma("pool", Rsb[:].rearrange("p (c e) -> p c e", c=8),
           T["R"].rearrange("(c p) e -> p c e", p=128))

    def enc_part(qg):
        for co in range(8):
            for ci in range(8):
                tr.mm(pb[6][:, 0:512],
                      Rsb[:, ci * 1024 + co * 128: ci * 1024 + co * 128 + 128],
                      xT[:, ci * S + qg * 512: ci * S + qg * 512 + 512],
                      start=(ci == 0), stop=(ci == 7))
            tr.ee("dve", "tensor_copy", encR[:, co * 512:(co + 1) * 512],
                  pb[6][:, 0:512])

    def s_part(q):
        b = q % 2
        qt = q % 4
        klen = 128 * (q + 1)
        ng = (klen + 511) // 512
        for kg in range(ng):
            w = min(512, klen - kg * 512)
            bank = pb[kg % 3]
            for ci in range(8):
                tr.mm(bank[:, 0:w],
                      encR[:, ci * 512 + qt * 128: ci * 512 + qt * 128 + 128],
                      xT[:, ci * S + kg * 512: ci * S + kg * 512 + w],
                      start=(ci == 0), stop=(ci == 7))
            if kg == ng - 1:
                if w > 128:
                    tr.ee("dve", "tensor_copy",
                          Srow[b][:, kg * 512: kg * 512 + w - 128],
                          bank[:, 0:w - 128])
                tr.ee("dve", "tensor_add",
                      Srow[b][:, kg * 512 + w - 128: kg * 512 + w],
                      bank[:, w - 128:w], C.mask[:])
            else:
                tr.ee("dve", "tensor_copy", Srow[b][:, kg * 512: kg * 512 + w],
                      bank[:, 0:w])
        negmax = sm[b][:, 0:1]
        sumexp = sm[b][:, 1:2]
        tr.ee("dve", "reduce_max", negmax, Srow[b][:, 0:klen], axis=AX, negate=True)
        tr.act(Prow[b][:, 0:klen], Srow[b][:, 0:klen], AF.Exp, bias=negmax,
               accum_out=sumexp)
        tr.ee("dve", "reciprocal", sm[b][:, 2:3], sumexp)

    def pv_part(q):
        b = q % 2
        klen = 128 * (q + 1)
        nk = klen // 128
        for j0 in range(0, nk, 4):
            jg = min(4, nk - j0)
            for q_ in range(jg):
                tr.trans(pbb[7][:, q_ * 128:(q_ + 1) * 128],
                         Prow[b][:, (j0 + q_) * 128:(j0 + q_ + 1) * 128],
                         C.ident_bf[:])
            tr.ee("dve", "tensor_copy", PT[b][:, j0 * 128: j0 * 128 + jg * 128],
                  pbb[7][:, 0:jg * 128])
        for j in range(nk):
            for h in range(2):
                tr.mm(pb[4 + h][:, 0:512],
                      PT[b][:, j * 128:(j + 1) * 128],
                      xnat[:, j * 1024 + h * 512: j * 1024 + h * 512 + 512],
                      start=(j == 0), stop=(j == nk - 1))
        zb = Zsb[b]
        for h in range(2):
            tr.ee("dve", "tensor_scalar", zb[:, h * 512:(h + 1) * 512],
                  pb[4 + h][:, 0:512], sm[b][:, 2:3], None, op0=ALU.mult)
        tr.dma("sp", T["Z0"][q * 128:(q + 1) * 128, :], zb[:])

    enc_part(0)
    s_part(0)
    for q in range(nt):
        if q + 1 < nt:
            if (q + 1) % 4 == 0:
                enc_part((q + 1) // 4)
            s_part(q + 1)
        pv_part(q)


def ph_ffn0(tr, es, C, T):
    nc = tr.nc
    pb = C.banks(es, 8)
    pbb = [p.bitcast(BF16) for p in pb]
    xT = _alloc(es, nc, "f0xT", [128, 8 * S], BF16)
    w1T = _alloc(es, nc, "f0w1", [128, 8 * 4096], BF16)
    w2b = [_alloc(es, nc, f"f0w2{i}", [128, 1024], BF16) for i in range(4)]
    hsp = [_alloc(es, nc, f"f0h{i}", [128, 8 * 512], BF16) for i in range(4)]
    b1s = _alloc(es, nc, "f0b1s", [32, 128], F32)
    b1T = _alloc(es, nc, "f0b1T", [128, 32], F32)
    b2r = _alloc(es, nc, "f0b2r", [1, 1024], F32)
    b2bc = _alloc(es, nc, "f0b2bc", [128, 1024], F32)
    xt0 = _alloc(es, nc, "f0x", [128, 1024], F32)
    xt = [xt0, xt0]
    zt0 = _alloc(es, nc, "f0z", [128, 1024], F32)
    zt = [zt0, zt0]
    jt = [_alloc(es, nc, f"f0j{i}", [128, 1024], F32) for i in range(2)]
    lnm = [_alloc(es, nc, f"f0lm{i}", [128, 8], F32) for i in range(2)]
    lnsq = [_alloc(es, nc, f"f0lsq{i}", [128, 1024], F32) for i in range(2)]
    lnxn = [_alloc(es, nc, f"f0lxn{i}", [128, 1024], BF16) for i in range(2)]
    _load_xT(tr, xT, T["xn0n"], 0, S)
    for c in range(8):
        tr.dma("sp", w1T[:, c * 4096:(c + 1) * 4096],
               T["wq1T"][c * 128:(c + 1) * 128, :])
    tr.dma("sp", b1s[:], T["b1f"].rearrange("(a p) -> a p", p=128))
    tr.trans(pb[6][:, 0:32], b1s[:], C.ident_f[0:32, 0:32])
    tr.ee("dve", "tensor_copy", b1T[:], pb[6][:, 0:32])
    tr.dma("sp", b2r[:], T["b2f"][None, :])
    for eh in range(2):
        tr.mm(pb[7][:, 0:512], C.ones[0:1, :], b2r[0:1, eh * 512:(eh + 1) * 512])
        tr.ee("dve", "tensor_copy", b2bc[:, eh * 512:(eh + 1) * 512],
              pb[7][:, 0:512])
    for sg in range(S // 512):
        for fs in range(32):
            bank = pb[fs % 2]
            for ci in range(8):
                tr.mm(bank[:, 0:512],
                      w1T[:, ci * 4096 + fs * 128: ci * 4096 + fs * 128 + 128],
                      xT[:, ci * S + sg * 512: ci * S + sg * 512 + 512],
                      start=(ci == 0), stop=(ci == 7))
            tr.act(hsp[fs // 8][:, (fs % 8) * 512:(fs % 8 + 1) * 512],
                   bank[:, 0:512], AF.Relu, bias=b1T[:, fs:fs + 1])
        for cc in range(32):
            tr.dma("act", w2b[cc % 4][:], T["wq2T"][cc * 128:(cc + 1) * 128, :])
            for stl in range(4):
                for eh in range(2):
                    tr.mm(pb[stl * 2 + eh][:, 0:512],
                          hsp[cc // 8][:, (cc % 8) * 512 + stl * 128: (cc % 8) * 512 + stl * 128 + 128],
                          w2b[cc % 4][:, eh * 512:(eh + 1) * 512],
                          start=(cc == 0), stop=(cc == 31))
        for stl in range(4):
            t_glob = sg * 4 + stl
            b = stl % 2
            tr.dma("sp", xt[b][:], T["x"][t_glob * 128:(t_glob + 1) * 128, :])
            tr.dma("sp", zt[b][:], T["Z0"][t_glob * 128:(t_glob + 1) * 128, :])
            for eh in range(2):
                o = jt[b][:, eh * 512:(eh + 1) * 512]
                tr.ee("dve", "tensor_add", o, pb[stl * 2 + eh][:, 0:512],
                      zt[b][:, eh * 512:(eh + 1) * 512])
                tr.ee("dve", "tensor_add", o, o, xt[b][:, eh * 512:(eh + 1) * 512])
                tr.ee("dve", "tensor_add", o, o, b2bc[:, eh * 512:(eh + 1) * 512])
            negm = lnm[b][:, 0:1]
            ssq = lnm[b][:, 1:2]
            std = lnm[b][:, 3:4]
            rstd = lnm[b][:, 4:5]
            nb = lnm[b][:, 5:6]
            tr.ee("dve", "reduce_sum", negm, jt[b][:], axis=AX, negate=True)
            tr.ee("dve", "tensor_scalar", negm, negm, 1.0 / 1024, None, op0=ALU.mult)
            tr.act(lnsq[b][:], jt[b][:], AF.Square, bias=negm, accum_out=ssq)
            tr.act(std, ssq, AF.Sqrt, bias=C.eps[:, 0:1], scale=1.0 / 1024)
            tr.ee("dve", "reciprocal", rstd, std)
            tr.ee("dve", "tensor_mul", nb, negm, rstd)
            tr.ee("dve", "tensor_scalar", lnxn[b][:], jt[b][:], rstd, nb,
                  op0=ALU.mult, op1=ALU.add)
            tr.dma("sp", T["x1"][t_glob * 128:(t_glob + 1) * 128, :], jt[b][:])
            tr.dma("sp", T["xn1n"][t_glob * 128:(t_glob + 1) * 128, :], lnxn[b][:])


def ph_qkvr(tr, es, C, T):
    nc = tr.nc
    pb = C.banks(es, 8)
    pbb = [p.bitcast(BF16) for p in pb]
    nt = S // 128
    xT = _alloc(es, nc, "qxT", [128, 8 * S], BF16)
    Wsb = {w: _alloc(es, nc, f"qW{w}", [128, 8 * 1024], BF16) for w in ("WQ", "WK", "WV")}
    bq8 = _alloc(es, nc, "qbq8", [128, 8], F32)
    bqs = _alloc(es, nc, "qbqs", [8, 128], F32)
    ktile = [_alloc(es, nc, f"qkt{i}", [128, 512], BF16) for i in range(4)]
    sqt = [_alloc(es, nc, f"qsqt{i}", [128, 512], F32) for i in range(2)]
    sel = _alloc(es, nc, "qsel", [128, 2], F32)
    kn_g = _alloc(es, nc, "qkng", [1, 128], F32)
    qn_g = _alloc(es, nc, "qqng", [1, 128], F32)
    hrow = _alloc(es, nc, "qhrow", [1, 64], F32)
    vpv = _alloc(es, nc, "qvpv", [128, nt * 16 * 65], BF16)
    biasM = _alloc(es, nc, "qbM", [128, 16], F32)
    _load_xT(tr, xT, T["xn1n"], 0, S)
    for w in ("WQ", "WK", "WV"):
        tr.dma("pool", Wsb[w][:].rearrange("p (c e) -> p c e", c=8),
               T[w].rearrange("(c p) e -> p c e", p=128))
    tr.dma("sp", bqs[:], T["bQ"].rearrange("(a p) -> a p", p=128))
    tr.trans(pb[6][:, 0:8], bqs[:], C.ident_f[0:8, 0:8])
    tr.ee("dve", "tensor_scalar", bq8[:], pb[6][:, 0:8], 0.125, None, op0=ALU.mult)
    bkT = _alloc(es, nc, "qbkT", [128, 8], F32)
    tr.dma("sp", bqs[:], T["bK"].rearrange("(a p) -> a p", p=128))
    tr.trans(pb[6][:, 0:8], bqs[:], C.ident_f[0:8, 0:8])
    tr.ee("dve", "tensor_copy", bkT[:], pb[6][:, 0:8])
    _memset(tr, sel[:], 0.0, "dve")
    _memset(tr, sel[0:64, 0:1], 1.0, "dve")
    _memset(tr, sel[64:128, 1:2], 1.0, "dve")
    _memset(tr, vpv[:], 1.0, "pool")
    for (w, dstn, scale, do_bias, ng) in (("WK", "kT", 1.0, False, kn_g),
                                          ("WQ", "qT8", 0.125, True, qn_g)):
        pending = None
        for co in range(8):
            for sg in range(S // 512):
                bank = pb[(co + sg) % 2]
                for ci in range(8):
                    tr.mm(bank[:, 0:512],
                          Wsb[w][:, ci * 1024 + co * 128: ci * 1024 + co * 128 + 128],
                          xT[:, ci * S + sg * 512: ci * S + sg * 512 + 512],
                          start=(ci == 0), stop=(ci == 7))
                kb = ktile[(co * (S // 512) + sg) % 4]
                if do_bias:
                    tr.ee("dve", "tensor_scalar", kb[:], bank[:, 0:512], scale,
                          bq8[:, co:co + 1], op0=ALU.mult, op1=ALU.add)
                else:
                    tr.ee("dve", "tensor_scalar", kb[:], bank[:, 0:512],
                          bkT[:, co:co + 1], None, op0=ALU.add)
                tr.dma("sp", T[dstn][co * 128:(co + 1) * 128, sg * 512:(sg + 1) * 512],
                       kb[:])
                sq_ = sqt[(co * (S // 512) + sg) % 2]
                tr.act(sq_[:], kb[:], AF.Square)
                if pending is not None:
                    pending()
                def _norm(co=co, sg=sg, sq_=sq_, ng=ng):
                    for hh in range(2):
                        tr.mm(pb[4 + hh][0:1, 0:512], sel[:, hh:hh + 1], sq_[:])
                        h = co * 2 + hh
                        nsl = S // 512
                        tr.ee("dve", "reduce_max",
                              ng[0:1, h * nsl + sg: h * nsl + sg + 1],
                              pb[4 + hh][0:1, 0:512], axis=AX)
                pending = _norm
        pending()
    qmax = _alloc(es, nc, "qqmax", [1, 16], F32)
    kmax = _alloc(es, nc, "qkmax", [1, 16], F32)
    prod = _alloc(es, nc, "qprod", [1, 16], F32)
    nsl = S // 512
    for h in range(16):
        tr.ee("dve", "reduce_max", qmax[0:1, h:h + 1],
              qn_g[0:1, h * nsl:(h + 1) * nsl], axis=AX)
        tr.ee("dve", "reduce_max", kmax[0:1, h:h + 1],
              kn_g[0:1, h * nsl:(h + 1) * nsl], axis=AX)
    tr.ee("dve", "tensor_mul", prod[:], qmax[:], kmax[:])
    tr.act(prod[:], prod[:], AF.Sqrt)
    tr.ee("dve", "tensor_scalar", prod[:], prod[:], -1.0, None, op0=ALU.mult)
    tr.mm(pb[6][:, 0:16], C.ones[0:1, :], prod[:])
    tr.ee("dve", "tensor_copy", biasM[:], pb[6][:, 0:16])
    tr.dma("sp", T["biasM"], biasM[:])
    for st in range(nt):
        for dh in range(2):
            bank = pb[(st + dh) % 2]
            for ci in range(8):
                tr.mm(bank[:, 0:512],
                      xT[:, ci * S + st * 128: ci * S + st * 128 + 128],
                      Wsb["WV"][:, ci * 1024 + dh * 512: ci * 1024 + dh * 512 + 512],
                      start=(ci == 0), stop=(ci == 7))
            for hh in range(8):
                h = dh * 8 + hh
                tr.ee("dve", "tensor_copy",
                      vpv[:, (st * 16 + h) * 65:(st * 16 + h) * 65 + 64],
                      bank[:, hh * 64:(hh + 1) * 64])
    for g in range(4):
        w = nt * 16 * 65 // 4
        tr.dma("sp", T["vpv"][:, g * w:(g + 1) * w], vpv[:, g * w:(g + 1) * w])
    wof = _alloc(es, nc, "qwof", [64, 1024], F32)
    bv16 = _alloc(es, nc, "qbv16", [16, 64], F32)
    bv64 = _alloc(es, nc, "qbv64", [64, 16], F32)
    bvs = _alloc(es, nc, "qbvs", [64, 1], F32)
    bvwo = _alloc(es, nc, "qbvwo", [1, 1024], F32)
    bor = _alloc(es, nc, "qbor", [1, 1024], F32)
    jbr = _alloc(es, nc, "qjbr", [1, 1024], F32)
    tr.dma("sp", wof[:], T["WO"])
    tr.dma("sp", bv16[:], T["bV"].rearrange("(h d) -> h d", d=64))
    tr.trans(pb[6][0:64, 0:16], bv16[:], C.ident_f[0:16, 0:16])
    tr.ee("dve", "tensor_copy", bv64[:], pb[6][0:64, 0:16])
    tr.ee("dve", "reduce_sum", bvs[:], bv64[:], axis=AX)
    for eh in range(2):
        tr.mm(pb[7][0:1, 0:512], bvs[:], wof[:, eh * 512:(eh + 1) * 512])
        tr.ee("dve", "tensor_copy", bvwo[0:1, eh * 512:(eh + 1) * 512],
              pb[7][0:1, 0:512])
    tr.dma("sp", bor[:], T["bO"][None, :])
    tr.ee("dve", "scalar_tensor_tensor", jbr[:], bor[:], 16.0, bvwo[:],
          op0=ALU.mult, op1=ALU.add)
    tr.dma("sp", T["jbrow"], jbr[:])


NBLK = 48
BS = 512
YROWS = NBLK * BS


def ph_router(tr, es, C, T):
    nc = tr.nc
    pb = C.banks(es, 8)
    nt = S // 128
    xT = _alloc(es, nc, "rxT", [128, 8 * S], BF16)
    _load_xT(tr, xT, T["xn1n"], 0, S)
    rwsb = _alloc(es, nc, "qrw", [128, 8 * 16], BF16)
    rbsb = _alloc(es, nc, "qrb", [16, 1], F32)
    lgT = _alloc(es, nc, "qlgT", [16, S], F32)
    lnt = _alloc(es, nc, "qlnt", [128, 16], F32)
    lw = [_alloc(es, nc, f"qlw{i}", [128, 16], F32) for i in range(2)]
    eqall = _alloc(es, nc, "qeq", [128, nt * 64], F32)
    dport = _alloc(es, nc, "qd", [128, 16], F32)
    indb = _alloc(es, nc, "qind", [128, 16], F32)
    indb16 = _alloc(es, nc, "qindb", [128, 16], BF16)
    expb = _alloc(es, nc, "qexp", [128, 16], F32)
    peb = _alloc(es, nc, "qpe", [128, 16], F32)
    cechunk = _alloc(es, nc, "qcec", [128, 16], F32)
    ceT_sb = _alloc(es, nc, "qceT", [16, S], F32)
    smr = _alloc(es, nc, "qsmr", [128, 8], F32)
    wcum = _alloc(es, nc, "qwcum", [128, 16], F32)
    wsel = _alloc(es, nc, "qwsel", [128, nt * 4], F32)
    cesel = _alloc(es, nc, "qcesel", [128, nt * 4], F32)
    tmp16 = _alloc(es, nc, "qtmp16", [128, 16], F32)
    crow = _alloc(es, nc, "qcrow", [1, nt * 16], F32)
    hs = [_alloc(es, nc, f"qhs{i}", [1, nt * 16], F32) for i in range(2)]
    cumrow = _alloc(es, nc, "qcumrow", [1, nt * 16], F32)
    trow = _alloc(es, nc, "qtrow", [1, 16], F32)
    t16 = _alloc(es, nc, "qt16", [1, 16], F32)
    hsb = [_alloc(es, nc, f"qhsb{i}", [1, 16], F32) for i in range(2)]
    sbrow = _alloc(es, nc, "qsbrow", [1, 16], F32)
    brow48 = _alloc(es, nc, "qbrow48", [1, NBLK], F32)
    ebrow = _alloc(es, nc, "qebrow", [1, NBLK], F32)
    eb48t = _alloc(es, nc, "qeb48t", [1, NBLK], F32)
    iotac = _alloc(es, nc, "qiotac", [128, 1], F32)
    rowt = _alloc(es, nc, "qrowt", [1, 16], F32)
    brt = _alloc(es, nc, "qbrt", [128, 16], F32)
    dselcol = _alloc(es, nc, "qdselc", [128, nt * 4], F32)
    rec = [_alloc(es, nc, f"qrec{i}", [128, 2], F32) for i in range(4)]
    reci = [_alloc(es, nc, f"qreci{i}", [128, 1], I32) for i in range(4)]
    ztab = _alloc(es, nc, "qztab", [128, YROWS * 2 // 128], F32)
    b1x = _alloc(es, nc, "qb1x", [16, 128], F32)
    b1all = _alloc(es, nc, "qb1all", [128, 256], F32)

    tr.dma("pool", rwsb[:].rearrange("p (c e) -> p c e", c=8),
           T["rw"].rearrange("(c p) e -> p c e", p=128))
    tr.dma("sp", rbsb[:], T["rb"][:, None])
    _iota(tr, iotac[:], 0.0, 1, [[1, 1]])
    tr.ee("dve", "memset", ztab[:], 0.0)
    tr.dma("sp", T["tab"].rearrange("(p a) c -> p (a c)", p=128), ztab[:])
    for e in range(16):
        tr.dma("sp", b1x[:], T["b1e"][e].rearrange("(i p) -> i p", p=128))
        tr.trans(pb[6][:, 0:16], b1x[:], C.ident_f[0:16, 0:16])
        tr.ee("dve", "tensor_copy", b1all[:, e * 16:(e + 1) * 16], pb[6][:, 0:16])
    tr.dma("sp", T["b1scr"].rearrange("e p i -> p e i"),
           b1all[:].rearrange("p (e i) -> p e i", e=16))
    for sg in range(S // 512):
        bank = pb[sg % 2]
        for ci in range(8):
            tr.mm(bank[0:16, 0:512], rwsb[:, ci * 16:(ci + 1) * 16],
                  xT[:, ci * S + sg * 512: ci * S + sg * 512 + 512],
                  start=(ci == 0), stop=(ci == 7))
        tr.ee("dve", "tensor_scalar", lgT[0:16, sg * 512:(sg + 1) * 512],
              bank[0:16, 0:512], rbsb[:, 0:1], None, op0=ALU.add)
    for st in range(nt):
        tr.trans(pb[2][:, 0:16], lgT[0:16, st * 128:(st + 1) * 128],
                 C.ident_f[0:16, 0:16])
        tr.ee("dve", "tensor_copy", lnt[:], pb[2][:, 0:16])
        mx = smr[:, 0:1]
        m1n = smr[:, 1:2]
        cur = lnt
        for it in range(4):
            tr.ee("dve", "reduce_max", mx, cur[:], axis=AX)
            if it == 0:
                tr.ee("dve", "tensor_scalar", m1n, mx, -1.0, None, op0=ALU.mult)
            eqs = eqall[:, st * 64 + it * 16: st * 64 + (it + 1) * 16]
            tr.ee("dve", "tensor_scalar", eqs, cur[:], mx, None, op0=ALU.is_equal)
            nxt = lw[it % 2]
            tr.ee("dve", "scalar_tensor_tensor", nxt[:], eqs, -1e30, cur[:],
                  op0=ALU.mult, op1=ALU.add)
            cur = nxt
        tr.ee("dve", "tensor_sub", dport[:], lnt[:], cur[:])
        tr.ee("dve", "tensor_scalar", indb[:], dport[:], 1e20, None, op0=ALU.is_gt)
        tr.act(expb[:], lnt[:], AF.Exp, bias=m1n)
        tr.ee("dve", "tensor_mul", peb[:], expb[:], indb[:])
        ssum = smr[:, 2:3]
        rr = smr[:, 3:4]
        tr.ee("dve", "reduce_sum", ssum, peb[:], axis=AX)
        tr.ee("dve", "reciprocal", rr, ssum)
        tr.ee("dve", "tensor_scalar", cechunk[:], peb[:], rr, None, op0=ALU.mult)
        tr.trans(pb[2][0:16, 0:128], cechunk[:], C.ident_f[:])
        tr.ee("dve", "tensor_copy", ceT_sb[0:16, st * 128:(st + 1) * 128],
              pb[2][0:16, 0:128])
        tr.ee("dve", "tensor_copy", indb16[:], indb[:])
        tr.mm(pb[3][:, 0:16], C.lstrict[:], indb16[:])
        tr.ee("dve", "tensor_copy", wcum[:], pb[3][:, 0:16])
        tr.mm(pb[4][0:1, 0:16], C.onescol_bf[:], indb16[:])
        tr.ee("dve", "tensor_copy", crow[0:1, st * 16:(st + 1) * 16],
              pb[4][0:1, 0:16])
        for k in range(4):
            eqs = eqall[:, st * 64 + k * 16: st * 64 + (k + 1) * 16]
            tr.ee("dve", "tensor_mul", tmp16[:], cechunk[:], eqs)
            tr.ee("dve", "reduce_sum", cesel[:, st * 4 + k: st * 4 + k + 1],
                  tmp16[:], axis=AX)
            tr.ee("dve", "tensor_mul", tmp16[:], wcum[:], eqs)
            tr.ee("dve", "reduce_sum", wsel[:, st * 4 + k: st * 4 + k + 1],
                  tmp16[:], axis=AX)
    tr.dma("sp", T["ceT"], ceT_sb[:])
    W = nt * 16
    srcrow = crow
    for i, sh in enumerate((16, 32, 64, 128, 256)):
        dst = hs[i % 2]
        tr.ee("dve", "tensor_copy", dst[0:1, 0:sh], srcrow[0:1, 0:sh])
        tr.ee("dve", "tensor_add", dst[0:1, sh:W], srcrow[0:1, sh:W],
              srcrow[0:1, 0:W - sh])
        srcrow = dst
    cuminc = srcrow
    tr.ee("dve", "tensor_sub", cumrow[:], cuminc[:], crow[:])
    nrow = cuminc[0:1, (nt - 1) * 16: nt * 16]
    tr.ee("dve", "memset", trow[:], 0.0)
    for j in range(8):
        tr.ee("dve", "tensor_scalar", t16[:], nrow, float(BS * j), None,
              op0=ALU.is_gt)
        tr.ee("dve", "tensor_add", trow[:], trow[:], t16[:])
    srcrow = trow
    for i, sh in enumerate((1, 2, 4, 8)):
        dst = hsb[i % 2]
        tr.ee("dve", "tensor_copy", dst[0:1, 0:sh], srcrow[0:1, 0:sh])
        tr.ee("dve", "tensor_add", dst[0:1, sh:16], srcrow[0:1, sh:16],
              srcrow[0:1, 0:16 - sh])
        srcrow = dst
    binc = srcrow
    tr.ee("dve", "tensor_sub", sbrow[:], binc[:], trow[:])
    tr.ee("dve", "tensor_scalar", sbrow[:], sbrow[:], float(BS), None,
          op0=ALU.mult)
    _iota(tr, brow48[:], 0.0, 0, [[1, NBLK]])
    tr.ee("dve", "memset", ebrow[:], 0.0)
    for e in range(16):
        tr.ee("dve", "tensor_scalar", eb48t[:], brow48[:], binc[0:1, e:e + 1],
              None, op0=ALU.is_ge)
        tr.ee("dve", "tensor_add", ebrow[:], ebrow[:], eb48t[:])
    tr.ee("dve", "tensor_scalar", ebrow[:], ebrow[:], 15.0, None, op0=ALU.min)
    tr.dma("sp", T["ebrow"], ebrow[:])
    for st in range(nt):
        tr.ee("dve", "tensor_add", rowt[:], cumrow[0:1, st * 16:(st + 1) * 16],
              sbrow[:])
        tr.mm(pb[5][:, 0:16], C.ones[0:1, :], rowt[:])
        tr.ee("dve", "tensor_copy", brt[:], pb[5][:, 0:16])
        for k in range(4):
            kb = k % 4
            eqs = eqall[:, st * 64 + k * 16: st * 64 + (k + 1) * 16]
            tr.ee("dve", "tensor_mul", tmp16[:], brt[:], eqs)
            dcol = dselcol[:, st * 4 + k: st * 4 + k + 1]
            tr.ee("dve", "reduce_sum", dcol, tmp16[:], axis=AX)
            tr.ee("dve", "tensor_add", dcol, dcol,
                  wsel[:, st * 4 + k: st * 4 + k + 1])
            tr.ee("dve", "tensor_scalar", rec[kb][:, 0:1], iotac[:],
                  float(128 * st), None, op0=ALU.add)
            tr.ee("dve", "tensor_copy", rec[kb][:, 1:2],
                  cesel[:, st * 4 + k: st * 4 + k + 1])
            tr.ee("dve", "tensor_copy", reci[kb][:], dcol)
            tr.idma("pool", T["tab"], rec[kb][:], reci[kb][:], gather=False,
                    waw_group="tab")
    tr.queue_fence("pool", ["tab"])
    tr.dma("sp", T["dsel"], dselcol[:])


def ph_wprep(tr, es, C, T):
    for e in range(16):
        tr.dma("pool", T["w1scr"][e].rearrange("p (c i) -> p c i", c=8),
               T["w1e"][e].rearrange("(c p) i -> p c i", p=128),
               waw_group="w1scr")
        tr.dma("pool", T["w2scr"][e].rearrange("p (c i) -> p c i", c=16),
               T["w2e"][e].rearrange("(c p) i -> p c i", p=128),
               waw_group="w2scr")
    tr.queue_fence("pool", ["w1scr", "w2scr"])


def ph_moeblk(tr, es, C, T):
    nc = tr.nc
    pb = C.banks(es, 8)
    pbb = [p.bitcast(BF16) for p in pb]
    w1sb = [_alloc(es, nc, f"kw1{i}", [128, 16384], BF16) for i in range(2)]
    w2sb = [_alloc(es, nc, f"kw2{i}", [128, 16384], BF16) for i in range(2)]
    b1sb = [_alloc(es, nc, f"kb1{i}", [128, 16], F32) for i in range(2)]
    tabt = [_alloc(es, nc, f"ktab{i}", [128, 8], F32) for i in range(2)]
    tki = [_alloc(es, nc, f"ktki{i}", [128, 4], I32) for i in range(2)]
    xtok = [_alloc(es, nc, f"kxt{i}", [128, 4096], BF16) for i in range(2)]
    xTb = _alloc(es, nc, "kxTb", [128, 4096], BF16)
    hsb = _alloc(es, nc, "khsb", [128, 8192], BF16)
    ybt = [_alloc(es, nc, f"kyb{i}", [128, 1024], F32) for i in range(2)]
    ebrow = _alloc(es, nc, "kebr", [1, NBLK], F32)
    ebbc = _alloc(es, nc, "kebbc", [128, NBLK], F32)
    widxf = _alloc(es, nc, "kwidf", [128, NBLK], F32)
    widxi = _alloc(es, nc, "kwidi", [128, NBLK], I32)
    iotac = _alloc(es, nc, "kiota", [128, 1], F32)
    w1flat = T["w1scr"].rearrange("e p n -> (e p) n")
    w2flat = T["w2scr"].rearrange("e p n -> (e p) n")
    b1flat = T["b1scr"].rearrange("e p i -> (e p) i")

    tr.dma("sp", ebrow[:], T["ebrow"])
    _iota(tr, iotac[:], 0.0, 1, [[1, 1]])
    for g in range(NBLK // 8):
        tr.mm(pb[7][:, 0:8], C.ones[0:1, :], ebrow[0:1, g * 8:(g + 1) * 8])
        tr.ee("dve", "tensor_copy", ebbc[:, g * 8:(g + 1) * 8], pb[7][:, 0:8])
    tr.ee("dve", "tensor_scalar", widxf[:], ebbc[:], 128.0, iotac[:, 0:1],
          op0=ALU.mult, op1=ALU.add)
    tr.ee("dve", "tensor_copy", widxi[:], widxf[:])

    def gathers(b):
        bb = b % 2
        tr.idma("pool", w1sb[bb][:], w1flat, widxi[:, b:b + 1], gather=True)
        tr.idma("pool", w2sb[bb][:], w2flat, widxi[:, b:b + 1], gather=True)
        tr.idma("pool", b1sb[bb][:], b1flat, widxi[:, b:b + 1], gather=True)
        tr.dma("sp", tabt[bb][:],
               T["tab"][b * BS:(b + 1) * BS].rearrange("(p x) c -> p (x c)",
                                                       p=128))
        tr.ee("dve", "tensor_copy",
              tki[bb][:].rearrange("p (x c) -> p x c", c=1),
              tabt[bb][:].rearrange("p (x c) -> p x c", c=2)[:, :, 0:1])
        for x in range(4):
            tr.idma("pool", xtok[bb][:, x * 1024:(x + 1) * 1024], T["xn1n"],
                    tki[bb][:, x:x + 1], gather=True)

    def compute(b):
        bb = b % 2
        for x in range(4):
            for cg in range(2):
                for q in range(4):
                    ci = cg * 4 + q
                    tr.trans(pbb[6][:, q * 128:(q + 1) * 128],
                             xtok[bb][:, x * 1024 + ci * 128:
                                      x * 1024 + ci * 128 + 128],
                             C.ident_bf[:])
                dst = xTb[:].rearrange("p (c s) -> p c s", s=512)[
                    :, cg * 4:(cg + 1) * 4, x * 128:(x + 1) * 128]
                src = pbb[6][:].rearrange("p (q s) -> p q s", s=128)[:, 0:4, :]
                tr.ee("dve", "tensor_copy", dst, src)
        for isl in range(16):
            bank = pb[isl % 2]
            for ci in range(8):
                tr.mm(bank[:, 0:512],
                      w1sb[bb][:, ci * 2048 + isl * 128:
                               ci * 2048 + isl * 128 + 128],
                      xTb[:, ci * 512:(ci + 1) * 512],
                      start=(ci == 0), stop=(ci == 7))
            tr.act(hsb[:, isl * 512:(isl + 1) * 512], bank[:, 0:512],
                   AF.Gelu, bias=b1sb[bb][:, isl:isl + 1])
        for x in range(4):
            for eh in range(2):
                bank = pb[2 + (x % 2) * 2 + eh]
                for cc in range(16):
                    tr.mm(bank[:, 0:512],
                          hsb[:, cc * 512 + x * 128: cc * 512 + x * 128 + 128],
                          w2sb[bb][:, cc * 1024 + eh * 512:
                                   cc * 1024 + eh * 512 + 512],
                          start=(cc == 0), stop=(cc == 15))
                tr.ee("dve", "tensor_scalar",
                      ybt[x % 2][:, eh * 512:(eh + 1) * 512], bank[:, 0:512],
                      tabt[bb][:, 2 * x + 1: 2 * x + 2], None, op0=ALU.mult)
            tr.dma("sp",
                   T["yplane"][b * BS:(b + 1) * BS]
                   .rearrange("(p x) d -> p x d", p=128)[:, x:x + 1, :],
                   ybt[x % 2][:].rearrange("p (x d) -> p x d", x=1),
                   waw_group="ypl")

    gathers(0)
    for b in range(NBLK):
        if b + 1 < NBLK:
            gathers(b + 1)
        compute(b)
    tr.queue_fence("sp", ["yplane"])


def ph_join(tr, es, C, T):
    nc = tr.nc
    pb = C.banks(es, 8)
    nt = S // 128
    dself = _alloc(es, nc, "jdsf", [128, nt * 4], F32)
    dseli = _alloc(es, nc, "jdsi", [128, nt * 4], I32)
    jbr = _alloc(es, nc, "jjbr", [1, 1024], F32)
    jbbc = _alloc(es, nc, "jjbbc", [128, 1024], F32)
    ceTsb = _alloc(es, nc, "jceT", [16, S], BF16)
    b2sb = _alloc(es, nc, "jb2", [16, 1024], BF16)
    ygt = [_alloc(es, nc, f"jyg{i}", [128, 4096], F32) for i in range(2)]
    x1t = [_alloc(es, nc, f"jx1{i}", [128, 1024], F32) for i in range(2)]
    att = [_alloc(es, nc, f"jat{i}", [128, 1024], F32) for i in range(2)]
    outt = [_alloc(es, nc, f"jot{i}", [128, 1024], F32) for i in range(2)]
    tr.dma("sp", dself[:], T["dsel"])
    tr.ee("dve", "tensor_copy", dseli[:], dself[:])
    tr.dma("sp", jbr[:], T["jbrow"])
    for eh in range(2):
        tr.mm(pb[6][:, 0:512], C.ones[0:1, :], jbr[0:1, eh * 512:(eh + 1) * 512])
        tr.ee("dve", "tensor_copy", jbbc[:, eh * 512:(eh + 1) * 512],
              pb[6][:, 0:512])
    tr.dma("pool", ceTsb[:], T["ceT"])
    tr.dma("pool", b2sb[:], T["b2e"])
    for st in range(nt):
        b = st % 2
        tr.dma("sp", x1t[b][:], T["x1"][st * 128:(st + 1) * 128, :])
        tr.dma("sp", att[b][:], T["attn1"][st * 128:(st + 1) * 128, :])
        for k in range(4):
            tr.idma("pool", ygt[b][:, k * 1024:(k + 1) * 1024], T["yplane"],
                    dseli[:, st * 4 + k: st * 4 + k + 1], gather=True)
        for eh in range(2):
            bank = pb[eh]
            tr.mm(bank[:, 0:512], ceTsb[0:16, st * 128:(st + 1) * 128],
                  b2sb[0:16, eh * 512:(eh + 1) * 512])
            o = outt[b][:, eh * 512:(eh + 1) * 512]
            tr.ee("dve", "tensor_add", o,
                  ygt[b][:, eh * 512: eh * 512 + 512],
                  ygt[b][:, 1024 + eh * 512: 1024 + eh * 512 + 512])
            tr.ee("dve", "tensor_add", o, o,
                  ygt[b][:, 2048 + eh * 512: 2048 + eh * 512 + 512])
            tr.ee("dve", "tensor_add", o, o,
                  ygt[b][:, 3072 + eh * 512: 3072 + eh * 512 + 512])
            tr.ee("dve", "tensor_add", o, o,
                  x1t[b][:, eh * 512:(eh + 1) * 512])
            tr.ee("dve", "tensor_add", o, o,
                  att[b][:, eh * 512:(eh + 1) * 512])
            tr.ee("dve", "tensor_add", o, o,
                  jbbc[:, eh * 512:(eh + 1) * 512])
            tr.ee("dve", "tensor_add", o, o, bank[:, 0:512])
        tr.dma("sp", T["out"][st * 128:(st + 1) * 128, :], outt[b][:],
               waw_group="outg")




def ph_attn1(tr, es, C, T):
    nc = tr.nc
    sp2 = [es.enter_context(nc.psum_tensor(f"t1sp{i}", [128, 1024], F32)) for i in range(2)]
    pv = [es.enter_context(nc.psum_tensor(f"t1pv{i}", [128, 512], F32)) for i in range(4)]
    nt = S // 128
    kT = _alloc(es, nc, "t1kT", [128, 8 * S], BF16)
    vpv = _alloc(es, nc, "t1v", [128, nt * 16 * 65], BF16)
    biasM = _alloc(es, nc, "t1bM", [128, 16], F32)
    biasMg = _alloc(es, nc, "t1bMg", [128, 1], F32)
    wobf = _alloc(es, nc, "t1wo_w", [64, 1024], BF16)
    Zacc = _alloc(es, nc, "t1Z", [64, S], F32)
    Zbf = _alloc(es, nc, "t1Zb", [64, 512], BF16)
    qsl = [_alloc(es, nc, f"t1q{i}", [128, 8 * 512], BF16) for i in range(2)]
    pt = [_alloc(es, nc, f"t1p{i}", [128, 1024], BF16) for i in range(2)]
    rp = _alloc(es, nc, "t1rp", [1, 512], F32)
    rtmp = _alloc(es, nc, "t1rt", [1, 512], F32)
    bcb = [_alloc(es, nc, f"t1bcs{i}", [64, 512], F32) for i in range(2)]
    ztmp = _alloc(es, nc, "t1zt", [64, 512], F32)
    att = [_alloc(es, nc, f"t1a{i}", [128, 1024], F32) for i in range(2)]
    for c in range(8):
        tr.dma("sp", kT[:, c * S:(c + 1) * S], T["kT"][c * 128:(c + 1) * 128, :])
    for g in range(4):
        w = nt * 16 * 65 // 4
        tr.dma("sp", vpv[:, g * w:(g + 1) * w], T["vpv"][:, g * w:(g + 1) * w])
    tr.dma("sp", biasM[:], T["biasM"])
    tr.op("dve", lambda e, o=biasMg[:], i=biasM[:]: e.tensor_reduce(
        o, i, AX, ALU.min), reads=[biasM[:]], writes=[biasMg[:]])
    tr.dma("pool", wobf[:], T["WO"])

    def s_step(qg, pr, j):
        sp = sp2[j % 2]
        qlo = max(0, 128 * j - 512 * qg)
        qb = qsl[qg % 2]
        c = pr
        for hh in range(2):
            lo = hh * 64
            tr.mm(sp[:, hh * 512 + qlo: hh * 512 + 512],
                  kT[lo:lo + 64, c * S + j * 128: c * S + j * 128 + 128],
                  qb[lo:lo + 64, c * 512 + qlo: c * 512 + 512],
                  start=True, stop=True)

    def pv_step(qg, pr, j, nk):
        sp = sp2[j % 2]
        ptb = pt[j % 2]
        qlo = max(0, 128 * j - 512 * qg)
        wd = 512 - qlo
        src3 = sp[:].rearrange("p (h n) -> p h n", h=2)[:, :, qlo:512]
        dst3 = ptb[:].rearrange("p (h n) -> p h n", h=2)[:, :, qlo:512]
        tr.act(dst3, src3, AF.Exp, bias=biasMg[:, 0:1])
        if j >= 4 * qg:
            for hh in range(2):
                a = ptb[:, hh * 512 + qlo: hh * 512 + 512]
                tr.op("pool",
                      lambda e, a=a, wd=wd: e.affine_select(
                          out=a, in_=a, compare_op=ALU.is_ge, fill=0.0,
                          base=0, pattern=[[1, wd]], channel_multiplier=-1),
                      reads=[a], writes=[a])
        for hh in range(2):
            h = 2 * pr + hh
            tr.mm(pv[2 * (pr % 2) + hh][0:65, qlo:512],
                  vpv[:, (j * 16 + h) * 65:(j * 16 + h) * 65 + 65],
                  ptb[:, hh * 512 + qlo: hh * 512 + 512],
                  start=(j == 0), stop=(j == nk - 1))

    def tail(qg, pr):
        for hh in range(2):
            h = 2 * pr + hh
            pvb = pv[2 * (pr % 2) + hh]
            bco = sp2[hh]
            tr.act(rtmp[:], pvb[64:65, 0:512], AF.Ln)
            tr.act(rp[:], rtmp[:], AF.Exp, scale=-1.0)
            tr.mm(bco[0:64, 0:512], C.ones[0:1, 0:64], rp[:])
            tr.ee("dve", "tensor_copy", bcb[hh][:], bco[0:64, 0:512])
            zsl = Zacc[:, qg * 512:(qg + 1) * 512]
            if h == 0:
                tr.ee("dve", "tensor_mul", zsl, pvb[0:64, 0:512], bcb[hh][:])
            else:
                tr.ee("dve", "tensor_mul", ztmp[:], pvb[0:64, 0:512], bcb[hh][:])
                tr.ee("dve", "tensor_add", zsl, zsl, ztmp[:])

    for qg in range(S // 512):
        qb = qsl[qg % 2]
        tr.dma("sp", qb[:].rearrange("p (c s) -> p c s", c=8),
               T["qT8"].rearrange("(c p) s -> p c s", p=128)[:, :, qg * 512:(qg + 1) * 512])
        nk = 4 * (qg + 1)
        for pr in range(8):
            s_step(qg, pr, 0)
            for j in range(1, nk):
                s_step(qg, pr, j)
                pv_step(qg, pr, j - 1, nk)
            pv_step(qg, pr, nk - 1, nk)
            if pr > 0:
                tail(qg, pr - 1)
        tail(qg, 7)
        tr.ee("dve", "tensor_copy", Zbf[:], Zacc[:, qg * 512:(qg + 1) * 512])
        for qt in range(4):
            ab = att[qt % 2]
            for eh in range(2):
                pwo = pv[qt % 2]
                tr.mm(pwo[:, 0:512], Zbf[:, qt * 128:(qt + 1) * 128],
                      wobf[:, eh * 512:(eh + 1) * 512])
                tr.ee("dve", "tensor_copy", ab[:, eh * 512:(eh + 1) * 512],
                      pwo[:, 0:512])
            q = qg * 4 + qt
            tr.dma("sp", T["attn1"][q * 128:(q + 1) * 128, :], ab[:])



INPUT_TENSORS = {
    "x": lambda: ([S, 1024], F32),
    "R": lambda: ([1024, 1024], F32),
    "W1": lambda: ([4096, 1024], F32),
    "b1f": lambda: ([4096], F32),
    "W2": lambda: ([1024, 4096], F32),
    "b2f": lambda: ([1024], F32),
    "WQ": lambda: ([1024, 1024], F32),
    "bQ": lambda: ([1024], F32),
    "bK": lambda: ([1024], F32),
    "WK": lambda: ([1024, 1024], F32),
    "WV": lambda: ([1024, 1024], F32),
    "WO": lambda: ([64, 1024], F32),
    "bO": lambda: ([1024], F32),
    "bV": lambda: ([1024], F32),
    "rw": lambda: ([1024, 16], F32),
    "rb": lambda: ([16], F32),
    "w1e": lambda: ([16, 1024, 2048], F32),
    "b1e": lambda: ([16, 2048], F32),
    "w2e": lambda: ([16, 2048, 1024], F32),
    "b2e": lambda: ([16, 1024], F32),
}
SCRATCH_TENSORS = {
    "xn0n": lambda: ([S, 1024], BF16),
    "Z0": lambda: ([S, 1024], F32),
    "wq1T": lambda: ([1024, 4096], BF16),
    "wq2T": lambda: ([4096, 1024], BF16),
    "x1": lambda: ([S, 1024], F32),
    "xn1n": lambda: ([S, 1024], BF16),
    "qT8": lambda: ([1024, S], BF16),
    "kT": lambda: ([1024, S], BF16),
    "vpv": lambda: ([128, (S // 128) * 16 * 65], BF16),
    "biasM": lambda: ([128, 16], F32),
    "jbrow": lambda: ([1, 1024], F32),
    "attn1": lambda: ([S, 1024], F32),
    "ceT": lambda: ([16, S], F32),
    "tab": lambda: ([YROWS, 2], F32),
    "dsel": lambda: ([128, (S // 128) * 4], F32),
    "ebrow": lambda: ([1, NBLK], F32),
    "b1scr": lambda: ([16, 128, 16], F32),
    "w1scr": lambda: ([16, 128, 16384], BF16),
    "w2scr": lambda: ([16, 128, 16384], BF16),
    "yplane": lambda: ([YROWS, 1024], F32),
    "out": lambda: ([S, 1024], F32),
}

PHASES = {
    "ln0": (lambda tr, es, C, T: ph_ln(tr, es, C, T, "x", "xn0n", "l0"),
            ["x"], ["xn0n"]),
    "quant": (ph_quant, ["W1", "W2"], ["wq1T", "wq2T"]),
    "attn0": (ph_attn0, ["xn0n", "R"], ["Z0"]),
    "ffn0": (ph_ffn0, ["xn0n", "wq1T", "wq2T", "b1f", "b2f", "x", "Z0"],
             ["x1", "xn1n"]),
    "qkvr": (ph_qkvr, ["xn1n", "WQ", "bQ", "WK", "bK", "WV", "WO", "bO", "bV"],
             ["qT8", "kT", "vpv", "biasM", "jbrow"]),
    "router": (ph_router, ["xn1n", "rw", "rb", "b1e"],
               ["ceT", "tab", "dsel", "ebrow", "b1scr"]),
    "wprep": (ph_wprep, ["w1e", "w2e"], ["w1scr", "w2scr"]),
    "attn1": (ph_attn1, ["qT8", "kT", "vpv", "biasM", "WO"], ["attn1"]),
    "moeblk": (ph_moeblk, ["tab", "ebrow", "w1scr", "w2scr", "b1scr", "xn1n"],
               ["yplane"]),
    "join": (ph_join, ["yplane", "dsel", "x1", "attn1", "jbrow", "ceT", "b2e"],
             ["out"]),
}

GROUPS = [["ln0"], ["quant"], ["wprep"], ["attn0"], ["ffn0"], ["qkvr"],
          ["router"], ["attn1"], ["moeblk"], ["join"]]


def build_program(phase_names, need_later):
    reads, writes = set(), set()
    for p in phase_names:
        _, r, w = PHASES[p]
        reads.update(r)
        writes.update(w)
    nc = bass.Bass("TRN2", target_bir_lowering=False)
    T = {}
    in_names, out_names = [], []
    for name in sorted(reads | writes):
        if name in INPUT_TENSORS:
            shape, dt = INPUT_TENSORS[name]()
            kind = "ExternalInput"
        else:
            shape, dt = SCRATCH_TENSORS[name]()
            if name in writes:
                kind = "ExternalOutput" if (name in need_later or name == "out") else "Internal"
            else:
                kind = "ExternalInput"
        if kind == "ExternalInput":
            in_names.append(name)
        elif kind == "ExternalOutput":
            out_names.append(name)
        T[name] = nc.dram_tensor(name, shape, dt, kind=kind).ap()
    tr = Tr(nc)
    with contextlib.ExitStack() as es_ctx:
        C = Ctx(tr, es_ctx)
        for p in phase_names:
            fn, _, _ = PHASES[p]
            with contextlib.ExitStack() as es:
                fn(tr, es, C, T)
            tr.barrier()
        tr.finalize()
        tr.emit()
    return nc, in_names, out_names


def _values_from_inputs(inputs):
    c = np.ascontiguousarray
    xs = [c(inputs["L0_attn_encoding"][i]) for i in range(8)]
    shared = {
        "R": inputs["L0_attn_R"], "W1": inputs["L0_ffn_W1"], "b1f": inputs["L0_ffn_b1"],
        "W2": inputs["L0_ffn_W2"], "b2f": inputs["L0_ffn_b2"],
        "WQ": inputs["L1_attn_WQ"], "bQ": inputs["L1_attn_bQ"], "bK": inputs["L1_attn_bK"],
        "WK": inputs["L1_attn_WK"], "WV": inputs["L1_attn_WV"],
        "WO": inputs["L1_attn_WO"], "bO": inputs["L1_attn_bO"],
        "bV": inputs["L1_attn_bV"],
        "rw": inputs["L1_ffn_router_w"], "rb": inputs["L1_ffn_router_b"],
        "w1e": inputs["L1_ffn_w1"], "b1e": inputs["L1_ffn_b1"],
        "w2e": inputs["L1_ffn_w2"], "b2e": inputs["L1_ffn_b2"],
    }
    vals = {"x": xs}
    for k, v in shared.items():
        a = c(np.asarray(v))
        vals[k] = [a] * 8
    return vals


FUSED = True


def kernel(**inputs):
    vals = _values_from_inputs(inputs)
    groups = [[p for g in GROUPS for p in g]] if FUSED else GROUPS
    for gi, g in enumerate(groups):
        later = set()
        for g2 in groups[gi + 1:]:
            for p in g2:
                later.update(PHASES[p][1])
        nc, in_names, out_names = build_program(g, later)
        in_maps = [{n: vals[n][i] for n in in_names} for i in range(8)]
        res = run_bass_kernel_spmd(nc, in_maps, core_ids=list(range(8)))
        for n in out_names:
            vals[n] = [np.asarray(res.results[i][n]) for i in range(8)]
    return np.stack([vals["out"][i] for i in range(8)], axis=0)


# version 11 (final)
# speedup vs baseline: 1.6920x; 1.0619x over previous
import math
import contextlib
import numpy as np
import ml_dtypes
import concourse.bass as bass
import concourse.mybir as mybir
from concourse.bass_core_rust import IndirectOffsetOnAxis
from concourse.bass_utils import run_bass_kernel_spmd

F32 = mybir.dt.float32
BF16 = mybir.dt.bfloat16
I32 = mybir.dt.int32
AF = mybir.ActivationFunctionType
ALU = mybir.AluOpType
AX = mybir.AxisListType.X

S = 4096
D = 1024
NT = S // 128
NC = D // 128
NEG = -1e9

ENGINES = ("pe", "dve", "act", "pool", "sp")
_ENG_ATTR = {"pe": "tensor", "dve": "vector", "act": "scalar", "pool": "gpsimd", "sp": "sync"}
_QUEUES = ("sp", "act", "pool")
_NROT = 4


class Op:
    __slots__ = ("build", "waits", "incs")

    def __init__(self, build):
        self.build = build
        self.waits = []
        self.incs = []


def _key(ap):
    t = getattr(ap, "tensor", ap)
    return t.name


class Tr:

    def __init__(self, nc):
        self.nc = nc
        self.ops = {e: [] for e in ENGINES}
        self.counts = {}
        self.mem = {}
        self.marks = {e: {} for e in ENGINES}
        self.sems = set()
        for e in ("pe", "dve", "act", "pool"):
            self._sem("S_" + e)
        self.qrot = {}
        for q in _QUEUES:
            for i in range(_NROT):
                self._sem(f"D_{q}{i}")
            self.qrot[q] = {"i": 0, "cum": [0] * _NROT}
        self.pe_pending_reads = []
        self.pe_open = set()
        self._last_op = {e: None for e in ENGINES}

    def _sem(self, name):
        self.sems.add(name)
        self.counts.setdefault(name, 0)

    def _rec(self, key):
        return self.mem.setdefault(key, {"w": None, "r": []})

    def _wait(self, op, eng, sem, val):
        if val <= 0:
            return
        wm = self.marks[eng]
        if wm.get(sem, 0) >= val:
            return
        op.waits.append((sem, val))
        wm[sem] = val

    def _engine_done_token(self, eng):
        return ("S_" + eng, self.counts["S_" + eng])

    def _flush_pe_reads(self, val):
        for key in self.pe_pending_reads:
            self._rec(key)["r"].append(("S_pe", val))
        self.pe_pending_reads = []

    def _force_pe_inc(self):
        if not self.pe_pending_reads:
            return
        last = self._last_op["pe"]
        assert last is not None
        self.counts["S_pe"] += 1
        v = self.counts["S_pe"]
        last.incs.append(("S_pe", 1))
        self._flush_pe_reads(v)

    def op(self, eng, build, reads=(), writes=(), inc=True,
           pe_start=True, pe_stop=True, is_mm=False):
        o = Op(build)
        for ap in reads:
            assert _key(ap) not in self.pe_open or eng == "pe", _key(ap)
        for ap in reads:
            k = _key(ap)
            r = self._rec(k)
            if r["w"] is not None:
                sem, val = r["w"]
                if not (eng == "pe" and sem == "S_pe"):
                    self._wait(o, eng, sem, val)
        for ap in writes:
            k = _key(ap)
            if k in self.pe_pending_reads and eng != "pe":
                self._force_pe_inc()
            r = self._rec(k)
            if r["w"] is not None:
                sem, val = r["w"]
                if not (eng == "pe" and sem == "S_pe"):
                    self._wait(o, eng, sem, val)
            for sem, val in r["r"]:
                if not (eng == "pe" and sem == "S_pe"):
                    self._wait(o, eng, sem, val)
        if eng == "pe":
            for ap in reads:
                self.pe_pending_reads.append(_key(ap))
            do_inc = inc and pe_stop
            if do_inc:
                self.counts["S_pe"] += 1
                v = self.counts["S_pe"]
                o.incs.append(("S_pe", 1))
                self._flush_pe_reads(v)
                for ap in writes:
                    rr = self._rec(_key(ap))
                    rr["w"] = ("S_pe", v)
                    rr["r"] = []
            else:
                pass
        else:
            sem = "S_" + eng
            self.counts[sem] += 1
            v = self.counts[sem]
            o.incs.append((sem, 1))
            for ap in writes:
                rr = self._rec(_key(ap))
                rr["w"] = (sem, v)
                rr["r"] = []
            for ap in reads:
                self._rec(_key(ap))["r"].append((sem, v))
        self.ops[eng].append(o)
        self._last_op[eng] = o
        return o

    def dma(self, q, dst, src, waw_group=None, **kw):
        rot = self.qrot[q]
        slot = rot["i"] % _NROT
        rot["i"] += 1
        sem = f"D_{q}{slot}"
        o = Op(lambda e, dst=dst, src=src, kw=kw: e.dma_start(dst, src, **kw))
        if rot["cum"][slot] > 0:
            self._wait(o, q, sem, rot["cum"][slot])
        for ap in (src,):
            k = _key(ap)
            r = self._rec(k)
            if r["w"] is not None:
                self._wait(o, q, *r["w"])
        kd = _key(dst)
        if kd in self.pe_pending_reads:
            self._force_pe_inc()
        r = self._rec(kd)
        skip_waw = waw_group is not None and r.get("wg") == waw_group
        if not skip_waw:
            if r["w"] is not None:
                self._wait(o, q, *r["w"])
            for s_, v_ in r["r"]:
                self._wait(o, q, s_, v_)
        rot["cum"][slot] += 16
        self.counts[sem] = rot["cum"][slot]
        v = rot["cum"][slot]
        o.incs.append((sem, 16))
        rr = self._rec(kd)
        rr["w"] = (sem, v)
        rr["r"] = []
        rr["wg"] = waw_group
        self._rec(_key(src))["r"].append((sem, v))
        self.ops[q].append(o)
        self._last_op[q] = o
        return o

    def idma(self, q, dst, src, off, gather, waw_group=None):
        rot = self.qrot[q]
        slot = rot["i"] % _NROT
        rot["i"] += 1
        sem = f"D_{q}{slot}"

        def build(e, dst=dst, src=src, off=off, g=gather):
            ooa = IndirectOffsetOnAxis(off, 0)
            return e.indirect_dma_start(
                dst, None if g else ooa, src, ooa if g else None,
                oob_is_err=False)

        o = Op(build)
        cap_back = 48 if waw_group is not None else 0
        if rot["cum"][slot] - cap_back > 0:
            self._wait(o, q, sem, rot["cum"][slot] - cap_back)
        for ap in (src, off):
            r = self._rec(_key(ap))
            if r["w"] is not None:
                self._wait(o, q, *r["w"])
        kd = _key(dst)
        if kd in self.pe_pending_reads:
            self._force_pe_inc()
        r = self._rec(kd)
        skip_waw = waw_group is not None and r.get("wg") == waw_group
        if not skip_waw:
            if r["w"] is not None:
                self._wait(o, q, *r["w"])
            for s_, v_ in r["r"]:
                self._wait(o, q, s_, v_)
        rot["cum"][slot] += 16
        self.counts[sem] = rot["cum"][slot]
        v = rot["cum"][slot]
        o.incs.append((sem, 16))
        rr = self._rec(kd)
        rr["w"] = (sem, v)
        rr["r"] = []
        rr["wg"] = waw_group
        self._rec(_key(src))["r"].append((sem, v))
        self._rec(_key(off))["r"].append((sem, v))
        self.ops[q].append(o)
        self._last_op[q] = o
        return o

    def queue_fence(self, q, keys):
        o = Op(lambda e: e.nop())
        rot = self.qrot[q]
        wm = self.marks[q]
        for i in range(_NROT):
            sem = f"D_{q}{i}"
            if rot["cum"][i] > 0 and wm.get(sem, 0) < rot["cum"][i]:
                o.waits.append((sem, rot["cum"][i]))
                wm[sem] = rot["cum"][i]
        self._sem("F_" + q)
        self.counts["F_" + q] += 1
        v = self.counts["F_" + q]
        o.incs.append(("F_" + q, 1))
        for k in keys:
            rr = self._rec(k)
            rr["w"] = ("F_" + q, v)
            rr["r"] = []
            rr["wg"] = None
        self.ops[q].append(o)
        self._last_op[q] = o
        return o

    def mm(self, out, lhsT, rhs, start=True, stop=True, **kw):
        k = _key(out)
        if start:
            assert k not in self.pe_open, k
        else:
            assert k in self.pe_open, k
        if stop:
            self.pe_open.discard(k)
        else:
            self.pe_open.add(k)
        return self.op(
            "pe",
            lambda e, o=out, l=lhsT, r=rhs, s=start, t=stop, kw=kw:
                e.matmul(o, l, r, start=s, stop=t, **kw),
            reads=[lhsT, rhs], writes=[out], pe_stop=stop)

    def trans(self, out, in_, ident, **kw):
        assert _key(out) not in self.pe_open
        return self.op(
            "pe",
            lambda e, o=out, i=in_, d=ident, kw=kw: e.transpose(o, i, d, **kw),
            reads=[in_, ident], writes=[out])

    def act(self, out, in_, func, bias=0.0, scale=1.0, accum_out=None):
        reads = [in_]
        writes = [out]
        if isinstance(bias, bass.AP):
            reads.append(bias)
        if isinstance(scale, bass.AP):
            reads.append(scale)
        if accum_out is not None:
            writes.append(accum_out)
        return self.op(
            "act",
            lambda e, o=out, i=in_, f=func, b=bias, s=scale, a=accum_out:
                e.activation(o, i, f, bias=b, scale=s, accum_out=a),
            reads=reads, writes=writes)

    def ee(self, eng, meth, out, *ins, extra_reads=(), **kw):
        reads = [a for a in ins if isinstance(a, bass.AP)] + list(extra_reads)
        for v in kw.values():
            if isinstance(v, bass.AP):
                reads.append(v)
        return self.op(
            eng,
            lambda e, m=meth, o=out, i=ins, kw=kw: getattr(e, m)(o, *i, **kw),
            reads=reads, writes=[out])

    def barrier(self):
        self._force_pe_inc()
        targets = [(s, v) for s, v in sorted(self.counts.items()) if v > 0]
        for e in ENGINES:
            o = Op(lambda en: en.nop())
            wm = self.marks[e]
            for s_, v_ in targets:
                if wm.get(s_, 0) < v_:
                    o.waits.append((s_, v_))
                    wm[s_] = v_
            self.ops[e].append(o)
            self._last_op[e] = o
        for rec in self.mem.values():
            rec["r"] = []
            rec["wg"] = None

    def finalize(self):
        o = Op(lambda e: e.nop())
        for q in _QUEUES:
            rot = self.qrot[q]
            for i in range(_NROT):
                sem = f"D_{q}{i}"
                if rot["cum"][i] > 0:
                    o.waits.append((sem, rot["cum"][i]))
        for e in ("pe", "dve", "act", "pool"):
            sem = "S_" + e
            if self.counts[sem] > 0:
                o.waits.append((sem, self.counts[sem]))
        self.ops["sp"].append(o)

    def emit(self):
        nc = self.nc
        with contextlib.ExitStack() as es:
            handles = {}
            for name in sorted(self.sems):
                handles[name] = es.enter_context(nc.semaphore(name))
            blk = es.enter_context(nc.Block())

            def emit_engine(engobj, eng_name):
                for o in self.ops[eng_name]:
                    for sem, val in o.waits:
                        engobj.wait_ge(handles[sem], val)
                    inst = o.build(engobj)
                    for sem, n in o.incs:
                        inst.then_inc(handles[sem], n)

            blk.tensor(lambda e: emit_engine(e, "pe"))
            blk.vector(lambda e: emit_engine(e, "dve"))
            blk.scalar(lambda e: emit_engine(e, "act"))
            blk.gpsimd(lambda e: emit_engine(e, "pool"))
            blk.sync(lambda e: emit_engine(e, "sp"))



class Ctx:

    def __init__(self, tr, es):
        nc = tr.nc
        self.nc = nc
        self._bn = 0
        self.shared = {}
        self.ident_bf = es.enter_context(nc.sbuf_tensor("identbf", [128, 128], BF16))
        self.ident_f = es.enter_context(nc.sbuf_tensor("identf", [128, 128], F32))
        self.mask = es.enter_context(nc.sbuf_tensor("maskdg", [128, 128], F32))
        self.ones = es.enter_context(nc.sbuf_tensor("onesr", [1, 128], F32))
        self.eps = es.enter_context(nc.sbuf_tensor("epsc", [128, 1], F32))
        t = tr
        t.ee("pool", "memset", self.eps[:], 1e-5)
        t.ee("pool", "memset", self.ones[:], 1.0)
        t.ee("pool", "memset", self.ident_bf[:], 0.0)
        t.op("pool", lambda e, a=self.ident_bf: e.affine_select(
            out=a[:], in_=a[:], compare_op=ALU.not_equal, fill=1.0, base=0,
            pattern=[[-1, 128]], channel_multiplier=1),
            reads=[self.ident_bf[:]], writes=[self.ident_bf[:]])
        t.ee("pool", "memset", self.ident_f[:], 0.0)
        t.op("pool", lambda e, a=self.ident_f: e.affine_select(
            out=a[:], in_=a[:], compare_op=ALU.not_equal, fill=1.0, base=0,
            pattern=[[-1, 128]], channel_multiplier=1),
            reads=[self.ident_f[:]], writes=[self.ident_f[:]])
        t.ee("pool", "memset", self.mask[:], 0.0)
        t.op("pool", lambda e, a=self.mask: e.affine_select(
            out=a[:], in_=a[:], compare_op=ALU.is_ge, fill=NEG, base=0,
            pattern=[[-1, 128]], channel_multiplier=1),
            reads=[self.mask[:]], writes=[self.mask[:]])
        self.lstrict = es.enter_context(nc.sbuf_tensor("lstrict", [128, 128], BF16))
        t.ee("pool", "memset", self.lstrict[:], 0.0)
        t.op("pool", lambda e, a=self.lstrict: e.affine_select(
            out=a[:], in_=a[:], compare_op=ALU.is_ge, fill=1.0, base=0,
            pattern=[[-1, 128]], channel_multiplier=1),
            reads=[self.lstrict[:]], writes=[self.lstrict[:]])
        self.onescol_bf = es.enter_context(nc.sbuf_tensor("onescbf", [128, 1], BF16))
        t.ee("pool", "memset", self.onescol_bf[:], 1.0)

    def banks(self, es, n=8, width=512):
        self._bn += 1
        return [es.enter_context(self.nc.psum_tensor(f"pb{self._bn}_{i}", [128, width], F32))
                for i in range(n)]


def _iota(tr, ap, base, chmul, pat):
    tr.op("pool", lambda e, a=ap, b=int(base), c=chmul, p=pat: e.iota(
        a, pattern=p, base=b, channel_multiplier=c,
        allow_small_or_imprecise_dtypes=True),
        reads=[], writes=[ap])


def _alloc(es, nc, name, shape, dtype):
    return es.enter_context(nc.sbuf_tensor(name, shape, dtype))


def _memset(tr, ap, val, eng="pool"):
    tr.op(eng, lambda e, a=ap, v=val: e.memset(a, v), reads=[], writes=[ap])



def ph_ln(tr, es, C, T, src, dst, pref):
    nc = tr.nc
    nt = S // 128
    xt = [_alloc(es, nc, f"{pref}xt{i}", [128, 1024], F32) for i in range(2)]
    xnb = [_alloc(es, nc, f"{pref}xnb{i}", [128, 1024], BF16) for i in range(2)]
    sq = [_alloc(es, nc, f"{pref}sq{i}", [128, 1024], F32) for i in range(2)]
    sm = [_alloc(es, nc, f"{pref}sm{i}", [128, 8], F32) for i in range(2)]
    for t in range(nt):
        b = t % 2
        negm = sm[b][:, 0:1]
        ssq = sm[b][:, 1:2]
        var = sm[b][:, 2:3]
        std = sm[b][:, 3:4]
        rstd = sm[b][:, 4:5]
        nb = sm[b][:, 5:6]
        tr.dma("sp", xt[b][:], T[src][t * 128:(t + 1) * 128, :])
        tr.ee("dve", "reduce_sum", negm, xt[b][:], axis=AX, negate=True)
        tr.ee("dve", "tensor_scalar", negm, negm, 1.0 / 1024, None, op0=ALU.mult)
        tr.act(sq[b][:], xt[b][:], AF.Square, bias=negm, accum_out=ssq)
        tr.act(std, ssq, AF.Sqrt, bias=C.eps[:, 0:1], scale=1.0 / 1024)
        tr.ee("dve", "reciprocal", rstd, std)
        tr.ee("dve", "tensor_mul", nb, negm, rstd)
        tr.ee("dve", "tensor_scalar", xnb[b][:], xt[b][:], rstd, nb,
              op0=ALU.mult, op1=ALU.add)
        tr.dma("sp", T[dst][t * 128:(t + 1) * 128, :], xnb[b][:],
               waw_group="lnst")


def ph_quant(tr, es, C, T):
    nc = tr.nc
    pb = C.banks(es, 8)
    pbb = [p.bitcast(BF16) for p in pb]
    wt = [_alloc(es, nc, f"qwt{i}", [128, 4096], F32) for i in range(2)]
    sgn = _alloc(es, nc, "qsgn", [128, 4096], BF16)
    absw = _alloc(es, nc, "qabs", [128, 4096], F32)
    mbit = _alloc(es, nc, "qmb", [128, 4096], BF16)
    wq = _alloc(es, nc, "qwq", [128, 4096], BF16)
    colacc = _alloc(es, nc, "qcol", [128, 1], F32)
    tmp1 = _alloc(es, nc, "qtmp", [128, 1], F32)
    oncol = _alloc(es, nc, "qonc", [128, 1], F32)
    thr = _alloc(es, nc, "qthr", [1, 1], F32)
    cbc = _alloc(es, nc, "qcbc", [128, 1], F32)
    wqT = _alloc(es, nc, "qwqT", [128, 8 * 4096], BF16)

    for (wname, oname, ntile, width) in (("W1", "wq1T", 32, 1024),
                                         ("W2", "wq2T", 8, 4096)):
        numel = ntile * 128 * width
        _memset(tr, colacc[:], 0.0, "dve")
        _memset(tr, oncol[:], 1.0, "dve")
        for t in range(ntile):
            b = t % 2
            tr.dma("sp", wt[b][:, 0:width], T[wname][t * 128:(t + 1) * 128, :])
            tr.ee("dve", "reduce_sum", tmp1[:], wt[b][:, 0:width], axis=AX,
                  apply_absolute_value=True)
            tr.ee("dve", "tensor_add", colacc[:], colacc[:], tmp1[:])
        tr.mm(pb[0][0:1, 0:1], colacc[:], oncol[:])
        tr.ee("dve", "tensor_scalar", thr[:], pb[0][0:1, 0:1], 0.5 / numel,
              0.5e-5, op0=ALU.mult, op1=ALU.add)
        tr.mm(pb[1][:, 0:1], C.ones[0:1, :], thr[:])
        tr.ee("dve", "tensor_copy", cbc[:], pb[1][:, 0:1])
        nchunk = width // 128
        for t in range(ntile):
            b = t % 2
            tr.dma("sp", wt[b][:, 0:width], T[wname][t * 128:(t + 1) * 128, :])
            tr.op("act", lambda e, o=sgn[:, 0:width], i=wt[b][:, 0:width]:
                  e.sign(o, i), reads=[wt[b][:, 0:width]], writes=[sgn[:, 0:width]])
            tr.act(absw[:, 0:width], wt[b][:, 0:width], AF.Abs)
            tr.ee("dve", "tensor_scalar", mbit[:, 0:width], absw[:, 0:width],
                  cbc[:, 0:1], None, op0=ALU.is_gt)
            tr.ee("dve", "tensor_mul", wq[:, 0:width], mbit[:, 0:width],
                  sgn[:, 0:width])
            for c0 in range(0, nchunk, 4):
                cg = min(4, nchunk - c0)
                for q in range(cg):
                    tr.trans(pbb[2][:, q * 128:(q + 1) * 128],
                             wq[:, (c0 + q) * 128:(c0 + q + 1) * 128],
                             C.ident_bf[:])
                dst = wqT[:].rearrange("p (c n) -> p c n", c=8 if wname == "W1" else 32)[
                    :, c0:c0 + cg, t * 128:t * 128 + 128]
                src = pbb[2][:].rearrange("p (q n) -> p q n", q=8)[:, 0:cg, 0:128]
                tr.ee("dve", "tensor_copy", dst, src)
        for c in range(nchunk):
            tr.dma("sp", T[oname][c * 128:(c + 1) * 128, :],
                   wqT[:, c * (32768 // nchunk):(c * (32768 // nchunk)) + (32768 // nchunk)],
                   waw_group=oname + "g")


def _load_xT(tr, dst_sb, src_dram, scol0, swidth):
    for c in range(8):
        tr.dma("sp", dst_sb[:, c * swidth:(c + 1) * swidth],
               src_dram[scol0:scol0 + swidth, c * 128:(c + 1) * 128],
               transpose=True)


def _load_nat(tr, dst_sb, src_dram, width=1024):
    nt = S // 128
    for g in range(4):
        t0, t1 = g * nt // 4, (g + 1) * nt // 4
        dst = dst_sb[:].rearrange("p (t d) -> p t d", d=width)[:, t0:t1, :]
        src = src_dram.rearrange("(t p) d -> p t d", p=128)[:, t0:t1, :]
        tr.dma("sp", dst, src)


def ph_attn0(tr, es, C, T):
    nc = tr.nc
    pb = C.banks(es, 8)
    pbb = [p.bitcast(BF16) for p in pb]
    nt = S // 128
    xnat = _alloc(es, nc, "a0nat", [128, nt * 1024], BF16)
    xT = _alloc(es, nc, "a0xT", [128, 8 * S], BF16)
    Rsb = _alloc(es, nc, "a0R", [128, 8 * 1024], BF16)
    encR = _alloc(es, nc, "a0enc", [128, 8 * 512], BF16)
    Sr0 = _alloc(es, nc, "a0S", [128, S], F32)
    Srow = [Sr0, Sr0]
    Prow = [_alloc(es, nc, f"a0P{i}", [128, S], BF16) for i in range(2)]
    PT0 = _alloc(es, nc, "a0PT", [128, S], BF16)
    PT = [PT0, PT0]
    Zs0 = _alloc(es, nc, "a0Z", [128, 1024], F32)
    Zsb = [Zs0, Zs0]
    sm = [_alloc(es, nc, f"a0sm{i}", [128, 8], F32) for i in range(2)]
    _load_nat(tr, xnat, T["xn0n"])
    _load_xT(tr, xT, T["xn0n"], 0, S)
    tr.dma("pool", Rsb[:].rearrange("p (c e) -> p c e", c=8),
           T["R"].rearrange("(c p) e -> p c e", p=128))

    def enc_part(qg):
        for co in range(8):
            for ci in range(8):
                tr.mm(pb[6][:, 0:512],
                      Rsb[:, ci * 1024 + co * 128: ci * 1024 + co * 128 + 128],
                      xT[:, ci * S + qg * 512: ci * S + qg * 512 + 512],
                      start=(ci == 0), stop=(ci == 7))
            tr.ee("dve", "tensor_copy", encR[:, co * 512:(co + 1) * 512],
                  pb[6][:, 0:512])

    def s_part(q):
        b = q % 2
        qt = q % 4
        klen = 128 * (q + 1)
        ng = (klen + 511) // 512
        for kg in range(ng):
            w = min(512, klen - kg * 512)
            bank = pb[kg % 3]
            for ci in range(8):
                tr.mm(bank[:, 0:w],
                      encR[:, ci * 512 + qt * 128: ci * 512 + qt * 128 + 128],
                      xT[:, ci * S + kg * 512: ci * S + kg * 512 + w],
                      start=(ci == 0), stop=(ci == 7))
            if kg == ng - 1:
                if w > 128:
                    tr.ee("dve", "tensor_copy",
                          Srow[b][:, kg * 512: kg * 512 + w - 128],
                          bank[:, 0:w - 128])
                tr.ee("dve", "tensor_add",
                      Srow[b][:, kg * 512 + w - 128: kg * 512 + w],
                      bank[:, w - 128:w], C.mask[:])
            else:
                tr.ee("dve", "tensor_copy", Srow[b][:, kg * 512: kg * 512 + w],
                      bank[:, 0:w])
        negmax = sm[b][:, 0:1]
        sumexp = sm[b][:, 1:2]
        tr.ee("dve", "reduce_max", negmax, Srow[b][:, 0:klen], axis=AX, negate=True)
        tr.act(Prow[b][:, 0:klen], Srow[b][:, 0:klen], AF.Exp, bias=negmax,
               accum_out=sumexp)
        tr.ee("dve", "reciprocal", sm[b][:, 2:3], sumexp)

    def pv_part(q):
        b = q % 2
        klen = 128 * (q + 1)
        nk = klen // 128
        for j0 in range(0, nk, 4):
            jg = min(4, nk - j0)
            for q_ in range(jg):
                tr.trans(pbb[7][:, q_ * 128:(q_ + 1) * 128],
                         Prow[b][:, (j0 + q_) * 128:(j0 + q_ + 1) * 128],
                         C.ident_bf[:])
            tr.ee("dve", "tensor_copy", PT[b][:, j0 * 128: j0 * 128 + jg * 128],
                  pbb[7][:, 0:jg * 128])
        for j in range(nk):
            for h in range(2):
                tr.mm(pb[4 + h][:, 0:512],
                      PT[b][:, j * 128:(j + 1) * 128],
                      xnat[:, j * 1024 + h * 512: j * 1024 + h * 512 + 512],
                      start=(j == 0), stop=(j == nk - 1))
        zb = Zsb[b]
        for h in range(2):
            tr.ee("dve", "tensor_scalar", zb[:, h * 512:(h + 1) * 512],
                  pb[4 + h][:, 0:512], sm[b][:, 2:3], None, op0=ALU.mult)
        tr.dma("sp", T["Z0"][q * 128:(q + 1) * 128, :], zb[:], waw_group="z0g")

    enc_part(0)
    s_part(0)
    for q in range(nt):
        if q + 1 < nt:
            if (q + 1) % 4 == 0:
                enc_part((q + 1) // 4)
            s_part(q + 1)
        pv_part(q)


def ph_ffn0(tr, es, C, T):
    nc = tr.nc
    pb = C.banks(es, 8)
    pbb = [p.bitcast(BF16) for p in pb]
    xT = _alloc(es, nc, "f0xT", [128, 8 * S], BF16)
    w1T = _alloc(es, nc, "f0w1", [128, 8 * 4096], BF16)
    w2b = [_alloc(es, nc, f"f0w2{i}", [128, 1024], BF16) for i in range(4)]
    hsp = [_alloc(es, nc, f"f0h{i}", [128, 8 * 512], BF16) for i in range(4)]
    b1s = _alloc(es, nc, "f0b1s", [32, 128], F32)
    b1T = _alloc(es, nc, "f0b1T", [128, 32], F32)
    b2r = _alloc(es, nc, "f0b2r", [1, 1024], F32)
    b2bc = _alloc(es, nc, "f0b2bc", [128, 1024], F32)
    xt0 = _alloc(es, nc, "f0x", [128, 1024], F32)
    xt = [xt0, xt0]
    zt0 = _alloc(es, nc, "f0z", [128, 1024], F32)
    zt = [zt0, zt0]
    jt = [_alloc(es, nc, f"f0j{i}", [128, 1024], F32) for i in range(2)]
    lnm = [_alloc(es, nc, f"f0lm{i}", [128, 8], F32) for i in range(2)]
    lnsq = [_alloc(es, nc, f"f0lsq{i}", [128, 1024], F32) for i in range(2)]
    lnxn = [_alloc(es, nc, f"f0lxn{i}", [128, 1024], BF16) for i in range(2)]
    _load_xT(tr, xT, T["xn0n"], 0, S)
    for c in range(8):
        tr.dma("sp", w1T[:, c * 4096:(c + 1) * 4096],
               T["wq1T"][c * 128:(c + 1) * 128, :])
    tr.dma("sp", b1s[:], T["b1f"].rearrange("(a p) -> a p", p=128))
    tr.trans(pb[6][:, 0:32], b1s[:], C.ident_f[0:32, 0:32])
    tr.ee("dve", "tensor_copy", b1T[:], pb[6][:, 0:32])
    tr.dma("sp", b2r[:], T["b2f"][None, :])
    for eh in range(2):
        tr.mm(pb[7][:, 0:512], C.ones[0:1, :], b2r[0:1, eh * 512:(eh + 1) * 512])
        tr.ee("dve", "tensor_copy", b2bc[:, eh * 512:(eh + 1) * 512],
              pb[7][:, 0:512])
    for sg in range(S // 512):
        for fs in range(32):
            bank = pb[fs % 2]
            for ci in range(8):
                tr.mm(bank[:, 0:512],
                      w1T[:, ci * 4096 + fs * 128: ci * 4096 + fs * 128 + 128],
                      xT[:, ci * S + sg * 512: ci * S + sg * 512 + 512],
                      start=(ci == 0), stop=(ci == 7))
            tr.act(hsp[fs // 8][:, (fs % 8) * 512:(fs % 8 + 1) * 512],
                   bank[:, 0:512], AF.Relu, bias=b1T[:, fs:fs + 1])
        for cc in range(32):
            tr.dma("act", w2b[cc % 4][:], T["wq2T"][cc * 128:(cc + 1) * 128, :])
            for stl in range(4):
                for eh in range(2):
                    tr.mm(pb[stl * 2 + eh][:, 0:512],
                          hsp[cc // 8][:, (cc % 8) * 512 + stl * 128: (cc % 8) * 512 + stl * 128 + 128],
                          w2b[cc % 4][:, eh * 512:(eh + 1) * 512],
                          start=(cc == 0), stop=(cc == 31))
        for stl in range(4):
            t_glob = sg * 4 + stl
            b = stl % 2
            tr.dma("sp", xt[b][:], T["x"][t_glob * 128:(t_glob + 1) * 128, :])
            tr.dma("sp", zt[b][:], T["Z0"][t_glob * 128:(t_glob + 1) * 128, :])
            for eh in range(2):
                o = jt[b][:, eh * 512:(eh + 1) * 512]
                tr.ee("dve", "tensor_add", o, pb[stl * 2 + eh][:, 0:512],
                      zt[b][:, eh * 512:(eh + 1) * 512])
                tr.ee("dve", "tensor_add", o, o, xt[b][:, eh * 512:(eh + 1) * 512])
                tr.ee("dve", "tensor_add", o, o, b2bc[:, eh * 512:(eh + 1) * 512])
            negm = lnm[b][:, 0:1]
            ssq = lnm[b][:, 1:2]
            std = lnm[b][:, 3:4]
            rstd = lnm[b][:, 4:5]
            nb = lnm[b][:, 5:6]
            tr.ee("dve", "reduce_sum", negm, jt[b][:], axis=AX, negate=True)
            tr.ee("dve", "tensor_scalar", negm, negm, 1.0 / 1024, None, op0=ALU.mult)
            tr.act(lnsq[b][:], jt[b][:], AF.Square, bias=negm, accum_out=ssq)
            tr.act(std, ssq, AF.Sqrt, bias=C.eps[:, 0:1], scale=1.0 / 1024)
            tr.ee("dve", "reciprocal", rstd, std)
            tr.ee("dve", "tensor_mul", nb, negm, rstd)
            tr.ee("dve", "tensor_scalar", lnxn[b][:], jt[b][:], rstd, nb,
                  op0=ALU.mult, op1=ALU.add)
            tr.dma("sp", T["x1"][t_glob * 128:(t_glob + 1) * 128, :], jt[b][:],
                   waw_group="x1g")
            tr.dma("sp", T["xn1n"][t_glob * 128:(t_glob + 1) * 128, :], lnxn[b][:],
                   waw_group="xn1g")


def ph_qkvr(tr, es, C, T):
    nc = tr.nc
    pb = C.banks(es, 8)
    C.shared["pb"] = pb
    pbb = [p.bitcast(BF16) for p in pb]
    nt = S // 128
    xT = _alloc(es, nc, "qxT", [128, 8 * S], BF16)
    Wsb = {w: _alloc(es, nc, f"qW{w}", [128, 8 * 1024], BF16) for w in ("WQ", "WK", "WV")}
    bq8 = _alloc(es, nc, "qbq8", [128, 8], F32)
    bqs = _alloc(es, nc, "qbqs", [8, 128], F32)
    ktile = [_alloc(es, nc, f"qkt{i}", [128, 512], BF16) for i in range(4)]
    sqt = [_alloc(es, nc, f"qsqt{i}", [128, 512], F32) for i in range(2)]
    sel = _alloc(es, nc, "qsel", [128, 2], F32)
    kn_g = _alloc(es, nc, "qkng", [1, 128], F32)
    qn_g = _alloc(es, nc, "qqng", [1, 128], F32)
    hrow = _alloc(es, nc, "qhrow", [1, 64], F32)
    vpv_t = [_alloc(es, nc, f"qvpv{i}", [128, 16 * 65], BF16) for i in range(2)]
    biasM = _alloc(es, nc, "qbM", [128, 16], F32)
    _load_xT(tr, xT, T["xn1n"], 0, S)
    rwsb = _alloc(es, nc, "qkrw", [128, 8 * 16], BF16)
    rbsb = _alloc(es, nc, "qkrb", [16, 1], F32)
    lgT_sb = _alloc(es, nc, "qklg", [16, 512], F32)
    tr.dma("pool", rwsb[:].rearrange("p (c e) -> p c e", c=8),
           T["rw"].rearrange("(c p) e -> p c e", p=128))
    tr.dma("sp", rbsb[:], T["rb"][:, None])
    for sg in range(S // 512):
        bank = pb[6 + sg % 2]
        for ci in range(8):
            tr.mm(bank[0:16, 0:512], rwsb[:, ci * 16:(ci + 1) * 16],
                  xT[:, ci * S + sg * 512: ci * S + sg * 512 + 512],
                  start=(ci == 0), stop=(ci == 7))
        tr.ee("dve", "tensor_scalar", lgT_sb[0:16, 0:512],
              bank[0:16, 0:512], rbsb[:, 0:1], None, op0=ALU.add)
        tr.dma("sp", T["lgT"][0:16, sg * 512:(sg + 1) * 512], lgT_sb[:],
               waw_group="lgT")
    tr.queue_fence("sp", ["lgT"])
    for w in ("WQ", "WK", "WV"):
        tr.dma("pool", Wsb[w][:].rearrange("p (c e) -> p c e", c=8),
               T[w].rearrange("(c p) e -> p c e", p=128))
    tr.dma("sp", bqs[:], T["bQ"].rearrange("(a p) -> a p", p=128))
    tr.trans(pb[6][:, 0:8], bqs[:], C.ident_f[0:8, 0:8])
    tr.ee("dve", "tensor_scalar", bq8[:], pb[6][:, 0:8], 0.125, None, op0=ALU.mult)
    bkT = _alloc(es, nc, "qbkT", [128, 8], F32)
    tr.dma("sp", bqs[:], T["bK"].rearrange("(a p) -> a p", p=128))
    tr.trans(pb[6][:, 0:8], bqs[:], C.ident_f[0:8, 0:8])
    tr.ee("dve", "tensor_copy", bkT[:], pb[6][:, 0:8])
    _memset(tr, sel[:], 0.0, "dve")
    _memset(tr, sel[0:64, 0:1], 1.0, "dve")
    _memset(tr, sel[64:128, 1:2], 1.0, "dve")
    _memset(tr, vpv_t[0][:], 1.0, "pool")
    _memset(tr, vpv_t[1][:], 1.0, "pool")
    for (w, dstn, scale, do_bias, ng) in (("WK", "kT", 1.0, False, kn_g),
                                          ("WQ", "qT8", 0.125, True, qn_g)):
        pending = None
        for co in range(8):
            for sg in range(S // 512):
                bank = pb[(co + sg) % 2]
                for ci in range(8):
                    tr.mm(bank[:, 0:512],
                          Wsb[w][:, ci * 1024 + co * 128: ci * 1024 + co * 128 + 128],
                          xT[:, ci * S + sg * 512: ci * S + sg * 512 + 512],
                          start=(ci == 0), stop=(ci == 7))
                kb = ktile[(co * (S // 512) + sg) % 4]
                if do_bias:
                    tr.ee("dve", "tensor_scalar", kb[:], bank[:, 0:512], scale,
                          bq8[:, co:co + 1], op0=ALU.mult, op1=ALU.add)
                else:
                    tr.ee("dve", "tensor_scalar", kb[:], bank[:, 0:512],
                          bkT[:, co:co + 1], None, op0=ALU.add)
                tr.dma("sp", T[dstn][co * 128:(co + 1) * 128, sg * 512:(sg + 1) * 512],
                       kb[:], waw_group=dstn + "g")
                sq_ = sqt[(co * (S // 512) + sg) % 2]
                tr.act(sq_[:], kb[:], AF.Square)
                if pending is not None:
                    pending()
                def _norm(co=co, sg=sg, sq_=sq_, ng=ng):
                    for hh in range(2):
                        tr.mm(pb[4 + hh][0:1, 0:512], sel[:, hh:hh + 1], sq_[:])
                        h = co * 2 + hh
                        nsl = S // 512
                        tr.ee("dve", "reduce_max",
                              ng[0:1, h * nsl + sg: h * nsl + sg + 1],
                              pb[4 + hh][0:1, 0:512], axis=AX)
                pending = _norm
        pending()
    qmax = _alloc(es, nc, "qqmax", [1, 16], F32)
    kmax = _alloc(es, nc, "qkmax", [1, 16], F32)
    prod = _alloc(es, nc, "qprod", [1, 16], F32)
    nsl = S // 512
    for h in range(16):
        tr.ee("dve", "reduce_max", qmax[0:1, h:h + 1],
              qn_g[0:1, h * nsl:(h + 1) * nsl], axis=AX)
        tr.ee("dve", "reduce_max", kmax[0:1, h:h + 1],
              kn_g[0:1, h * nsl:(h + 1) * nsl], axis=AX)
    tr.ee("dve", "tensor_mul", prod[:], qmax[:], kmax[:])
    tr.act(prod[:], prod[:], AF.Sqrt)
    tr.ee("dve", "tensor_scalar", prod[:], prod[:], -1.0, None, op0=ALU.mult)
    tr.mm(pb[6][:, 0:16], C.ones[0:1, :], prod[:])
    tr.ee("dve", "tensor_copy", biasM[:], pb[6][:, 0:16])
    tr.dma("sp", T["biasM"], biasM[:])
    for st in range(nt):
        vt = vpv_t[st % 2]
        for dh in range(2):
            bank = pb[(st + dh) % 2]
            for ci in range(8):
                tr.mm(bank[:, 0:512],
                      xT[:, ci * S + st * 128: ci * S + st * 128 + 128],
                      Wsb["WV"][:, ci * 1024 + dh * 512: ci * 1024 + dh * 512 + 512],
                      start=(ci == 0), stop=(ci == 7))
            for hh in range(8):
                h = dh * 8 + hh
                tr.ee("dve", "tensor_copy", vt[:, h * 65: h * 65 + 64],
                      bank[:, hh * 64:(hh + 1) * 64])
        tr.dma("sp", T["vpv"][:, st * 16 * 65:(st + 1) * 16 * 65], vt[:],
               waw_group="vpvg")
    wof = _alloc(es, nc, "qwof", [64, 1024], F32)
    bv16 = _alloc(es, nc, "qbv16", [16, 64], F32)
    bv64 = _alloc(es, nc, "qbv64", [64, 16], F32)
    bvs = _alloc(es, nc, "qbvs", [64, 1], F32)
    bvwo = _alloc(es, nc, "qbvwo", [1, 1024], F32)
    bor = _alloc(es, nc, "qbor", [1, 1024], F32)
    jbr = _alloc(es, nc, "qjbr", [1, 1024], F32)
    tr.dma("sp", wof[:], T["WO"])
    tr.dma("sp", bv16[:], T["bV"].rearrange("(h d) -> h d", d=64))
    tr.trans(pb[6][0:64, 0:16], bv16[:], C.ident_f[0:16, 0:16])
    tr.ee("dve", "tensor_copy", bv64[:], pb[6][0:64, 0:16])
    tr.ee("dve", "reduce_sum", bvs[:], bv64[:], axis=AX)
    for eh in range(2):
        tr.mm(pb[7][0:1, 0:512], bvs[:], wof[:, eh * 512:(eh + 1) * 512])
        tr.ee("dve", "tensor_copy", bvwo[0:1, eh * 512:(eh + 1) * 512],
              pb[7][0:1, 0:512])
    tr.dma("sp", bor[:], T["bO"][None, :])
    tr.ee("dve", "scalar_tensor_tensor", jbr[:], bor[:], 16.0, bvwo[:],
          op0=ALU.mult, op1=ALU.add)
    tr.dma("sp", T["jbrow"], jbr[:])


NBLK = 48
BS = 512
YROWS = NBLK * BS


def ph_router(tr, es, C, T):
    nc = tr.nc
    pb = C.shared.pop("pb", None) or C.banks(es, 8)
    nt = S // 128
    lgt2 = [_alloc(es, nc, f"qlgt{i}", [16, 128], F32) for i in range(2)]
    lnt = _alloc(es, nc, "qlnt", [128, 16], F32)
    lw = [_alloc(es, nc, f"qlw{i}", [128, 16], F32) for i in range(2)]
    eqall = _alloc(es, nc, "qeq", [128, nt * 64], F32)
    dport = _alloc(es, nc, "qd", [128, 16], F32)
    indb = _alloc(es, nc, "qind", [128, 16], F32)
    indb16 = _alloc(es, nc, "qindb", [128, 16], BF16)
    expb = _alloc(es, nc, "qexp", [128, 16], F32)
    peb = _alloc(es, nc, "qpe", [128, 16], F32)
    cechunk = _alloc(es, nc, "qcec", [128, 16], F32)
    ceT_sb = _alloc(es, nc, "qceT", [16, S], F32)
    smr = _alloc(es, nc, "qsmr", [128, 8], F32)
    wcum = _alloc(es, nc, "qwcum", [128, 16], F32)
    wsel = _alloc(es, nc, "qwsel", [128, nt * 4], F32)
    cesel = _alloc(es, nc, "qcesel", [128, nt * 4], F32)
    tmp16 = _alloc(es, nc, "qtmp16", [128, 16], F32)
    crow = _alloc(es, nc, "qcrow", [1, nt * 16], F32)
    hs = [_alloc(es, nc, f"qhs{i}", [1, nt * 16], F32) for i in range(2)]
    cumrow = _alloc(es, nc, "qcumrow", [1, nt * 16], F32)
    trow = _alloc(es, nc, "qtrow", [1, 16], F32)
    t16 = _alloc(es, nc, "qt16", [1, 16], F32)
    hsb = [_alloc(es, nc, f"qhsb{i}", [1, 16], F32) for i in range(2)]
    sbrow = _alloc(es, nc, "qsbrow", [1, 16], F32)
    brow48 = _alloc(es, nc, "qbrow48", [1, NBLK], F32)
    ebrow = _alloc(es, nc, "qebrow", [1, NBLK], F32)
    eb48t = _alloc(es, nc, "qeb48t", [1, NBLK], F32)
    iotac = _alloc(es, nc, "qiotac", [128, 1], F32)
    rowt = _alloc(es, nc, "qrowt", [1, 16], F32)
    brt = _alloc(es, nc, "qbrt", [128, 16], F32)
    dselcol = _alloc(es, nc, "qdselc", [128, nt * 4], F32)
    rec = [_alloc(es, nc, f"qrec{i}", [128, 2], F32) for i in range(4)]
    reci = [_alloc(es, nc, f"qreci{i}", [128, 1], I32) for i in range(4)]
    ztab = _alloc(es, nc, "qztab", [128, YROWS * 2 // 128], F32)
    b1x = _alloc(es, nc, "qb1x", [16, 128], F32)
    b1all = _alloc(es, nc, "qb1all", [128, 256], F32)

    _iota(tr, iotac[:], 0.0, 1, [[1, 1]])
    tr.ee("dve", "memset", ztab[:], 0.0)
    tr.dma("sp", T["tab"].rearrange("(p a) c -> p (a c)", p=128), ztab[:])
    for e in range(16):
        tr.dma("sp", b1x[:], T["b1e"][e].rearrange("(i p) -> i p", p=128))
        tr.trans(pb[6][:, 0:16], b1x[:], C.ident_f[0:16, 0:16])
        tr.ee("dve", "tensor_copy", b1all[:, e * 16:(e + 1) * 16], pb[6][:, 0:16])
    tr.dma("sp", T["b1scr"].rearrange("e p i -> p e i"),
           b1all[:].rearrange("p (e i) -> p e i", e=16))
    for st in range(nt):
        tr.dma("sp", lgt2[st % 2][:], T["lgT"][0:16, st * 128:(st + 1) * 128])
        tr.trans(pb[2][:, 0:16], lgt2[st % 2][:],
                 C.ident_f[0:16, 0:16])
        tr.ee("dve", "tensor_copy", lnt[:], pb[2][:, 0:16])
        mx = smr[:, 0:1]
        m1n = smr[:, 1:2]
        cur = lnt
        for it in range(4):
            tr.ee("dve", "reduce_max", mx, cur[:], axis=AX)
            if it == 0:
                tr.ee("dve", "tensor_scalar", m1n, mx, -1.0, None, op0=ALU.mult)
            eqs = eqall[:, st * 64 + it * 16: st * 64 + (it + 1) * 16]
            tr.ee("dve", "tensor_scalar", eqs, cur[:], mx, None, op0=ALU.is_equal)
            nxt = lw[it % 2]
            tr.ee("dve", "scalar_tensor_tensor", nxt[:], eqs, -1e30, cur[:],
                  op0=ALU.mult, op1=ALU.add)
            cur = nxt
        tr.ee("dve", "tensor_sub", dport[:], lnt[:], cur[:])
        tr.ee("dve", "tensor_scalar", indb[:], dport[:], 1e20, None, op0=ALU.is_gt)
        tr.act(expb[:], lnt[:], AF.Exp, bias=m1n)
        tr.ee("dve", "tensor_mul", peb[:], expb[:], indb[:])
        ssum = smr[:, 2:3]
        rr = smr[:, 3:4]
        tr.ee("dve", "reduce_sum", ssum, peb[:], axis=AX)
        tr.ee("dve", "reciprocal", rr, ssum)
        tr.ee("dve", "tensor_scalar", cechunk[:], peb[:], rr, None, op0=ALU.mult)
        tr.trans(pb[2][0:16, 0:128], cechunk[:], C.ident_f[:])
        tr.ee("dve", "tensor_copy", ceT_sb[0:16, st * 128:(st + 1) * 128],
              pb[2][0:16, 0:128])
        tr.ee("dve", "tensor_copy", indb16[:], indb[:])
        tr.mm(pb[3][:, 0:16], C.lstrict[:], indb16[:])
        tr.ee("dve", "tensor_copy", wcum[:], pb[3][:, 0:16])
        tr.mm(pb[4][0:1, 0:16], C.onescol_bf[:], indb16[:])
        tr.ee("dve", "tensor_copy", crow[0:1, st * 16:(st + 1) * 16],
              pb[4][0:1, 0:16])
        for k in range(4):
            eqs = eqall[:, st * 64 + k * 16: st * 64 + (k + 1) * 16]
            tr.ee("dve", "tensor_mul", tmp16[:], cechunk[:], eqs)
            tr.ee("dve", "reduce_sum", cesel[:, st * 4 + k: st * 4 + k + 1],
                  tmp16[:], axis=AX)
            tr.ee("dve", "tensor_mul", tmp16[:], wcum[:], eqs)
            tr.ee("dve", "reduce_sum", wsel[:, st * 4 + k: st * 4 + k + 1],
                  tmp16[:], axis=AX)
    tr.dma("sp", T["ceT"], ceT_sb[:])
    W = nt * 16
    srcrow = crow
    for i, sh in enumerate((16, 32, 64, 128, 256)):
        dst = hs[i % 2]
        tr.ee("dve", "tensor_copy", dst[0:1, 0:sh], srcrow[0:1, 0:sh])
        tr.ee("dve", "tensor_add", dst[0:1, sh:W], srcrow[0:1, sh:W],
              srcrow[0:1, 0:W - sh])
        srcrow = dst
    cuminc = srcrow
    tr.ee("dve", "tensor_sub", cumrow[:], cuminc[:], crow[:])
    nrow = cuminc[0:1, (nt - 1) * 16: nt * 16]
    tr.ee("dve", "memset", trow[:], 0.0)
    for j in range(8):
        tr.ee("dve", "tensor_scalar", t16[:], nrow, float(BS * j), None,
              op0=ALU.is_gt)
        tr.ee("dve", "tensor_add", trow[:], trow[:], t16[:])
    srcrow = trow
    for i, sh in enumerate((1, 2, 4, 8)):
        dst = hsb[i % 2]
        tr.ee("dve", "tensor_copy", dst[0:1, 0:sh], srcrow[0:1, 0:sh])
        tr.ee("dve", "tensor_add", dst[0:1, sh:16], srcrow[0:1, sh:16],
              srcrow[0:1, 0:16 - sh])
        srcrow = dst
    binc = srcrow
    tr.ee("dve", "tensor_sub", sbrow[:], binc[:], trow[:])
    tr.ee("dve", "tensor_scalar", sbrow[:], sbrow[:], float(BS), None,
          op0=ALU.mult)
    _iota(tr, brow48[:], 0.0, 0, [[1, NBLK]])
    tr.ee("dve", "memset", ebrow[:], 0.0)
    for e in range(16):
        tr.ee("dve", "tensor_scalar", eb48t[:], brow48[:], binc[0:1, e:e + 1],
              None, op0=ALU.is_ge)
        tr.ee("dve", "tensor_add", ebrow[:], ebrow[:], eb48t[:])
    tr.ee("dve", "tensor_scalar", ebrow[:], ebrow[:], 15.0, None, op0=ALU.min)
    tr.dma("sp", T["ebrow"], ebrow[:])
    for st in range(nt):
        tr.ee("dve", "tensor_add", rowt[:], cumrow[0:1, st * 16:(st + 1) * 16],
              sbrow[:])
        tr.mm(pb[5][:, 0:16], C.ones[0:1, :], rowt[:])
        tr.ee("dve", "tensor_copy", brt[:], pb[5][:, 0:16])
        for k in range(4):
            kb = k % 4
            eqs = eqall[:, st * 64 + k * 16: st * 64 + (k + 1) * 16]
            tr.ee("dve", "tensor_mul", tmp16[:], brt[:], eqs)
            dcol = dselcol[:, st * 4 + k: st * 4 + k + 1]
            tr.ee("dve", "reduce_sum", dcol, tmp16[:], axis=AX)
            tr.ee("dve", "tensor_add", dcol, dcol,
                  wsel[:, st * 4 + k: st * 4 + k + 1])
            tr.ee("dve", "tensor_scalar", rec[kb][:, 0:1], iotac[:],
                  float(128 * st), None, op0=ALU.add)
            tr.ee("dve", "tensor_copy", rec[kb][:, 1:2],
                  cesel[:, st * 4 + k: st * 4 + k + 1])
            tr.ee("dve", "tensor_copy", reci[kb][:], dcol)
            tr.idma("pool", T["tab"], rec[kb][:], reci[kb][:], gather=False,
                    waw_group="tab")
    tr.queue_fence("pool", ["tab"])
    tr.dma("sp", T["dsel"], dselcol[:])


def ph_wprep(tr, es, C, T):
    for e in range(16):
        tr.dma("pool", T["w1scr"][e].rearrange("p (c i) -> p c i", c=8),
               T["w1e"][e].rearrange("(c p) i -> p c i", p=128),
               waw_group="w1scr")
        tr.dma("pool", T["w2scr"][e].rearrange("p (c i) -> p c i", c=16),
               T["w2e"][e].rearrange("(c p) i -> p c i", p=128),
               waw_group="w2scr")
    tr.queue_fence("pool", ["w1scr", "w2scr"])


def ph_moeblk(tr, es, C, T):
    nc = tr.nc
    pb = C.banks(es, 8)
    pbb = [p.bitcast(BF16) for p in pb]
    w1sb = [_alloc(es, nc, f"kw1{i}", [128, 16384], BF16) for i in range(2)]
    w2sb = [_alloc(es, nc, f"kw2{i}", [128, 16384], BF16) for i in range(2)]
    b1sb = [_alloc(es, nc, f"kb1{i}", [128, 16], F32) for i in range(2)]
    tabt = [_alloc(es, nc, f"ktab{i}", [128, 8], F32) for i in range(2)]
    tki = [_alloc(es, nc, f"ktki{i}", [128, 4], I32) for i in range(2)]
    xtok = [_alloc(es, nc, f"kxt{i}", [128, 4096], BF16) for i in range(2)]
    xTb = _alloc(es, nc, "kxTb", [128, 4096], BF16)
    hsb = _alloc(es, nc, "khsb", [128, 8192], BF16)
    ybt = [_alloc(es, nc, f"kyb{i}", [128, 1024], F32) for i in range(2)]
    ebrow = _alloc(es, nc, "kebr", [1, NBLK], F32)
    ebbc = _alloc(es, nc, "kebbc", [128, NBLK], F32)
    widxf = _alloc(es, nc, "kwidf", [128, NBLK], F32)
    widxi = _alloc(es, nc, "kwidi", [128, NBLK], I32)
    iotac = _alloc(es, nc, "kiota", [128, 1], F32)
    w1flat = T["w1scr"].rearrange("e p n -> (e p) n")
    w2flat = T["w2scr"].rearrange("e p n -> (e p) n")
    b1flat = T["b1scr"].rearrange("e p i -> (e p) i")

    tr.dma("sp", ebrow[:], T["ebrow"])
    _iota(tr, iotac[:], 0.0, 1, [[1, 1]])
    for g in range(NBLK // 8):
        tr.mm(pb[7][:, 0:8], C.ones[0:1, :], ebrow[0:1, g * 8:(g + 1) * 8])
        tr.ee("dve", "tensor_copy", ebbc[:, g * 8:(g + 1) * 8], pb[7][:, 0:8])
    tr.ee("dve", "tensor_scalar", widxf[:], ebbc[:], 128.0, iotac[:, 0:1],
          op0=ALU.mult, op1=ALU.add)
    tr.ee("dve", "tensor_copy", widxi[:], widxf[:])

    def gathers(b):
        bb = b % 2
        tr.idma("pool", w1sb[bb][:], w1flat, widxi[:, b:b + 1], gather=True)
        tr.idma("pool", w2sb[bb][:], w2flat, widxi[:, b:b + 1], gather=True)
        tr.idma("pool", b1sb[bb][:], b1flat, widxi[:, b:b + 1], gather=True)
        tr.dma("sp", tabt[bb][:],
               T["tab"][b * BS:(b + 1) * BS].rearrange("(p x) c -> p (x c)",
                                                       p=128))
        tr.ee("dve", "tensor_copy",
              tki[bb][:].rearrange("p (x c) -> p x c", c=1),
              tabt[bb][:].rearrange("p (x c) -> p x c", c=2)[:, :, 0:1])
        for x in range(4):
            tr.idma("pool", xtok[bb][:, x * 1024:(x + 1) * 1024], T["xn1n"],
                    tki[bb][:, x:x + 1], gather=True)

    def compute(b):
        bb = b % 2
        for x in range(4):
            for cg in range(2):
                tb = pbb[6 + (x * 2 + cg) % 2]
                for q in range(4):
                    ci = cg * 4 + q
                    tr.trans(tb[:, q * 128:(q + 1) * 128],
                             xtok[bb][:, x * 1024 + ci * 128:
                                      x * 1024 + ci * 128 + 128],
                             C.ident_bf[:])
                dst = xTb[:].rearrange("p (c s) -> p c s", s=512)[
                    :, cg * 4:(cg + 1) * 4, x * 128:(x + 1) * 128]
                src = tb[:].rearrange("p (q s) -> p q s", s=128)[:, 0:4, :]
                tr.ee("dve", "tensor_copy", dst, src)
        for isl in range(16):
            bank = pb[isl % 2]
            for ci in range(8):
                tr.mm(bank[:, 0:512],
                      w1sb[bb][:, ci * 2048 + isl * 128:
                               ci * 2048 + isl * 128 + 128],
                      xTb[:, ci * 512:(ci + 1) * 512],
                      start=(ci == 0), stop=(ci == 7))
            tr.act(hsb[:, isl * 512:(isl + 1) * 512], bank[:, 0:512],
                   AF.Gelu, bias=b1sb[bb][:, isl:isl + 1])
        for x in range(4):
            for eh in range(2):
                bank = pb[2 + (x % 2) * 2 + eh]
                for cc in range(16):
                    tr.mm(bank[:, 0:512],
                          hsb[:, cc * 512 + x * 128: cc * 512 + x * 128 + 128],
                          w2sb[bb][:, cc * 1024 + eh * 512:
                                   cc * 1024 + eh * 512 + 512],
                          start=(cc == 0), stop=(cc == 15))
                tr.ee("dve", "tensor_scalar",
                      ybt[x % 2][:, eh * 512:(eh + 1) * 512], bank[:, 0:512],
                      tabt[bb][:, 2 * x + 1: 2 * x + 2], None, op0=ALU.mult)
            tr.dma("sp",
                   T["yplane"][b * BS:(b + 1) * BS]
                   .rearrange("(p x) d -> p x d", p=128)[:, x:x + 1, :],
                   ybt[x % 2][:].rearrange("p (x d) -> p x d", x=1),
                   waw_group="ypl")

    gathers(0)
    for b in range(NBLK):
        if b + 1 < NBLK:
            gathers(b + 1)
        compute(b)
    tr.queue_fence("sp", ["yplane"])


def ph_join(tr, es, C, T):
    nc = tr.nc
    pb = C.banks(es, 8)
    nt = S // 128
    dself = _alloc(es, nc, "jdsf", [128, nt * 4], F32)
    dseli = _alloc(es, nc, "jdsi", [128, nt * 4], I32)
    jbr = _alloc(es, nc, "jjbr", [1, 1024], F32)
    jbbc = _alloc(es, nc, "jjbbc", [128, 1024], F32)
    ceTsb = _alloc(es, nc, "jceT", [16, S], BF16)
    b2sb = _alloc(es, nc, "jb2", [16, 1024], BF16)
    ygt = [_alloc(es, nc, f"jyg{i}", [128, 4096], F32) for i in range(2)]
    x1t = [_alloc(es, nc, f"jx1{i}", [128, 1024], F32) for i in range(2)]
    att = [_alloc(es, nc, f"jat{i}", [128, 1024], F32) for i in range(2)]
    outt = [_alloc(es, nc, f"jot{i}", [128, 1024], F32) for i in range(2)]
    tr.dma("sp", dself[:], T["dsel"])
    tr.ee("dve", "tensor_copy", dseli[:], dself[:])
    tr.dma("sp", jbr[:], T["jbrow"])
    for eh in range(2):
        tr.mm(pb[6][:, 0:512], C.ones[0:1, :], jbr[0:1, eh * 512:(eh + 1) * 512])
        tr.ee("dve", "tensor_copy", jbbc[:, eh * 512:(eh + 1) * 512],
              pb[6][:, 0:512])
    tr.dma("pool", ceTsb[:], T["ceT"])
    tr.dma("pool", b2sb[:], T["b2e"])
    for st in range(nt):
        b = st % 2
        tr.dma("sp", x1t[b][:], T["x1"][st * 128:(st + 1) * 128, :])
        tr.dma("sp", att[b][:], T["attn1"][st * 128:(st + 1) * 128, :])
        for k in range(4):
            tr.idma("pool", ygt[b][:, k * 1024:(k + 1) * 1024], T["yplane"],
                    dseli[:, st * 4 + k: st * 4 + k + 1], gather=True)
        for eh in range(2):
            bank = pb[eh]
            tr.mm(bank[:, 0:512], ceTsb[0:16, st * 128:(st + 1) * 128],
                  b2sb[0:16, eh * 512:(eh + 1) * 512])
            o = outt[b][:, eh * 512:(eh + 1) * 512]
            tr.ee("dve", "tensor_add", o,
                  ygt[b][:, eh * 512: eh * 512 + 512],
                  ygt[b][:, 1024 + eh * 512: 1024 + eh * 512 + 512])
            tr.ee("dve", "tensor_add", o, o,
                  ygt[b][:, 2048 + eh * 512: 2048 + eh * 512 + 512])
            tr.ee("dve", "tensor_add", o, o,
                  ygt[b][:, 3072 + eh * 512: 3072 + eh * 512 + 512])
            tr.ee("dve", "tensor_add", o, o,
                  x1t[b][:, eh * 512:(eh + 1) * 512])
            tr.ee("dve", "tensor_add", o, o,
                  att[b][:, eh * 512:(eh + 1) * 512])
            tr.ee("dve", "tensor_add", o, o,
                  jbbc[:, eh * 512:(eh + 1) * 512])
            tr.ee("dve", "tensor_add", o, o, bank[:, 0:512])
        tr.dma("sp", T["out"][st * 128:(st + 1) * 128, :], outt[b][:],
               waw_group="outg")




def ph_attn1(tr, es, C, T):
    nc = tr.nc
    sp2 = [es.enter_context(nc.psum_tensor(f"t1sp{i}", [128, 1024], F32)) for i in range(2)]
    pv = [es.enter_context(nc.psum_tensor(f"t1pv{i}", [128, 512], F32)) for i in range(4)]
    nt = S // 128
    kT = _alloc(es, nc, "t1kT", [128, 8 * S], BF16)
    vpv = _alloc(es, nc, "t1v", [128, nt * 16 * 65], BF16)
    biasM = _alloc(es, nc, "t1bM", [128, 16], F32)
    biasMg = _alloc(es, nc, "t1bMg", [128, 1], F32)
    wobf = _alloc(es, nc, "t1wo_w", [64, 1024], BF16)
    Zacc = _alloc(es, nc, "t1Z", [64, S], F32)
    Zbf = _alloc(es, nc, "t1Zb", [64, 512], BF16)
    qsl = [_alloc(es, nc, f"t1q{i}", [128, 8 * 512], BF16) for i in range(2)]
    pt = [_alloc(es, nc, f"t1p{i}", [128, 1024], BF16) for i in range(2)]
    rp = _alloc(es, nc, "t1rp", [1, 512], F32)
    rtmp = _alloc(es, nc, "t1rt", [1, 512], F32)
    bcb = [_alloc(es, nc, f"t1bcs{i}", [64, 512], F32) for i in range(2)]
    ztmp = _alloc(es, nc, "t1zt", [64, 512], F32)
    att = [_alloc(es, nc, f"t1a{i}", [128, 1024], F32) for i in range(2)]
    for c in range(8):
        tr.dma("sp", kT[:, c * S:(c + 1) * S], T["kT"][c * 128:(c + 1) * 128, :])
    for g in range(4):
        w = nt * 16 * 65 // 4
        tr.dma("sp", vpv[:, g * w:(g + 1) * w], T["vpv"][:, g * w:(g + 1) * w])
    tr.dma("sp", biasM[:], T["biasM"])
    tr.op("dve", lambda e, o=biasMg[:], i=biasM[:]: e.tensor_reduce(
        o, i, AX, ALU.min), reads=[biasM[:]], writes=[biasMg[:]])
    tr.dma("pool", wobf[:], T["WO"])

    def s_step(qg, pr, j):
        sp = sp2[j % 2]
        qlo = max(0, 128 * j - 512 * qg)
        qb = qsl[qg % 2]
        c = pr
        for hh in range(2):
            lo = hh * 64
            tr.mm(sp[:, hh * 512 + qlo: hh * 512 + 512],
                  kT[lo:lo + 64, c * S + j * 128: c * S + j * 128 + 128],
                  qb[lo:lo + 64, c * 512 + qlo: c * 512 + 512],
                  start=True, stop=True)

    def pv_step(qg, pr, j, nk):
        sp = sp2[j % 2]
        ptb = pt[j % 2]
        qlo = max(0, 128 * j - 512 * qg)
        wd = 512 - qlo
        src3 = sp[:].rearrange("p (h n) -> p h n", h=2)[:, :, qlo:512]
        dst3 = ptb[:].rearrange("p (h n) -> p h n", h=2)[:, :, qlo:512]
        tr.act(dst3, src3, AF.Exp, bias=biasMg[:, 0:1])
        if j >= 4 * qg:
            for hh in range(2):
                a = ptb[:, hh * 512 + qlo: hh * 512 + 512]
                tr.op("pool",
                      lambda e, a=a, wd=wd: e.affine_select(
                          out=a, in_=a, compare_op=ALU.is_ge, fill=0.0,
                          base=0, pattern=[[1, wd]], channel_multiplier=-1),
                      reads=[a], writes=[a])
        for hh in range(2):
            h = 2 * pr + hh
            tr.mm(pv[2 * (pr % 2) + hh][0:65, qlo:512],
                  vpv[:, (j * 16 + h) * 65:(j * 16 + h) * 65 + 65],
                  ptb[:, hh * 512 + qlo: hh * 512 + 512],
                  start=(j == 0), stop=(j == nk - 1))

    def tail(qg, pr):
        for hh in range(2):
            h = 2 * pr + hh
            pvb = pv[2 * (pr % 2) + hh]
            bco = sp2[hh]
            tr.act(rtmp[:], pvb[64:65, 0:512], AF.Ln)
            tr.act(rp[:], rtmp[:], AF.Exp, scale=-1.0)
            tr.mm(bco[0:64, 0:512], C.ones[0:1, 0:64], rp[:])
            tr.ee("dve", "tensor_copy", bcb[hh][:], bco[0:64, 0:512])
            zsl = Zacc[:, qg * 512:(qg + 1) * 512]
            if h == 0:
                tr.ee("dve", "tensor_mul", zsl, pvb[0:64, 0:512], bcb[hh][:])
            else:
                tr.ee("dve", "tensor_mul", ztmp[:], pvb[0:64, 0:512], bcb[hh][:])
                tr.ee("dve", "tensor_add", zsl, zsl, ztmp[:])

    for qg in range(S // 512):
        qb = qsl[qg % 2]
        tr.dma("sp", qb[:].rearrange("p (c s) -> p c s", c=8),
               T["qT8"].rearrange("(c p) s -> p c s", p=128)[:, :, qg * 512:(qg + 1) * 512])
        nk = 4 * (qg + 1)
        for pr in range(8):
            s_step(qg, pr, 0)
            for j in range(1, nk):
                s_step(qg, pr, j)
                pv_step(qg, pr, j - 1, nk)
            pv_step(qg, pr, nk - 1, nk)
            if pr > 0:
                tail(qg, pr - 1)
        tail(qg, 7)
        tr.ee("dve", "tensor_copy", Zbf[:], Zacc[:, qg * 512:(qg + 1) * 512])
        for qt in range(4):
            ab = att[qt % 2]
            for eh in range(2):
                pwo = pv[qt % 2]
                tr.mm(pwo[:, 0:512], Zbf[:, qt * 128:(qt + 1) * 128],
                      wobf[:, eh * 512:(eh + 1) * 512])
                tr.ee("dve", "tensor_copy", ab[:, eh * 512:(eh + 1) * 512],
                      pwo[:, 0:512])
            q = qg * 4 + qt
            tr.dma("sp", T["attn1"][q * 128:(q + 1) * 128, :], ab[:],
                   waw_group="at1g")



INPUT_TENSORS = {
    "x": lambda: ([S, 1024], F32),
    "R": lambda: ([1024, 1024], F32),
    "W1": lambda: ([4096, 1024], F32),
    "b1f": lambda: ([4096], F32),
    "W2": lambda: ([1024, 4096], F32),
    "b2f": lambda: ([1024], F32),
    "WQ": lambda: ([1024, 1024], F32),
    "bQ": lambda: ([1024], F32),
    "bK": lambda: ([1024], F32),
    "WK": lambda: ([1024, 1024], F32),
    "WV": lambda: ([1024, 1024], F32),
    "WO": lambda: ([64, 1024], F32),
    "bO": lambda: ([1024], F32),
    "bV": lambda: ([1024], F32),
    "rw": lambda: ([1024, 16], F32),
    "rb": lambda: ([16], F32),
    "w1e": lambda: ([16, 1024, 2048], F32),
    "b1e": lambda: ([16, 2048], F32),
    "w2e": lambda: ([16, 2048, 1024], F32),
    "b2e": lambda: ([16, 1024], F32),
}
SCRATCH_TENSORS = {
    "xn0n": lambda: ([S, 1024], BF16),
    "Z0": lambda: ([S, 1024], F32),
    "wq1T": lambda: ([1024, 4096], BF16),
    "wq2T": lambda: ([4096, 1024], BF16),
    "x1": lambda: ([S, 1024], F32),
    "xn1n": lambda: ([S, 1024], BF16),
    "qT8": lambda: ([1024, S], BF16),
    "kT": lambda: ([1024, S], BF16),
    "vpv": lambda: ([128, (S // 128) * 16 * 65], BF16),
    "biasM": lambda: ([128, 16], F32),
    "jbrow": lambda: ([1, 1024], F32),
    "attn1": lambda: ([S, 1024], F32),
    "lgT": lambda: ([16, S], F32),
    "ceT": lambda: ([16, S], F32),
    "tab": lambda: ([YROWS, 2], F32),
    "dsel": lambda: ([128, (S // 128) * 4], F32),
    "ebrow": lambda: ([1, NBLK], F32),
    "b1scr": lambda: ([16, 128, 16], F32),
    "w1scr": lambda: ([16, 128, 16384], BF16),
    "w2scr": lambda: ([16, 128, 16384], BF16),
    "yplane": lambda: ([YROWS, 1024], F32),
    "out": lambda: ([S, 1024], F32),
}

PHASES = {
    "ln0": (lambda tr, es, C, T: ph_ln(tr, es, C, T, "x", "xn0n", "l0"),
            ["x"], ["xn0n"]),
    "quant": (ph_quant, ["W1", "W2"], ["wq1T", "wq2T"]),
    "attn0": (ph_attn0, ["xn0n", "R"], ["Z0"]),
    "ffn0": (ph_ffn0, ["xn0n", "wq1T", "wq2T", "b1f", "b2f", "x", "Z0"],
             ["x1", "xn1n"]),
    "qkvr": (ph_qkvr, ["xn1n", "WQ", "bQ", "WK", "bK", "WV", "WO", "bO", "bV",
                       "rw", "rb"],
             ["qT8", "kT", "vpv", "biasM", "jbrow", "lgT"]),
    "router": (ph_router, ["lgT", "b1e"],
               ["ceT", "tab", "dsel", "ebrow", "b1scr"]),
    "wprep": (ph_wprep, ["w1e", "w2e"], ["w1scr", "w2scr"]),
    "attn1": (ph_attn1, ["qT8", "kT", "vpv", "biasM", "WO"], ["attn1"]),
    "moeblk": (ph_moeblk, ["tab", "ebrow", "w1scr", "w2scr", "b1scr", "xn1n"],
               ["yplane"]),
    "join": (ph_join, ["yplane", "dsel", "x1", "attn1", "jbrow", "ceT", "b2e"],
             ["out"]),
}

GROUPS = [["ln0", "quant"], ["attn0", "wprep"], ["ffn0"], ["qkvr", "router"],
          ["attn1"], ["moeblk"], ["join"]]


def build_program(groups, need_later):
    reads, writes = set(), set()
    for g in groups:
        for p in g:
            _, r, w = PHASES[p]
            reads.update(r)
            writes.update(w)
    nc = bass.Bass("TRN2", target_bir_lowering=False)
    T = {}
    in_names, out_names = [], []
    for name in sorted(reads | writes):
        if name in INPUT_TENSORS:
            shape, dt = INPUT_TENSORS[name]()
            kind = "ExternalInput"
        else:
            shape, dt = SCRATCH_TENSORS[name]()
            if name in writes:
                kind = "ExternalOutput" if (name in need_later or name == "out") else "Internal"
            else:
                kind = "ExternalInput"
        if kind == "ExternalInput":
            in_names.append(name)
        elif kind == "ExternalOutput":
            out_names.append(name)
        T[name] = nc.dram_tensor(name, shape, dt, kind=kind).ap()
    tr = Tr(nc)
    with contextlib.ExitStack() as es_ctx:
        C = Ctx(tr, es_ctx)
        for g in groups:
            with contextlib.ExitStack() as es:
                for p in g:
                    fn, _, _ = PHASES[p]
                    fn(tr, es, C, T)
            tr.barrier()
        tr.finalize()
        tr.emit()
    return nc, in_names, out_names


def _values_from_inputs(inputs):
    c = np.ascontiguousarray
    xs = [c(inputs["L0_attn_encoding"][i]) for i in range(8)]
    shared = {
        "R": inputs["L0_attn_R"], "W1": inputs["L0_ffn_W1"], "b1f": inputs["L0_ffn_b1"],
        "W2": inputs["L0_ffn_W2"], "b2f": inputs["L0_ffn_b2"],
        "WQ": inputs["L1_attn_WQ"], "bQ": inputs["L1_attn_bQ"], "bK": inputs["L1_attn_bK"],
        "WK": inputs["L1_attn_WK"], "WV": inputs["L1_attn_WV"],
        "WO": inputs["L1_attn_WO"], "bO": inputs["L1_attn_bO"],
        "bV": inputs["L1_attn_bV"],
        "rw": inputs["L1_ffn_router_w"], "rb": inputs["L1_ffn_router_b"],
        "w1e": inputs["L1_ffn_w1"], "b1e": inputs["L1_ffn_b1"],
        "w2e": inputs["L1_ffn_w2"], "b2e": inputs["L1_ffn_b2"],
    }
    vals = {"x": xs}
    for k, v in shared.items():
        a = c(np.asarray(v))
        vals[k] = [a] * 8
    return vals


FUSED = True


def kernel(**inputs):
    vals = _values_from_inputs(inputs)
    launches = [GROUPS] if FUSED else [[g] for g in GROUPS]
    for gi, g in enumerate(launches):
        later = set()
        for g2 in launches[gi + 1:]:
            for grp in g2:
                for p in grp:
                    later.update(PHASES[p][1])
        nc, in_names, out_names = build_program(g, later)
        in_maps = [{n: vals[n][i] for n in in_names} for i in range(8)]
        res = run_bass_kernel_spmd(nc, in_maps, core_ids=list(range(8)))
        for n in out_names:
            vals[n] = [np.asarray(res.results[i][n]) for i in range(8)]
    return np.stack([vals["out"][i] for i in range(8)], axis=0)
